# version 10
# baseline (speedup 1.0000x reference)
import contextlib
import numpy as np
import ml_dtypes
import concourse.bass as bass
import concourse.mybir as mybir
from concourse.bass_utils import run_bass_kernel_spmd
from concourse.aot_env import allow_ici_all_to_all

F32 = mybir.dt.float32
BF16 = mybir.dt.bfloat16
I32 = mybir.dt.int32
U8 = mybir.dt.uint8
AF = mybir.ActivationFunctionType
ALU = mybir.AluOpType
AX = mybir.AxisListType
BF = ml_dtypes.bfloat16

NC = 8
T = 8192
D = 2048
TC = T // NC
DT = D // 128
NBAND = 16
RG = [list(range(NC))]
ISQ = float(1.0 / np.sqrt(128.0))
SHIFT = 24.0
EPS = 1e-5

_S16 = [2.0 ** (-0.5 * (i + 1)) for i in range(16)]
SLOPES = [_S16[h // 4] for h in range(16)]


def _boff(b):
    return 2 * b * (b + 1)


N_STRIP = _boff(NBAND)


class Sem:
    def __init__(self, nc, name):
        self.h = nc.alloc_semaphore(name)
        self.n = 0
        self.b = 0

    def inc(self, inst, k=1):
        inst.then_inc(self.h, k)
        self.n += k
        return inst


def _sems(nc, names):
    return {k: Sem(nc, k) for k in names.split()}


def _r(ap):
    return ap.rearrange("p a b -> p (a b)")



def emit_consts(nc, blk, s, ones_b, ones_f1, idf, eps_c):
    def c_ve(ve):
        ve.memset(ones_b[:], 1.0)
        ve.memset(ones_f1[:], 1.0)
        ve.memset(idf[:], 1.0)
        ve.memset(eps_c[:], EPS)
        s["ve"].inc(ve.engine_nop())
    blk.vector(c_ve)
    ve0 = s["ve"].n

    def c_gp(gp):
        gp.wait_ge(s["ve"].h, ve0)
        s["cst"].inc(gp.affine_select(idf[:], idf[:], pattern=[[-1, 128]],
                                      compare_op=ALU.is_equal, fill=0.0, base=0,
                                      channel_multiplier=1))
    blk.gpsimd(c_gp)


def emit_load_x(nc, blk, s, x_in, xT, xtm, idf, psA, psB):
    se0, pe0 = s["se"].n, s["pe"].n
    ld0 = [s["ldx0"].n, s["ldx1"].n]
    cst0 = s["cst"].n

    def p_sy(sy):
        for tt in range(8):
            if tt >= 2:
                sy.wait_ge(s["se"].h, se0 + 4 * (tt - 1))
            s["ldx%d" % (tt % 2)].inc(sy.dma_start(out=xtm[:, tt % 2, :],
                                      in_=x_in[128 * tt:128 * (tt + 1), :]), 16)
    blk.sync(p_sy)

    def p_te(te):
        te.wait_ge(s["cst"].h, cst0)
        for tt in range(8):
            te.wait_ge(s["ldx%d" % (tt % 2)].h, ld0[tt % 2] + 16 * (tt // 2 + 1))
            for g in range(4):
                i = tt * 4 + g
                ps = psA if i % 2 == 0 else psB
                if i >= 2:
                    te.wait_ge(s["se"].h, se0 + i - 1)
                for j in range(4):
                    dt = g * 4 + j
                    inst = te.transpose(ps[:, 128 * j:128 * (j + 1)],
                                        xtm[:, tt % 2, 128 * dt:128 * (dt + 1)], idf[:])
                s["pe"].inc(inst)
    blk.tensor(p_te)

    def p_se(se):
        for tt in range(8):
            for g in range(4):
                i = tt * 4 + g
                ps = psA if i % 2 == 0 else psB
                se.wait_ge(s["pe"].h, pe0 + i + 1)
                s["se"].inc(se.copy(xT[:, 4 * g:4 * g + 4, 128 * tt:128 * (tt + 1)], ps[:]))
    blk.scalar(p_se)
    return s["se"].n


def emit_ln(nc, blk, s, xT, xlnT, xb, srow, tmpf, ones_b, ones_f1, eps_c,
            psA, psB, psC, psD, psE, psF, ready):
    rs, rv = ready
    ve0, pe0, se0 = s["ve"].n, s["pe"].n, s["se"].n

    def l_ve(ve):
        ve.wait_ge(s[rs].h, rv)
        s["ve"].inc(ve.tensor_copy(_r(xb[:]), _r(xT[:])))
    blk.vector(l_ve)

    def l_te(te):
        te.wait_ge(s["ve"].h, ve0 + 1)
        for th, ps in ((0, psC), (1, psE)):
            for dt in range(DT):
                inst = te.matmul(ps[0:1, :], ones_b[:], xb[:, dt, 512 * th:512 * (th + 1)],
                                 start=(dt == 0), stop=(dt == DT - 1))
        s["pe"].inc(inst)
        te.wait_ge(s["se"].h, se0 + 1)
        for th, ps in ((0, psD), (1, psF)):
            for dt in range(DT):
                inst = te.matmul(ps[0:1, :], ones_b[:], xb[:, dt, 512 * th:512 * (th + 1)],
                                 start=(dt == 0), stop=(dt == DT - 1))
        s["pe"].inc(inst)
    blk.tensor(l_te)

    def l_se(se):
        se.wait_ge(s["pe"].h, pe0 + 1)
        s["se"].inc(se.activation(_r(xb[:]), _r(xb[:]), AF.Square, bias=0.0, scale=1.0))
    blk.scalar(l_se)

    def l_ve2(ve):
        ve.wait_ge(s["pe"].h, pe0 + 2)
        for th, psS, psQ in ((0, psC, psD), (1, psE, psF)):
            s["chv"].inc(ve.tensor_scalar(srow[:, 0, 512 * th:512 * (th + 1)], psS[0:1, :],
                                          1.0 / D, None, op0=ALU.mult))
            s["chv"].inc(ve.tensor_scalar(srow[:, 1, 512 * th:512 * (th + 1)], psQ[0:1, :],
                                          1.0 / D, None, op0=ALU.mult))
        ve.wait_ge(s["chv"].h, s["chv"].n)
        s["chv"].inc(ve.tensor_tensor(srow[:, 2, :], srow[:, 0, :], srow[:, 0, :], op=ALU.mult))
        ve.wait_ge(s["chv"].h, s["chv"].n)
        s["ve"].inc(ve.tensor_tensor(srow[:, 1, :], srow[:, 1, :], srow[:, 2, :], op=ALU.subtract))
    blk.vector(l_ve2)

    def l_se2(se):
        se.wait_ge(s["ve"].h, ve0 + 2)
        s["se"].inc(se.activation(srow[:, 1, :], srow[:, 1, :], AF.Sqrt,
                                  bias=eps_c[0:1, :], scale=1.0))
    blk.scalar(l_se2)

    def l_ve3(ve):
        ve.wait_ge(s["se"].h, se0 + 2)
        s["ve"].inc(ve.reciprocal(srow[:, 2, :], srow[:, 1, :]))
    blk.vector(l_ve3)

    ve_ap0 = s["ve"].n
    pe_b = s["pe"].n

    def l_te2(te):
        te.wait_ge(s["ve"].h, ve_ap0)
        for th in range(2):
            if th == 1:
                te.wait_ge(s["ve"].h, ve_ap0 + 1)
            te.matmul(psA[:], ones_f1[:], srow[:, 0, 512 * th:512 * (th + 1)], start=True, stop=True)
            s["pe"].inc(te.matmul(psB[:], ones_f1[:], srow[:, 2, 512 * th:512 * (th + 1)],
                                  start=True, stop=True))
    blk.tensor(l_te2)

    def l_ve4(ve):
        for th in range(2):
            ve.wait_ge(s["pe"].h, pe_b + th + 1)
            for dt in range(DT):
                s["chv"].inc(ve.tensor_tensor(tmpf[:, dt % 2, :],
                                              xT[:, dt, 512 * th:512 * (th + 1)], psA[:],
                                              op=ALU.subtract))
                ve.wait_ge(s["chv"].h, s["chv"].n)
                inst = ve.tensor_tensor(xlnT[:, dt, 512 * th:512 * (th + 1)],
                                        tmpf[:, dt % 2, :], psB[:], op=ALU.mult)
            s["ve"].inc(inst)
    blk.vector(l_ve4)
    return s["ve"].n


def emit_store_x(nc, blk, s, xT, x_out, xrow, idf, psA, psB, ready):
    rs, rv = ready
    pe0, se0 = s["pe"].n, s["se"].n
    st0 = [s["st0"].n, s["st1"].n]

    def t_te(te):
        te.wait_ge(s[rs].h, rv)
        for tt in range(8):
            for g in range(4):
                i = tt * 4 + g
                ps = psA if i % 2 == 0 else psB
                if i >= 2:
                    te.wait_ge(s["se"].h, se0 + i - 1)
                for j in range(4):
                    dt = g * 4 + j
                    inst = te.transpose(ps[:, 128 * j:128 * (j + 1)],
                                        xT[:, dt, 128 * tt:128 * (tt + 1)], idf[:])
                s["pe"].inc(inst)
    blk.tensor(t_te)

    def t_se(se):
        for tt in range(8):
            if tt >= 2:
                se.wait_ge(s["st%d" % (tt % 2)].h, st0[tt % 2] + 16 * (tt // 2))
            for g in range(4):
                i = tt * 4 + g
                ps = psA if i % 2 == 0 else psB
                se.wait_ge(s["pe"].h, pe0 + i + 1)
                s["se"].inc(se.copy(xrow[:, tt % 2, 512 * g:512 * (g + 1)], ps[:]))
    blk.scalar(t_se)

    def t_sy(sy):
        for tt in range(8):
            sy.wait_ge(s["se"].h, se0 + 4 * (tt + 1))
            s["st%d" % (tt % 2)].inc(sy.dma_start(out=x_out[128 * tt:128 * (tt + 1), :],
                                     in_=xrow[:, tt % 2, :]), 16)
        sy.wait_ge(s["st0"].h, s["st0"].n)
        sy.wait_ge(s["st1"].h, s["st1"].n)
    blk.sync(t_sy)
    return ("st0", s["st0"].n)



SHARED_SEMS = "cst pe se ve gp cc chv chg ldx0 ldx1 st0 st1"
L1_SEMS = ("xg0 xg1 rsw0 rsw1 cld agxw wld a2aw wold otld bcl "
           "pex sev stt expp msk peo vep vbd vno")
L2_SEMS = ("sga chs pk0 pk1 rp0 rp1 dqx plc pw0 pw1 pgl0 pgl1 pdl0 pdl1 "
           "peg seg vgl actt gin ped vda")
L3_SEMS = ("chs wld pjmm pjev actl vsg vcs ecs vkc pea pet vmsk peo pes "
           "sekh ssb soev vst qwb agw dal dsl "
           "sjl0 sjl1 djl rel epx cl0 cl1 pco vcp vca wold pwo vwo")
L4_SEMS = ("ywm chs rwl agl rpe rse rve rpw wev wwr wbc zpe zev zms zsh0 zsh1 "
           "wg0 wg1 pmm pyy slu vhe gme vacc mxw0 mxw1 "
           "yev pyt ytv mxl0 mxl1 wd0 wd1 bd0 bd1 pdn vdn")


class SemPool:

    def __init__(self, nc):
        self.nc = nc
        self.sems = []

    def view(self, shared, names):
        v = dict(shared)
        needed = [k for k in names.split() if k not in v]
        while len(self.sems) < len(needed):
            self.sems.append(Sem(self.nc, "pool%d" % len(self.sems)))
        for k, sem in zip(needed, self.sems):
            sem.b = sem.n
            v[k] = sem
        return v


def _sem_view(nc, shared, names, pfx):
    v = dict(shared)
    for k in names.split():
        if k not in v:
            v[k] = Sem(nc, pfx + k)
    return v


def _gate_all(blk, s):
    t0, t1 = s["st0"].n, s["st1"].n

    def g(eng):
        eng.wait_ge(s["st0"].h, t0)
        eng.wait_ge(s["st1"].h, t1)
    blk.sync(g)
    blk.tensor(g)
    blk.scalar(g)
    blk.vector(g)
    blk.gpsimd(g)


def _emit_l1(nc, blk, s, ones_b, ones_f1, idf, eps_c,
             x_in, x_out, wq, wk, wv, wo, slopes_in, bias_in, pfx):
    agx_src = nc.dram_tensor(pfx + "agx_src", [D, TC], BF16, kind="Internal").ap()
    agx_dst = nc.dram_tensor(pfx + "agx_dst", [NC * D, TC], BF16, kind="Internal").ap()
    a2a_src = nc.dram_tensor(pfx + "a2a_src", [D, TC], BF16, kind="Internal").ap()
    a2a_dst = nc.dram_tensor(pfx + "a2a_dst", [D, TC], BF16, kind="Internal").ap()
    rs_scr = nc.dram_tensor(pfx + "rs_scr", [32, 512], BF16, kind="Internal").ap()

    with contextlib.ExitStack() as es:
        E = es.enter_context
        xT = E(nc.sbuf_tensor(pfx + "xT", [128, DT, TC], F32))
        psA = E(nc.psum_tensor(pfx + "psA", [128, 512], F32))
        psB = E(nc.psum_tensor(pfx + "psB", [128, 512], F32))
        psC = E(nc.psum_tensor(pfx + "psC", [128, 512], F32))
        psD = E(nc.psum_tensor(pfx + "psD", [128, 512], F32))
        psE = E(nc.psum_tensor(pfx + "psE", [128, 512], F32))
        psF = E(nc.psum_tensor(pfx + "psF", [128, 512], F32))

        Rt = E(nc.sbuf_tensor(pfx + "Rt", [128, 5, 512], F32))
        slop = E(nc.sbuf_tensor(pfx + "slop", [128, 2], F32))
        abias = E(nc.sbuf_tensor(pfx + "abias", [128, 2, N_STRIP], F32))
        if True:
            def c2_gp(gp):
                s["gp"].inc(gp.iota(Rt[:, 0, :], pattern=[[-1, 512]], base=0,
                                    channel_multiplier=1,
                                    allow_small_or_imprecise_dtypes=True))
                gp.wait_ge(s["gp"].h, s["gp"].n)
                for m in range(4):
                    inst = gp.affine_select(Rt[:, 1 + m, :], Rt[:, 0, :], pattern=[[1, 512]],
                                            compare_op=ALU.is_ge, fill=-1e30, base=-128 * m,
                                            channel_multiplier=-1)
                s["cst"].inc(inst)
                s["cld"].inc(gp.dma_start(out=slop[:],
                                          in_=slopes_in.unsqueeze(0).broadcast_to([128, 2])), 16)
                s["cld"].inc(gp.dma_start(out=abias[:],
                                          in_=bias_in.unsqueeze(0).broadcast_to([128, 2, N_STRIP])), 16)
            blk.gpsimd(c2_gp)
            cst_flash = s["cst"].n
            cld_flash = s["cld"].n

            with contextlib.ExitStack() as es0:
                E0 = es0.enter_context
                xtm = E0(nc.sbuf_tensor(pfx + "xtm", [128, 2, D], F32))
                xb = E0(nc.sbuf_tensor(pfx + "xb", [128, DT, TC], BF16))
                srow = E0(nc.sbuf_tensor(pfx + "srow", [1, 3, TC], F32))
                xlnT = E0(nc.sbuf_tensor(pfx + "xlnT", [128, DT, TC], BF16))
                tmpf = E0(nc.sbuf_tensor(pfx + "tmpf", [128, 2, 512], F32))
                emit_load_x(nc, blk, s, x_in, xT, xtm, idf, psA, psB)
                ve_ln = emit_ln(nc, blk, s, xT, xlnT, xb, srow, tmpf, ones_b, ones_f1, eps_c,
                                psA, psB, psC, psD, psE, psF, ("se", s["se"].n))

                def pub_gp(gp):
                    gp.wait_ge(s["ve"].h, ve_ln)
                    s["agxw"].inc(gp.dma_start(out=agx_src.rearrange("(dt p) t -> p dt t", p=128),
                                               in_=xlnT[:]), 16)
                    gp.wait_ge(s["agxw"].h, s["agxw"].n)
                    s["cc"].inc(gp.collective_compute(
                        "AllGather", ALU.bypass, replica_groups=RG,
                        ins=[agx_src.opt()], outs=[agx_dst.opt()]))
                blk.gpsimd(pub_gp)
                cc_agx = s["cc"].n

            with contextlib.ExitStack() as es1:
                E1 = es1.enter_context
                wq_sb = E1(nc.sbuf_tensor(pfx + "wq_sb", [128, DT, 256], BF16))
                wk_sb = E1(nc.sbuf_tensor(pfx + "wk_sb", [128, DT, 128], BF16))
                wv_sb = E1(nc.sbuf_tensor(pfx + "wv_sb", [128, DT, 128], BF16))
                qT = E1(nc.sbuf_tensor(pfx + "qT", [128, 2, T], BF16))
                kT = E1(nc.sbuf_tensor(pfx + "kT", [128, 64, 128], BF16))
                Vt = E1(nc.sbuf_tensor(pfx + "Vt", [128, 64, 128], BF16))
                xg = E1(nc.sbuf_tensor(pfx + "xg", [128, 2, 512], BF16))
                Ofull = E1(nc.sbuf_tensor(pfx + "Ofull", [128, 2, T], BF16))
                tmpE = E1(nc.sbuf_tensor(pfx + "tmpE", [128, 3, 512], F32))
                Ebuf = E1(nc.sbuf_tensor(pfx + "Ebuf", [128, 3, 512], BF16))
                rssm = E1(nc.sbuf_tensor(pfx + "rssm", [1, 2, 512], BF16))
                bc_g = E1(nc.sbuf_tensor(pfx + "bc_g", [128, 2, 512], BF16))
                def w_gp(gp):
                    s["wld"].inc(gp.dma_start(out=wq_sb[:],
                                              in_=wq.rearrange("(kt p) m -> p kt m", p=128)), 16)
                    s["wld"].inc(gp.dma_start(out=wk_sb[:],
                                              in_=wk.rearrange("(kt p) m -> p kt m", p=128)), 16)
                    s["wld"].inc(gp.dma_start(out=wv_sb[:],
                                              in_=wv.rearrange("(kt p) m -> p kt m", p=128)), 16)
                blk.gpsimd(w_gp)
                ldg_w = s["wld"].n

                xg0 = [s["xg0"].n, s["xg1"].n]
                pex0 = s["pex"].n
                sev0 = s["sev"].n

                def pj_sy(sy):
                    sy.wait_ge(s["cc"].h, cc_agx)
                    for ch in range(16):
                        for kt in range(DT):
                            L = ch * DT + kt
                            if L >= 2:
                                sy.wait_ge(s["pex"].h, pex0 + L - 1)
                            blkr = ch // 2
                            co = (ch % 2) * 512
                            s["xg%d" % (L % 2)].inc(sy.dma_start(
                                out=xg[:, L % 2, :],
                                in_=agx_dst[2048 * blkr + 128 * kt: 2048 * blkr + 128 * (kt + 1),
                                            co:co + 512]), 16)
                blk.sync(pj_sy)

                ve_pr = s["ve"].n
                def pj_te(te):
                    te.wait_ge(s["wld"].h, ldg_w)
                    te.wait_ge(s["ve"].h, ve_pr)
                    for ch in range(16):
                        if ch >= 1:
                            te.wait_ge(s["sev"].h, sev0 + 4 * ch)
                        for kt in range(DT):
                            L = ch * DT + kt
                            te.wait_ge(s["xg%d" % (L % 2)].h, xg0[L % 2] + 16 * (L // 2 + 1))
                            st = (kt == 0)
                            sp = (kt == DT - 1)
                            te.matmul(psA[:], wq_sb[:, kt, 0:128], xg[:, L % 2, :], start=st, stop=sp)
                            te.matmul(psB[:], wq_sb[:, kt, 128:256], xg[:, L % 2, :], start=st, stop=sp)
                            te.matmul(psC[:], wk_sb[:, kt, :], xg[:, L % 2, :], start=st, stop=sp)
                            for j in range(4):
                                inst = te.matmul(psD[:, 128 * j:128 * (j + 1)],
                                                 xg[:, L % 2, 128 * j:128 * (j + 1)],
                                                 wv_sb[:, kt, :],
                                                 start=(st and j == 0), stop=(sp and j == 3))
                            s["pex"].inc(inst)
                blk.tensor(pj_te)

                def pj_se(se):
                    for ch in range(16):
                        se.wait_ge(s["pex"].h, pex0 + DT * (ch + 1))
                        s["sev"].inc(se.activation(qT[:, 0, 512 * ch:512 * (ch + 1)],
                                                   psA[:], AF.Identity, bias=0.0, scale=ISQ))
                        s["sev"].inc(se.activation(qT[:, 1, 512 * ch:512 * (ch + 1)],
                                                   psB[:], AF.Identity, bias=0.0, scale=ISQ))
                        s["sev"].inc(se.copy(kT[:, 4 * ch:4 * ch + 4, :], psC[:]))
                        s["sev"].inc(se.copy(Vt[:, 4 * ch:4 * ch + 4, :], psD[:]))
                blk.scalar(pj_se)
                sev_proj = s["sev"].n

                strips = []
                for b in range(NBAND):
                    for hh in range(2):
                        for sc in range(4 * (b + 1)):
                            strips.append((b, hh, sc))
                G = len(strips)
                nmask_before = []
                nm = 0
                for (b, hh, sc) in strips:
                    nmask_before.append(nm)
                    if sc >= 4 * b:
                        nm += 1
                NM = nm
                kidx = []
                k = -1
                for gi, (b, hh, sc) in enumerate(strips):
                    if sc == 0:
                        k += 1
                    kidx.append(k)
                NK = k + 1

                stt0, exp0, msk0, peo0 = s["stt"].n, s["expp"].n, s["msk"].n, s["peo"].n
                pes0 = s["pex"].n
                vep0, vbd0, vno0 = s["vep"].n, s["vbd"].n, s["vno"].n
                rsw0 = [s["rsw0"].n, s["rsw1"].n]
                bcl0 = s["bcl"].n

                def fl_te(te):
                    te.wait_ge(s["sev"].h, sev_proj)
                    te.wait_ge(s["cst"].h, cst_flash)

                    def rso(gi):
                        b, hh, sc = strips[gi]
                        kk = kidx[gi]
                        te.wait_ge(s["expp"].h, exp0 + gi + 1)
                        if sc == 0:
                            te.wait_ge(s["vbd"].h, vbd0 + kk)
                        st = (sc == 0)
                        sp = (sc == 4 * (b + 1) - 1)
                        E = Ebuf[:, gi % 3, :]
                        te.matmul(psD[0:1, :], ones_b[:], E, start=st, stop=sp)
                        s["peo"].inc(te.matmul(psC[:], Vt[:, sc, :], E, start=st, stop=sp))

                    SPS = (psA, psB, psE)
                    for gi, (b, hh, sc) in enumerate(strips):
                        if gi >= 3:
                            te.wait_ge(s["stt"].h, stt0 + gi - 2)
                        s["pex"].inc(te.matmul(SPS[gi % 3][:],
                                     kT[:, sc, :], qT[:, hh, 512 * b:512 * (b + 1)],
                                     start=True, stop=True))
                        if gi >= 1:
                            rso(gi - 1)
                    rso(G - 1)
                blk.tensor(fl_te)

                def fl_ve(ve):
                    ve.wait_ge(s["cld"].h, cld_flash)
                    ve.wait_ge(s["cst"].h, cst_flash)
                    for gi, (b, hh, sc) in enumerate(strips):
                        kk = kidx[gi]
                        ve.wait_ge(s["pex"].h, pes0 + gi + 1)
                        if gi >= 3:
                            ve.wait_ge(s["expp"].h, exp0 + gi - 2)
                        vidx = 0 if sc < 4 * b else (sc - 4 * b + 1)
                        s["stt"].inc(ve.scalar_tensor_tensor(
                            tmpE[:, gi % 3, :], Rt[:, vidx, :], slop[:, hh:hh + 1],
                            (psA, psB, psE)[gi % 3][:], op0=ALU.mult, op1=ALU.add))
                        if sc == 4 * (b + 1) - 1:
                            ve.wait_ge(s["peo"].h, peo0 + gi + 1)
                            if kk >= 2:
                                ve.wait_ge(s["rsw%d" % (kk % 2)].h,
                                           rsw0[kk % 2] + 16 * (kk // 2))
                            with nc.allow_low_precision(reason="bf16 rowsum recip"):
                                s["vep"].inc(ve.reciprocal(rssm[:, kk % 2, :], psD[0:1, :]))
                            s["vbd"].inc(ve.tensor_copy(
                                Ofull[:, hh, 512 * b:512 * (b + 1)], psC[:]))
                blk.vector(fl_ve)

                def fl_se(se):
                    for gi, (b, hh, sc) in enumerate(strips):
                        se.wait_ge(s["stt"].h, stt0 + gi + 1)
                        if gi >= 3:
                            se.wait_ge(s["peo"].h, peo0 + gi - 2)
                        i = _boff(b) + sc
                        s["expp"].inc(se.activation(Ebuf[:, gi % 3, :], tmpE[:, gi % 3, :],
                                                    AF.Exp, bias=abias[:, hh, i:i + 1], scale=1.0))
                blk.scalar(fl_se)

                bh_of = []
                for b in range(NBAND):
                    for hh in range(2):
                        bh_of.append((hh, b))

                def fl_gp(gp):
                    for kk in range(NK):
                        gp.wait_ge(s["vep"].h, vep0 + kk + 1)
                        s["rsw%d" % (kk % 2)].inc(gp.dma_start(out=rs_scr[kk:kk + 1, :],
                                                  in_=rssm[:, kk % 2, :]), 16)
                    for g in range(16):
                        if g >= 1:
                            gp.wait_ge(s["vno"].h, vno0 + 2 * g)
                        gp.wait_ge(s["rsw0"].h, rsw0[0] + 16 * (g + 1))
                        gp.wait_ge(s["rsw1"].h, rsw0[1] + 16 * (g + 1))
                        s["bcl"].inc(gp.dma_start(
                            out=bc_g[:],
                            in_=rs_scr[2 * g:2 * (g + 1), :].unsqueeze(0)
                                .broadcast_to([128, 2, 512])), 16)
                blk.gpsimd(fl_gp)

                def fl_ve2(ve):
                    ve.wait_ge(s["vbd"].h, vbd0 + NK)
                    for kk in range(NK):
                        hh, b = bh_of[kk]
                        g = kk // 2
                        ve.wait_ge(s["bcl"].h, bcl0 + 16 * (g + 1))
                        s["vno"].inc(ve.tensor_tensor(
                            Ofull[:, hh, 512 * b:512 * (b + 1)],
                            Ofull[:, hh, 512 * b:512 * (b + 1)],
                            bc_g[:, kk % 2, :], op=ALU.mult))
                blk.vector(fl_ve2)

                def a2a_gp(gp):
                    gp.wait_ge(s["vno"].h, vno0 + NK)
                    for j in range(NC):
                        s["a2aw"].inc(gp.dma_start(
                            out=a2a_src[256 * j:256 * (j + 1), :].rearrange("(h p) t -> p h t", p=128),
                            in_=Ofull[:, :, TC * j:TC * (j + 1)]), 16)
                    gp.wait_ge(s["a2aw"].h, s["a2aw"].n)
                    with allow_ici_all_to_all():
                        s["cc"].inc(gp.collective_compute(
                            "AllToAll", ALU.bypass, replica_groups=RG,
                            ins=[a2a_src.opt()], outs=[a2a_dst.opt()]))
                blk.gpsimd(a2a_gp)
                cc_a2a = s["cc"].n

            with contextlib.ExitStack() as es2:
                E2 = es2.enter_context
                wo_sb = E2(nc.sbuf_tensor(pfx + "wo_sb", [128, DT, D], BF16))
                oT = E2(nc.sbuf_tensor(pfx + "oT", [128, DT, TC], BF16))
                def wo_gp(gp):
                    for q in range(4):
                        s["wold"].inc(gp.dma_start(
                            out=wo_sb[:, 4 * q:4 * (q + 1), :],
                            in_=wo.rearrange("(kt p) m -> p kt m", p=128)[:, 4 * q:4 * (q + 1), :]), 16)
                blk.gpsimd(wo_gp)
                ldg_wo = s["wold"].n

                def wo_sy(sy):
                    sy.wait_ge(s["cc"].h, cc_a2a)
                    s["otld"].inc(sy.dma_start(out=oT[:],
                                              in_=a2a_dst.rearrange("(dt p) t -> p dt t", p=128)), 16)
                blk.sync(wo_sy)
                ldq_o = s["otld"].n

                pe_wo0 = s["pex"].n
                ve_wo0 = s["ve"].n

                def wo_te(te):
                    te.wait_ge(s["wold"].h, ldg_wo)
                    te.wait_ge(s["otld"].h, ldq_o)
                    for ho in range(DT):
                        for th in range(2):
                            i = ho * 2 + th
                            ps = psA if i % 2 == 0 else psB
                            if i >= 2:
                                te.wait_ge(s["ve"].h, ve_wo0 + i - 1)
                            for kt in range(DT):
                                inst = te.matmul(ps[:], wo_sb[:, kt, 128 * ho:128 * (ho + 1)],
                                                 oT[:, kt, 512 * th:512 * (th + 1)],
                                                 start=(kt == 0), stop=(kt == DT - 1))
                            s["pex"].inc(inst)
                blk.tensor(wo_te)

                def wo_ve(ve):
                    for ho in range(DT):
                        for th in range(2):
                            i = ho * 2 + th
                            ps = psA if i % 2 == 0 else psB
                            ve.wait_ge(s["pex"].h, pe_wo0 + i + 1)
                            s["ve"].inc(ve.tensor_tensor(
                                xT[:, ho, 512 * th:512 * (th + 1)],
                                xT[:, ho, 512 * th:512 * (th + 1)], ps[:], op=ALU.add))
                blk.vector(wo_ve)
                ve_res = s["ve"].n

            with nc.sbuf_tensor(pfx + "xrow", [128, 2, D], F32) as xrow:
                emit_store_x(nc, blk, s, xT, x_out, xrow, idf, psA, psB, ("ve", ve_res))


def build_l1():
    nc = bass.Bass("TRN2", target_bir_lowering=False)
    x_in = nc.dram_tensor("x_in", [TC, D], F32, kind="ExternalInput").ap()
    wq = nc.dram_tensor("wq", [D, 256], F32, kind="ExternalInput").ap()
    wk = nc.dram_tensor("wk", [D, 128], F32, kind="ExternalInput").ap()
    wv = nc.dram_tensor("wv", [D, 128], F32, kind="ExternalInput").ap()
    wo = nc.dram_tensor("wo", [D, D], F32, kind="ExternalInput").ap()
    slopes_in = nc.dram_tensor("slopes", [2], F32, kind="ExternalInput").ap()
    bias_in = nc.dram_tensor("abias_in", [2, N_STRIP], F32, kind="ExternalInput").ap()
    x_out = nc.dram_tensor("x_out", [TC, D], F32, kind="ExternalOutput").ap()
    shared = _sems(nc, SHARED_SEMS)
    s = _sem_view(nc, shared, L1_SEMS, "")
    with contextlib.ExitStack() as es:
        E = es.enter_context
        ones_b = E(nc.sbuf_tensor("ones_b", [128, 1], BF16))
        ones_f1 = E(nc.sbuf_tensor("ones_f1", [1, 128], F32))
        idf = E(nc.sbuf_tensor("idf", [128, 128], F32))
        eps_c = E(nc.sbuf_tensor("eps_c", [128, 1], F32))
        blk = E(nc.Block())
        emit_consts(nc, blk, s, ones_b, ones_f1, idf, eps_c)
        _emit_l1(nc, blk, s, ones_b, ones_f1, idf, eps_c,
                 x_in, x_out, wq, wk, wv, wo, slopes_in, bias_in, "")
    return nc



def _deq_extract(ve, ch, pk_u8, ci, b1, b2, braw_t, gbuf_t, slot):
    def w():
        ve.wait_ge(ch.h, ch.n)
    ch.inc(ve.tensor_copy(ci[:], pk_u8))
    for nib in range(2):
        w()
        if nib == 0:
            ch.inc(ve.tensor_scalar(b1[:], ci[:], 7, None, op0=ALU.bitwise_and))
            ch.inc(ve.tensor_scalar(b2[:], ci[:], 8, 28, op0=ALU.bitwise_and,
                                    op1=ALU.logical_shift_left))
        else:
            ch.inc(ve.tensor_scalar(b1[:], ci[:], 4, 7, op0=ALU.logical_shift_right,
                                    op1=ALU.bitwise_and))
            ch.inc(ve.tensor_scalar(b2[:], ci[:], 128, 24, op0=ALU.bitwise_and,
                                    op1=ALU.logical_shift_left))
        w()
        ch.inc(ve.tensor_scalar(gbuf_t[:, slot, nib, :], b1[:], 2.0 / 3.0, 1.0,
                                op0=ALU.mult, op1=ALU.min))
        ch.inc(ve.tensor_scalar(b1[:], b1[:], 4194304.0, 1056964608.0,
                                op0=ALU.mult, op1=ALU.add))
        w()
        ch.inc(ve.tensor_tensor(braw_t[:].bitcast(I32)[:, slot, nib, :], b1[:], b2[:],
                                op=ALU.bitwise_or))


def _deq_tail(gp, ch, braw_t, gbuf_t, slot, gg, rep_lo, rep_hi, pl_lo, pl_hi):
    for nib, rep_ap, pl_ap in ((0, rep_lo, pl_lo), (1, rep_hi, pl_hi)):
        gp.wait_ge(ch.h, ch.n)
        ch.inc(gp.tensor_tensor(gg[:], braw_t[:, slot, nib, :], gbuf_t[:, slot, nib, :],
                                op=ALU.mult))
        gp.wait_ge(ch.h, ch.n)
        ch.inc(gp.tensor_tensor(pl_ap, gg[:], rep_ap, op=ALU.mult))


def _emit_l2(nc, blk, s, ones_b, ones_f1, idf, eps_c,
             x_in, x_out, gpk, upk, dpkt, gsc, usc, dsc, pfx):
    with contextlib.ExitStack() as es:
        E = es.enter_context
        xT = E(nc.sbuf_tensor(pfx + "xT", [128, DT, TC], F32))
        c05 = E(nc.sbuf_tensor(pfx + "c05", [128, 1], F32))
        sel = E(nc.sbuf_tensor(pfx + "sel", [128, 4, 128], BF16))
        xlnP = E(nc.sbuf_tensor(pfx + "xlnP", [128, DT, TC], BF16))

        cst_c = s["cst"].n

        def sel_ve(ve):
            ve.memset(c05[:], 0.5)
            s["ve"].inc(ve.memset(sel[:], 1.0))
        blk.vector(sel_ve)
        ve_sel = s["ve"].n

        def sel_gp(gp):
            gp.wait_ge(s["ve"].h, ve_sel)
            for q, base in ((0, 0), (1, 128), (2, -1), (3, 127)):
                inst = gp.affine_select(sel[:, q, :], sel[:, q, :], pattern=[[-2, 128]],
                                        compare_op=ALU.is_equal, fill=0.0, base=base,
                                        channel_multiplier=1)
            s["cst"].inc(inst)
        blk.gpsimd(sel_gp)
        cst_sel = s["cst"].n

        with contextlib.ExitStack() as es0:
            E0 = es0.enter_context
            xtm = E0(nc.sbuf_tensor(pfx + "xtm", [128, 2, D], F32))
            xb = E0(nc.sbuf_tensor(pfx + "xb", [128, DT, TC], BF16))
            srow = E0(nc.sbuf_tensor(pfx + "srow", [1, 3, TC], F32))
            xlnT = E0(nc.sbuf_tensor(pfx + "xlnT", [128, DT, TC], BF16))
            tmpf = E0(nc.sbuf_tensor(pfx + "tmpf", [128, 2, 512], F32))
            psA = E0(nc.psum_tensor(pfx + "psA", [128, 512], F32))
            psB = E0(nc.psum_tensor(pfx + "psB", [128, 512], F32))
            psC = E0(nc.psum_tensor(pfx + "psC", [128, 512], F32))
            psD = E0(nc.psum_tensor(pfx + "psD", [128, 512], F32))
            psE = E0(nc.psum_tensor(pfx + "psE", [128, 512], F32))
            psF = E0(nc.psum_tensor(pfx + "psF", [128, 512], F32))
            emit_load_x(nc, blk, s, x_in, xT, xtm, idf, psA, psB)
            ve_ln = emit_ln(nc, blk, s, xT, xlnT, xb, srow, tmpf, ones_b, ones_f1, eps_c,
                            psA, psB, psC, psD, psE, psF, ("se", s["se"].n))

            pe_p0 = s["pe"].n
            se_p0 = s["se"].n

            def prm_te(te):
                te.wait_ge(s["ve"].h, ve_ln)
                te.wait_ge(s["cst"].h, cst_sel)
                for B in range(8):
                    for par in range(2):
                        for th in range(2):
                            i = (B * 2 + par) * 2 + th
                            ps = psA if i % 2 == 0 else psB
                            if i >= 2:
                                te.wait_ge(s["se"].h, se_p0 + i - 1)
                            te.matmul(ps[:], sel[:, 2 * par, :],
                                      xlnT[:, 2 * B, 512 * th:512 * (th + 1)],
                                      start=True, stop=False)
                            inst = te.matmul(ps[:], sel[:, 2 * par + 1, :],
                                             xlnT[:, 2 * B + 1, 512 * th:512 * (th + 1)],
                                             start=False, stop=True)
                            s["pe"].inc(inst)
            blk.tensor(prm_te)

            def prm_se(se):
                for B in range(8):
                    for par in range(2):
                        for th in range(2):
                            i = (B * 2 + par) * 2 + th
                            ps = psA if i % 2 == 0 else psB
                            se.wait_ge(s["pe"].h, pe_p0 + i + 1)
                            s["se"].inc(se.copy(
                                xlnP[:, 2 * B + par, 512 * th:512 * (th + 1)], ps[:]))
            blk.scalar(prm_se)
            se_prm = s["se"].n

        pl_srcs = [nc.dram_tensor(pfx + "pl_src%d" % k2, [1, 2, 128, 4096], BF16,
                                  kind="Internal").ap() for k2 in range(8)]
        pl_alls = [nc.dram_tensor(pfx + "pl_all%d" % k2, [8, 2, 128, 4096], BF16,
                                  kind="Internal").ap() for k2 in range(8)]
        pd_srcs = [nc.dram_tensor(pfx + "pd_src%d" % k2, [1, 128, 4096], BF16,
                                  kind="Internal").ap() for k2 in range(8)]
        pd_alls = [nc.dram_tensor(pfx + "pd_all%d" % k2, [8, 128, 4096], BF16,
                                  kind="Internal").ap() for k2 in range(8)]

        with contextlib.ExitStack() as esp:
            EP = esp.enter_context
            pkb = EP(nc.sbuf_tensor(pfx + "pkb", [128, 2, 3, 2048], U8))
            repb = EP(nc.sbuf_tensor(pfx + "repb", [128, 2, 3, 2048], BF16))
            cib = EP(nc.sbuf_tensor(pfx + "cib", [128, 1024], I32))
            b1b = EP(nc.sbuf_tensor(pfx + "b1b", [128, 1024], I32))
            b2b = EP(nc.sbuf_tensor(pfx + "b2b", [128, 1024], I32))
            braw = EP(nc.sbuf_tensor(pfx + "braw", [128, 2, 2, 1024], F32))
            gbuf = EP(nc.sbuf_tensor(pfx + "gbuf", [128, 2, 2, 1024], F32))
            ggp = EP(nc.sbuf_tensor(pfx + "ggp", [128, 1024], F32))
            pst = EP(nc.sbuf_tensor(pfx + "pst", [128, 3, 4096], BF16))

            def prod_sy(sy):
                sy.wait_ge(s["se"].h, se_prm)
                for j in range(8):
                    if j >= 2:
                        sy.wait_ge(s["dqx"].h, s["dqx"].b + 6 * (j - 1))
                    sm = s["pk%d" % (j % 2)]
                    sm.inc(sy.dma_start(out=pkb[:, j % 2, 0, :], in_=_r(gpk[j])), 16)
                    sm.inc(sy.dma_start(out=pkb[:, j % 2, 1, :], in_=_r(upk[j])), 16)
                    sm.inc(sy.dma_start(out=pkb[:, j % 2, 2, :], in_=dpkt[:, j, :]), 16)
            blk.sync(prod_sy)

            def prod_ve(ve):
                ve.wait_ge(s["cst"].h, cst_c)
                ve.wait_ge(s["se"].h, se_prm)
                k = 0
                for j in range(8):
                    sm = s["pk%d" % (j % 2)]
                    ve.wait_ge(sm.h, sm.b + 48 * (j // 2 + 1))
                    for mat in range(3):
                        for u in range(2):
                            if k >= 2:
                                ve.wait_ge(s["plc"].h, s["plc"].b + 4 * (k - 1))
                            _deq_extract(ve, s["chv"],
                                         pkb[:, j % 2, mat, 1024 * u:1024 * (u + 1)],
                                         cib, b1b, b2b, braw, gbuf, k % 2)
                            s["dqx"].inc(ve.engine_nop())
                            k += 1
            blk.vector(prod_ve)

            def prod_gp(gp):
                gp.wait_ge(s["se"].h, se_prm)
                k = 0
                for j in range(8):
                    if j >= 1:
                        pv = j - 1
                        gp.wait_ge(s["pw%d" % (pv % 2)].h,
                                   s["pw%d" % (pv % 2)].b + 48 * (pv // 2 + 1))
                    rp = s["rp%d" % (j % 2)]
                    rp.inc(gp.dma_start(
                        out=repb[:, j % 2, 0, :],
                        in_=gsc[j].unsqueeze(1).broadcast_to([8, 16, 2048])), 16)
                    rp.inc(gp.dma_start(
                        out=repb[:, j % 2, 1, :],
                        in_=usc[j].unsqueeze(1).broadcast_to([8, 16, 2048])), 16)
                    rp.inc(gp.dma_start(
                        out=repb[:, j % 2, 2, :],
                        in_=dsc[j].unsqueeze(1).broadcast_to([8, 16, 2048])), 16)
                    gp.wait_ge(rp.h, rp.b + 48 * (j // 2 + 1))
                    for mat in range(3):
                        for u in range(2):
                            gp.wait_ge(s["dqx"].h, s["dqx"].b + k + 1)
                            _deq_tail(gp, s["chg"], braw, gbuf, k % 2, ggp,
                                      repb[:, j % 2, mat, 1024 * u:1024 * (u + 1)],
                                      repb[:, j % 2, mat, 1024 * u:1024 * (u + 1)],
                                      pst[:, mat, 1024 * u:1024 * (u + 1)],
                                      pst[:, mat, 2048 + 1024 * u:2048 + 1024 * (u + 1)])
                            s["plc"].inc(gp.engine_nop(), 4)
                            k += 1
                    gp.wait_ge(s["chg"].h, s["chg"].n)
                    pw = s["pw%d" % (j % 2)]
                    pw.inc(gp.dma_start(out=pl_srcs[j][0, 0],
                                        in_=pst[:, 0, :]), 16)
                    pw.inc(gp.dma_start(out=pl_srcs[j][0, 1],
                                        in_=pst[:, 1, :]), 16)
                    pw.inc(gp.dma_start(out=pd_srcs[j][0],
                                        in_=pst[:, 2, :]), 16)
                    pwj = s["pw%d" % (j % 2)]
                    gp.wait_ge(pwj.h, pwj.b + 48 * (j // 2 + 1))
                    s["cc"].inc(gp.collective_compute(
                        "AllGather", ALU.bypass, replica_groups=RG,
                        ins=[pl_srcs[j].opt()], outs=[pl_alls[j].opt()]))
                    s["cc"].inc(gp.collective_compute(
                        "AllGather", ALU.bypass, replica_groups=RG,
                        ins=[pd_srcs[j].opt()], outs=[pd_alls[j].opt()]))
            blk.gpsimd(prod_gp)
            cc_pl0 = s["cc"].n - 16

        with contextlib.ExitStack() as es1:
            E1 = es1.enter_context
            inter = E1(nc.sbuf_tensor(pfx + "inter", [128, 4, TC], BF16))
            gel = E1(nc.sbuf_tensor(pfx + "gel", [128, 2, TC], BF16))
            gsb = E1(nc.sbuf_tensor(pfx + "gsb", [128, TC], BF16))
            usb = E1(nc.sbuf_tensor(pfx + "usb", [128, TC], BF16))
            pgu = E1(nc.sbuf_tensor(pfx + "pgu", [128, 2, 2, 4096], BF16))
            dplc = E1(nc.sbuf_tensor(pfx + "dplc", [128, 2, 2, 4096], BF16))
            psG = E1(nc.psum_tensor(pfx + "psG", [128, 1024], F32))
            psU = E1(nc.psum_tensor(pfx + "psU", [128, 1024], F32))
            psDD = E1(nc.psum_tensor(pfx + "psDD", [128, 1024], F32))

            def con_sy(sy):
                sy.wait_ge(s["pw0"].h, s["pw0"].n)
                sy.wait_ge(s["pw1"].h, s["pw1"].n)
                sy.wait_ge(s["plc"].h, s["plc"].n)
                sy.wait_ge(s["dqx"].h, s["dqx"].n)
                for g in range(32):
                    for br in range(2):
                        b = 2 * g + br
                        j, rk = b // 8, b % 8
                        sy.wait_ge(s["cc"].h, cc_pl0 + 2 * (j + 1))
                        src = pl_alls[j][rk]
                        if b >= 2:
                            sy.wait_ge(s["peg"].h, s["peg"].b + 2 * (b - 1))
                        sm = s["pgl%d" % (b % 2)]
                        sm.inc(sy.dma_start(out=pgu[:, b % 2, 0, :], in_=src[0]), 16)
                        sm.inc(sy.dma_start(out=pgu[:, b % 2, 1, :], in_=src[1]), 16)
                    for dr in range(2):
                        pbd = 2 * g + dr
                        j, rk = pbd // 8, pbd % 8
                        sy.wait_ge(s["cc"].h, cc_pl0 + 2 * (j + 1))
                        if g >= 2 and dr == 0:
                            sy.wait_ge(s["ped"].h, s["ped"].b + 16 * (g - 1))
                        s["pdl%d" % (g % 2)].inc(sy.dma_start(
                            out=dplc[:, g % 2, dr, :],
                            in_=pd_alls[j][rk]), 16)
            blk.sync(con_sy)

            def con_ve(ve):
                for g in range(32):
                    for br in range(2):
                        b = 2 * g + br
                        for par in range(2):
                            idx = 2 * b + par
                            ve.wait_ge(s["peg"].h, s["peg"].b + idx + 1)
                            if idx >= 1:
                                ve.wait_ge(s["gin"].h, s["gin"].b + idx)
                            s["seg"].inc(ve.tensor_copy(usb[:], psU[:]))
                            ve.wait_ge(s["sga"].h, s["sga"].b + idx + 1)
                            s["chv"].inc(ve.tensor_tensor(gel[:, 0, :], gsb[:], gsb[:], op=ALU.mult))
                            ve.wait_ge(s["chv"].h, s["chv"].n)
                            s["chv"].inc(ve.tensor_scalar(gel[:, 0, :], gel[:, 0, :],
                                                          0.044715, 1.0, op0=ALU.mult, op1=ALU.add))
                            ve.wait_ge(s["chv"].h, s["chv"].n)
                            ve.wait_ge(s["gin"].h, s["gin"].b + idx)
                            s["vgl"].inc(ve.tensor_tensor(gel[:, 1, :], gel[:, 0, :], gsb[:], op=ALU.mult))
                    for h in range(DT):
                        ve.wait_ge(s["ped"].h, s["ped"].b + 16 * g + h + 1)
                        s["vda"].inc(ve.tensor_tensor(
                            xT[:, h, :], xT[:, h, :], psDD[:], op=ALU.add))
            blk.vector(con_ve)

            def con_gp(gp):
                for b in range(64):
                    for par in range(2):
                        idx = 2 * b + par
                        slot2 = (b % 2) * 2 + par
                        gp.wait_ge(s["actt"].h, s["actt"].b + idx + 1)
                        s["chg"].inc(gp.tensor_tensor(gel[:, 1, :], gel[:, 1, :], gsb[:], op=ALU.mult))
                        gp.wait_ge(s["chg"].h, s["chg"].n)
                        if b >= 2:
                            gp.wait_ge(s["ped"].h, s["ped"].b + 16 * (b // 2))
                        s["gin"].inc(gp.tensor_tensor(inter[:, slot2, :], gel[:, 1, :],
                                                      usb[:], op=ALU.mult))
            blk.gpsimd(con_gp)

            def ffn_se(se):
                for b in range(64):
                    for par in range(2):
                        idx = 2 * b + par
                        se.wait_ge(s["peg"].h, s["peg"].b + idx + 1)
                        if idx >= 1:
                            se.wait_ge(s["gin"].h, s["gin"].b + idx)
                            se.wait_ge(s["vgl"].h, s["vgl"].b + idx)
                        s["sga"].inc(se.copy(gsb[:], psG[:]))
                        se.wait_ge(s["vgl"].h, s["vgl"].b + idx + 1)
                        s["chs"].inc(se.activation(gel[:, 1, :], gel[:, 1, :], AF.Tanh,
                                                   bias=0.0, scale=0.7978845608028654))
                        se.wait_ge(s["chs"].h, s["chs"].n)
                        s["actt"].inc(se.activation(gel[:, 1, :], gel[:, 1, :], AF.Identity,
                                                    bias=c05[:], scale=0.5))
            blk.scalar(ffn_se)

            def ffn_te(te):
                te.wait_ge(s["se"].h, se_prm)
                for g in range(32):
                    for br in range(2):
                        b = 2 * g + br
                        sm = s["pgl%d" % (b % 2)]
                        te.wait_ge(sm.h, sm.b + 32 * (b // 2 + 1))
                        for par in range(2):
                            idx = 2 * b + par
                            if idx >= 1:
                                te.wait_ge(s["sga"].h, s["sga"].b + idx)
                                te.wait_ge(s["seg"].h, s["seg"].b + idx)
                            for mat, ps in ((0, psG), (1, psU)):
                                vw = pgu[:, b % 2, mat, :].rearrange(
                                    "p (hi pb n) -> p hi pb n", hi=2, pb=8)
                                for th in range(2):
                                    for kti in range(16):
                                        pb, hi = kti >> 1, kti & 1
                                        inst = te.matmul(
                                            ps[:, 512 * th:512 * (th + 1)],
                                            vw[:, hi, pb, par:256:2],
                                            xlnP[:, kti, 512 * th:512 * (th + 1)],
                                            start=(kti == 0), stop=(kti == 15))
                            s["peg"].inc(inst)
                    te.wait_ge(s["gin"].h, s["gin"].b + 4 * (g + 1))
                    smd = s["pdl%d" % (g % 2)]
                    te.wait_ge(smd.h, smd.b + 32 * (g // 2 + 1))
                    dvw = dplc[:, g % 2, :, :].rearrange(
                        "p dr (nib n) -> p dr nib n", nib=2)
                    for h in range(DT):
                        te.wait_ge(s["vda"].h, s["vda"].b + 16 * g + h)
                        for th in range(2):
                            for kk2 in range(4):
                                inst = te.matmul(
                                    psDD[:, 512 * th:512 * (th + 1)],
                                    dvw[:, kk2 >> 1, kk2 & 1, 128 * h:128 * (h + 1)],
                                    inter[:, kk2, 512 * th:512 * (th + 1)],
                                    start=(kk2 == 0), stop=(kk2 == 3))
                        s["ped"].inc(inst)
            blk.tensor(ffn_te)
            vda_all = 32 * DT

        with contextlib.ExitStack() as es2:
            E2 = es2.enter_context
            xrow = E2(nc.sbuf_tensor(pfx + "xrow", [128, 2, D], F32))
            psA2 = E2(nc.psum_tensor(pfx + "psA2", [128, 512], F32))
            psB2 = E2(nc.psum_tensor(pfx + "psB2", [128, 512], F32))
            emit_store_x(nc, blk, s, xT, x_out, xrow, idf, psA2, psB2,
                         ("vda", s["vda"].b + vda_all))


def build_l2():
    nc = bass.Bass("TRN2", target_bir_lowering=False)
    x_in = nc.dram_tensor("x_in", [TC, D], F32, kind="ExternalInput").ap()
    gpk = nc.dram_tensor("gpk", [8, 128, 8, 256], U8, kind="ExternalInput").ap()
    upk = nc.dram_tensor("upk", [8, 128, 8, 256], U8, kind="ExternalInput").ap()
    dpkt = nc.dram_tensor("dpkt", [128, 8, 2048], U8, kind="ExternalInput").ap()
    gsc = nc.dram_tensor("gsc", [8, 8, 2048], F32, kind="ExternalInput").ap()
    usc = nc.dram_tensor("usc", [8, 8, 2048], F32, kind="ExternalInput").ap()
    dsc = nc.dram_tensor("dsc", [8, 8, 2048], F32, kind="ExternalInput").ap()
    x_out = nc.dram_tensor("x_out", [TC, D], F32, kind="ExternalOutput").ap()
    shared = _sems(nc, SHARED_SEMS)
    s = _sem_view(nc, shared, L2_SEMS, "")
    with contextlib.ExitStack() as es:
        E = es.enter_context
        ones_b = E(nc.sbuf_tensor("ones_b", [128, 1], BF16))
        ones_f1 = E(nc.sbuf_tensor("ones_f1", [1, 128], F32))
        idf = E(nc.sbuf_tensor("idf", [128, 128], F32))
        eps_c = E(nc.sbuf_tensor("eps_c", [128, 1], F32))
        blk = E(nc.Block())
        emit_consts(nc, blk, s, ones_b, ones_f1, idf, eps_c)
        _emit_l2(nc, blk, s, ones_b, ones_f1, idf, eps_c,
                 x_in, x_out, gpk, upk, dpkt, gsc, usc, dsc, "")
    return nc


def _host_l2_inputs(inputs):
    def bytes_of(a):
        a = np.ascontiguousarray(a)
        return a.view(np.uint8).reshape(a.shape[0], a.shape[1], 4)[:, :, 0]
    gb = bytes_of(np.asarray(inputs["L0_ffn_gate_W_packed"]))
    ub = bytes_of(np.asarray(inputs["L0_ffn_up_W_packed"]))
    db = bytes_of(np.asarray(inputs["L0_ffn_down_W_packed"]))
    def blkgu(x):
        t = np.ascontiguousarray(x.T)
        t = t.reshape(8, 128, 64, 256)
        return np.ascontiguousarray(t.transpose(2, 1, 0, 3))
    g3 = blkgu(gb)
    u3 = blkgu(ub)
    d_t = np.ascontiguousarray(db.T)
    d3 = np.ascontiguousarray(d_t.reshape(64, 128, 2048).transpose(1, 0, 2))
    def repgu(scales):
        t = np.ascontiguousarray(scales.T)
        t = t.reshape(8, 8, 64, 256).transpose(2, 1, 0, 3)
        return np.ascontiguousarray(t.reshape(64, 8, 2048))
    gs = repgu(np.asarray(inputs["L0_ffn_gate_scales"]))
    us = repgu(np.asarray(inputs["L0_ffn_up_scales"]))
    ds = np.ascontiguousarray(np.asarray(inputs["L0_ffn_down_scales"]).T)
    return g3, u3, d3, gs, us, ds


def _host_l2_core(l2h, c):
    g3, u3, d3, gs, us, ds = l2h
    return {
        "gpk": np.ascontiguousarray(g3[c::8]),
        "upk": np.ascontiguousarray(u3[c::8]),
        "dpkt": np.ascontiguousarray(d3[:, c::8, :]),
        "gsc": np.ascontiguousarray(gs[c::8]),
        "usc": np.ascontiguousarray(us[c::8]),
        "dsc": np.ascontiguousarray(ds.reshape(64, 8, 2048)[c::8]),
    }


def run_l2(x2d, inputs):
    nc = build_l2()
    l2h = _host_l2_inputs(inputs)
    in_maps = []
    for c in range(NC):
        m = {"x_in": np.ascontiguousarray(x2d[TC * c:TC * (c + 1)])}
        m.update(_host_l2_core(l2h, c))
        in_maps.append(m)
    res = run_bass_kernel_spmd(nc, in_maps, core_ids=list(range(NC)))
    return np.concatenate([res.results[c]["x_out"] for c in range(NC)], axis=0)




def _emit_l3(nc, blk, s, ones_b, ones_f1, idf, eps_c,
             x_in, x_out, wq, wk, wv, wg, wo, sel_in, pfx):
    qscr = nc.dram_tensor(pfx + "qscr", [16, 128, TC], BF16, kind="Internal").ap()
    ags_src = nc.dram_tensor(pfx + "ags_src", [128, 2048], F32, kind="Internal").ap()
    ags_dst = nc.dram_tensor(pfx + "ags_dst", [128 * NC, 2048], F32, kind="Internal").ap()
    agd_src = nc.dram_tensor(pfx + "agd_src", [128, 16], F32, kind="Internal").ap()
    agd_dst = nc.dram_tensor(pfx + "agd_dst", [128 * NC, 16], F32, kind="Internal").ap()

    with contextlib.ExitStack() as es:
        E = es.enter_context
        xT = E(nc.sbuf_tensor(pfx + "xT", [128, DT, TC], F32))
        idb = E(nc.sbuf_tensor(pfx + "idb", [128, 128], BF16))
        onesf = E(nc.sbuf_tensor(pfx + "onesf", [128, 128], F32))
        um = E(nc.sbuf_tensor(pfx + "um", [128, 128], BF16))
        selc = E(nc.sbuf_tensor(pfx + "selc", [128, 7], F32))
        oT = E(nc.sbuf_tensor(pfx + "oT", [128, DT, TC], BF16))
        Sall = E(nc.sbuf_tensor(pfx + "Sall", [128, 16, 128], F32))
        gsum = E(nc.sbuf_tensor(pfx + "gsum", [128, 16], F32))
        Dsb = E(nc.sbuf_tensor(pfx + "Dsb", [128, 16], F32))
        glpre = E(nc.sbuf_tensor(pfx + "glpre", [128, 16, 8], F32))
        eglp = E(nc.sbuf_tensor(pfx + "eglp", [128, 16, 8], F32))

        cst_c = s["cst"].n

        def c3_ve(ve):
            ve.memset(onesf[:], 1.0)
            ve.memset(idb[:], 1.0)
            ve.memset(gsum[:], 0.0)
            ve.memset(glpre[:], 0.0)
            s["ve"].inc(ve.memset(um[:], 1.0))
        blk.vector(c3_ve)
        ve_c3 = s["ve"].n

        def c3_gp(gp):
            gp.wait_ge(s["ve"].h, ve_c3)
            gp.affine_select(idb[:], idb[:], pattern=[[-1, 128]],
                             compare_op=ALU.is_equal, fill=0.0, base=0,
                             channel_multiplier=1)
            s["cst"].inc(gp.affine_select(um[:], um[:], pattern=[[1, 128]],
                                          compare_op=ALU.is_ge, fill=0.0, base=0,
                                          channel_multiplier=-1))
            s["dsl"].inc(gp.dma_start(out=selc[:],
                                      in_=sel_in.unsqueeze(0).broadcast_to([128, 7])), 16)
        blk.gpsimd(c3_gp)
        cst_um = s["cst"].n

        with contextlib.ExitStack() as es0:
            E0 = es0.enter_context
            xlnT = E0(nc.sbuf_tensor(pfx + "xlnT", [128, DT, TC], BF16))
            with contextlib.ExitStack() as esp:
                EP = esp.enter_context
                xtm = EP(nc.sbuf_tensor(pfx + "xtm", [128, 2, D], F32))
                xb = EP(nc.sbuf_tensor(pfx + "xb", [128, DT, TC], BF16))
                srow = EP(nc.sbuf_tensor(pfx + "srow", [1, 3, TC], F32))
                tmpf = EP(nc.sbuf_tensor(pfx + "tmpf", [128, 2, 512], F32))
                psA = EP(nc.psum_tensor(pfx + "psA", [128, 512], F32))
                psB = EP(nc.psum_tensor(pfx + "psB", [128, 512], F32))
                psC = EP(nc.psum_tensor(pfx + "psC", [128, 512], F32))
                psD = EP(nc.psum_tensor(pfx + "psD", [128, 512], F32))
                psE = EP(nc.psum_tensor(pfx + "psE", [128, 512], F32))
                psF = EP(nc.psum_tensor(pfx + "psF", [128, 512], F32))
                emit_load_x(nc, blk, s, x_in, xT, xtm, idf, psA, psB)
                ve_ln = emit_ln(nc, blk, s, xT, xlnT, xb, srow, tmpf, ones_b, ones_f1,
                                eps_c, psA, psB, psC, psD, psE, psF, ("se", s["se"].n))

            with contextlib.ExitStack() as ess:
                ES = ess.enter_context
                psQ = ES(nc.psum_tensor(pfx + "psQ", [128, 512], F32))
                psR = ES(nc.psum_tensor(pfx + "psR", [128, 512], F32))
                psPA = ES(nc.psum_tensor(pfx + "psPA", [128, 512], F32))
                psPO = ES(nc.psum_tensor(pfx + "psPO", [128, 512], F32))
                psPS = ES(nc.psum_tensor(pfx + "psPS", [128, 512], F32))
                psPT = ES(nc.psum_tensor(pfx + "psPT", [128, 512], F32))
                wsl = ES(nc.sbuf_tensor(pfx + "wsl", [128, 4, DT, 128], BF16))
                qh = ES(nc.sbuf_tensor(pfx + "qh", [128, TC], BF16))
                kh = ES(nc.sbuf_tensor(pfx + "kh", [128, TC], BF16))
                vh = ES(nc.sbuf_tensor(pfx + "vh", [128, 8, 128], BF16))
                kc = ES(nc.sbuf_tensor(pfx + "kc", [128, 128], BF16))
                khat = ES(nc.sbuf_tensor(pfx + "khat", [128, 128], BF16))
                khtm = ES(nc.sbuf_tensor(pfx + "khtm", [128, 128], BF16))
                pbuf = ES(nc.sbuf_tensor(pfx + "pbuf", [128, 128], BF16))
                sgb = ES(nc.sbuf_tensor(pfx + "sgb", [128, TC], F32))
                Gb = ES(nc.sbuf_tensor(pfx + "Gb", [128, TC], F32))
                e1 = ES(nc.sbuf_tensor(pfx + "e1", [128, 128], F32))
                e2 = ES(nc.sbuf_tensor(pfx + "e2", [128, 128], F32))
                egl = ES(nc.sbuf_tensor(pfx + "egl", [128, 1], F32))
                lga = ES(nc.sbuf_tensor(pfx + "lga", [128, TC], F32))
                lgb = ES(nc.sbuf_tensor(pfx + "lgb", [128, TC], F32))
                Shb = ES(nc.sbuf_tensor(pfx + "Shb", [128, 128], BF16))

                def gl_gp(gp):
                    gp.wait_ge(s["ve"].h, ve_ln)
                    for h in range(16):
                        if h >= 1:
                            gp.wait_ge(s["pjmm"].h, s["pjmm"].b + 8 * h)
                        for wi, wt in ((0, wq), (1, wk), (2, wv), (3, wg)):
                            s["wld"].inc(gp.dma_start(
                                out=wsl[:, wi, :, :],
                                in_=wt.rearrange("(kt p) m -> p kt m", p=128)
                                      [:, :, 128 * h:128 * (h + 1)]), 16)
                        if h >= 1:
                            gp.wait_ge(s["vst"].h, s["vst"].b + 8 * h)
                            s["qwb"].inc(gp.dma_start(out=qscr[h - 1], in_=qh[:]), 16)
                    gp.wait_ge(s["vst"].h, s["vst"].b + 128)
                    s["qwb"].inc(gp.dma_start(out=qscr[15], in_=qh[:]), 16)
                    gp.wait_ge(s["ssb"].h, s["ssb"].b + 128)
                    s["agw"].inc(gp.dma_start(
                        out=ags_src.rearrange("p (a b) -> p a b", a=16), in_=Sall[:]), 16)
                    gp.wait_ge(s["dal"].h, s["dal"].b + 1)
                    s["agw"].inc(gp.dma_start(out=agd_src, in_=Dsb[:]), 16)
                    gp.wait_ge(s["agw"].h, s["agw"].n)
                    s["cc"].inc(gp.collective_compute(
                        "AllGather", ALU.bypass, replica_groups=RG,
                        ins=[ags_src.opt()], outs=[ags_dst.opt()]))
                    s["cc"].inc(gp.collective_compute(
                        "AllGather", ALU.bypass, replica_groups=RG,
                        ins=[agd_src.opt()], outs=[agd_dst.opt()]))
                blk.gpsimd(gl_gp)
                cc_ags = s["cc"].n

                def gl_te(te):
                    te.wait_ge(s["ve"].h, ve_ln)
                    te.wait_ge(s["cst"].h, cst_um)
                    for h in range(16):
                        te.wait_ge(s["wld"].h, s["wld"].b + 64 * (h + 1))
                        if h >= 1:
                            te.wait_ge(s["qwb"].h, s["qwb"].b + 16 * h)
                            te.wait_ge(s["vkc"].h, s["vkc"].b + 8 * h)
                            te.wait_ge(s["pes"].h, s["pes"].b + 8 * h)
                        for gidx in range(8):
                            i = 8 * h + gidx
                            ps = psQ if gidx % 2 == 0 else psR
                            if i >= 2:
                                te.wait_ge(s["pjev"].h, s["pjev"].b + i - 1)
                            if gidx < 2:
                                wi, th = 0, gidx
                            elif gidx < 4:
                                wi, th = 1, gidx - 2
                            elif gidx < 6:
                                wi, th = 2, gidx - 4
                            else:
                                wi, th = 3, gidx - 6
                            if wi == 2:
                                for j in range(4):
                                    tt4 = th * 4 + j
                                    for kt in range(DT):
                                        inst = te.matmul(
                                            ps[:, 128 * j:128 * (j + 1)],
                                            xlnT[:, kt, 128 * tt4:128 * (tt4 + 1)],
                                            wsl[:, 2, kt, :],
                                            start=(j == 0 and kt == 0),
                                            stop=(j == 3 and kt == DT - 1))
                            else:
                                for kt in range(DT):
                                    inst = te.matmul(
                                        ps[:], wsl[:, wi, kt, :],
                                        xlnT[:, kt, 512 * th:512 * (th + 1)],
                                        start=(kt == 0), stop=(kt == DT - 1))
                            s["pjmm"].inc(inst)
                        for ch in range(8):
                            u = 8 * h + ch
                            te.wait_ge(s["vkc"].h, s["vkc"].b + u + 1)
                            s["pea"].inc(te.matmul(psPA[:, 0:128], kc[:],
                                                   qh[:, 128 * ch:128 * (ch + 1)],
                                                   start=True, stop=True))
                            if u >= 1:
                                te.wait_ge(s["sekh"].h, s["sekh"].b + u)
                            s["pet"].inc(te.transpose(
                                psPT[:].bitcast(BF16)[:, 0:128], khat[:], idb[:]))
                            te.wait_ge(s["vmsk"].h, s["vmsk"].b + u + 1)
                            if u >= 1:
                                te.wait_ge(s["soev"].h, s["soev"].b + u)
                            if ch > 0:
                                te.wait_ge(s["ssb"].h, s["ssb"].b + u)
                                te.matmul(psPO[:, 0:128], Shb[:],
                                          qh[:, 128 * ch:128 * (ch + 1)],
                                          start=True, stop=False)
                            s["peo"].inc(te.matmul(psPO[:, 0:128], vh[:, ch, :], pbuf[:],
                                                   start=(ch == 0), stop=True))
                            te.wait_ge(s["sekh"].h, s["sekh"].b + u + 1)
                            if u >= 1:
                                te.wait_ge(s["vst"].h, s["vst"].b + u)
                            s["pes"].inc(te.matmul(psPS[:, 0:128], khtm[:], vh[:, ch, :],
                                                   start=True, stop=True))
                blk.tensor(gl_te)

                def gl_se(se):
                    for h in range(16):
                        for gidx in range(6):
                            i = 8 * h + gidx
                            ps = psQ if gidx % 2 == 0 else psR
                            se.wait_ge(s["pjmm"].h, s["pjmm"].b + i + 1)
                            if gidx < 2:
                                if h >= 1 and gidx == 0:
                                    se.wait_ge(s["qwb"].h, s["qwb"].b + 16 * h)
                                inst = se.activation(qh[:, 512 * gidx:512 * (gidx + 1)], ps[:],
                                                     AF.Identity, bias=0.0, scale=ISQ)
                            elif gidx < 4:
                                if h >= 1 and gidx == 2:
                                    se.wait_ge(s["vkc"].h, s["vkc"].b + 8 * h)
                                inst = se.copy(kh[:, 512 * (gidx - 2):512 * (gidx - 1)], ps[:])
                            else:
                                if h >= 1 and gidx == 4:
                                    se.wait_ge(s["pes"].h, s["pes"].b + 8 * h)
                                inst = se.copy(vh[:, 4 * (gidx - 4):4 * (gidx - 3), :], ps[:])
                            s["pjev"].inc(inst)
                        for th in range(2):
                            ps = psQ if th == 0 else psR
                            se.wait_ge(s["pjmm"].h, s["pjmm"].b + 8 * h + 7 + th)
                            if h >= 1 and th == 0:
                                se.wait_ge(s["vsg"].h, s["vsg"].b + 2 * h)
                            sl = slice(512 * th, 512 * (th + 1))
                            s["chs"].inc(se.activation(lgb[:, sl], ps[:], AF.Abs,
                                                       bias=0.0, scale=1.0))
                            se.wait_ge(s["chs"].h, s["chs"].n)
                            s["chs"].inc(se.activation(lgb[:, sl], lgb[:, sl], AF.Exp,
                                                       bias=0.0, scale=-1.0))
                            se.wait_ge(s["chs"].h, s["chs"].n)
                            s["actl"].inc(se.activation(lgb[:, sl], lgb[:, sl], AF.Ln,
                                                        bias=1.0, scale=1.0))
                        for ch in range(8):
                            u = 8 * h + ch
                            se.wait_ge(s["vcs"].h, s["vcs"].b + u + 1)
                            if u >= 1:
                                se.wait_ge(s["vst"].h, s["vst"].b + u)
                            gsl = slice(128 * ch, 128 * (ch + 1))
                            se.activation(e1[:], Gb[:, gsl], AF.Exp, bias=0.0, scale=1.0 / 16)
                            se.activation(e2[:], Gb[:, gsl], AF.Exp, bias=0.0, scale=-1.0 / 16)
                            s["ecs"].inc(se.activation(egl[:], Gb[:, 128 * ch + 127:128 * (ch + 1)],
                                                       AF.Exp, bias=0.0, scale=1.0 / 16))
                            se.wait_ge(s["pet"].h, s["pet"].b + u + 1)
                            s["sekh"].inc(se.copy(khtm[:], psPT[:].bitcast(BF16)[:, 0:128]))
                            se.wait_ge(s["vst"].h, s["vst"].b + u + 1)
                            s["ssb"].inc(se.copy(Shb[:], Sall[:, h, :]))
                            se.wait_ge(s["peo"].h, s["peo"].b + u + 1)
                            s["soev"].inc(se.copy(oT[:, h, gsl], psPO[:, 0:128]))
                    se.wait_ge(s["vst"].h, s["vst"].b + 128)
                    s["dal"].inc(se.activation(Dsb[:], gsum[:], AF.Exp, bias=0.0, scale=1.0 / 16))
                blk.scalar(gl_se)

                def gl_ve(ve):
                    ve.wait_ge(s["cst"].h, cst_um)
                    for h in range(16):
                        for th in range(2):
                            ps = psQ if th == 0 else psR
                            sl = slice(512 * th, 512 * (th + 1))
                            ve.wait_ge(s["chs"].h, s["chs"].b + (4 * h + 2 * th + 1))
                            if h >= 1 and th == 0:
                                ve.wait_ge(s["vcs"].h, s["vcs"].b + 8 * h)
                            s["chv"].inc(ve.tensor_scalar(lga[:, sl], ps[:], 0.0, None,
                                                          op0=ALU.min))
                            s["pjev"].inc(ve.engine_nop())
                            ve.wait_ge(s["actl"].h, s["actl"].b + 2 * h + th + 1)
                            ve.wait_ge(s["chv"].h, s["chv"].n)
                            s["vsg"].inc(ve.scalar_tensor_tensor(sgb[:, sl], lgb[:, sl], -1.0,
                                                                 lga[:, sl], op0=ALU.mult,
                                                                 op1=ALU.add))
                        for ch in range(8):
                            u = 8 * h + ch
                            gsl = slice(128 * ch, 128 * (ch + 1))
                            ve.wait_ge(s["vsg"].h, s["vsg"].b + 2 * h + 1 + (1 if ch >= 4 else 0))
                            if h >= 1 and ch == 0:
                                ve.wait_ge(s["ecs"].h, s["ecs"].b + 8 * h)
                            s["vcs"].inc(ve.tensor_tensor_scan(Gb[:, gsl], onesf[:], sgb[:, gsl],
                                                               0.0, op0=ALU.mult, op1=ALU.add))
                            ve.wait_ge(s["ecs"].h, s["ecs"].b + u + 1)
                            s["chv"].inc(ve.tensor_tensor(qh[:, gsl], qh[:, gsl], e1[:], op=ALU.mult))
                            s["chv"].inc(ve.tensor_tensor(kc[:], kh[:, gsl], e2[:], op=ALU.mult))
                            ve.wait_ge(s["chv"].h, s["chv"].n)
                            ve.wait_ge(s["pea"].h, s["pea"].b + u)
                            s["vkc"].inc(ve.tensor_scalar(khat[:], kc[:], egl[:, 0:1], None,
                                                          op0=ALU.mult))
                            ve.wait_ge(s["pea"].h, s["pea"].b + u + 1)
                            if u >= 1:
                                ve.wait_ge(s["peo"].h, s["peo"].b + u)
                            s["vmsk"].inc(ve.tensor_tensor(pbuf[:], psPA[:, 0:128], um[:],
                                                           op=ALU.mult))
                            ve.wait_ge(s["pes"].h, s["pes"].b + u + 1)
                            if ch == 0:
                                s["chv"].inc(ve.tensor_scalar(Sall[:, h, :], psPS[:, 0:128],
                                                              1.0, None, op0=ALU.mult))
                            else:
                                s["chv"].inc(ve.scalar_tensor_tensor(
                                    Sall[:, h, :], Sall[:, h, :], egl[:, 0:1], psPS[:, 0:128],
                                    op0=ALU.mult, op1=ALU.add))
                            ve.wait_ge(s["chv"].h, s["chv"].n)
                            s["chv"].inc(ve.tensor_copy(glpre[:, h, ch:ch + 1], gsum[:, h:h + 1]))
                            ve.wait_ge(s["chv"].h, s["chv"].n)
                            s["vst"].inc(ve.tensor_tensor(gsum[:, h:h + 1], gsum[:, h:h + 1],
                                                          Gb[:, 128 * ch + 127:128 * (ch + 1)],
                                                          op=ALU.add))
                blk.vector(gl_ve)

        with contextlib.ExitStack() as esr:
            ER = esr.enter_context
            Pbuf = ER(nc.sbuf_tensor(pfx + "Pbuf", [128, 16, 128], F32))
            Sin = ER(nc.sbuf_tensor(pfx + "Sin", [128, 16, 128], F32))
            Sjb = ER(nc.sbuf_tensor(pfx + "Sjb", [128, 2, 16, 128], F32))
            Djall = ER(nc.sbuf_tensor(pfx + "Djall", [128, 8, 16], F32))
            qld = ER(nc.sbuf_tensor(pfx + "qld", [128, 2, TC], BF16))
            csd = ER(nc.sbuf_tensor(pfx + "csd", [128, 2, 128], BF16))
            psX = ER(nc.psum_tensor(pfx + "psX", [128, 512], F32))
            psY = ER(nc.psum_tensor(pfx + "psY", [128, 512], F32))

            def rl_sy(sy):
                sy.wait_ge(s["cc"].h, cc_ags)
                s["djl"].inc(sy.dma_start(
                    out=Djall[:],
                    in_=agd_dst.rearrange("(j p) d -> p j d", p=128)), 16)
                for j in range(7):
                    if j >= 2:
                        sy.wait_ge(s["rel"].h, s["rel"].b + j - 1)
                    s["sjl%d" % (j % 2)].inc(sy.dma_start(
                        out=Sjb[:, j % 2, :, :],
                        in_=ags_dst[128 * j:128 * (j + 1), :]
                            .rearrange("p (a b) -> p a b", a=16)), 16)
                for h in range(16):
                    sy.wait_ge(s["qwb"].h, s["qwb"].b + 16 * (h + 1))
                    if h >= 2:
                        sy.wait_ge(s["pco"].h, s["pco"].b + 8 * (h - 1))
                    s["cl%d" % (h % 2)].inc(sy.dma_start(
                        out=qld[:, h % 2, :], in_=qscr[h]), 16)
            blk.sync(rl_sy)

            def rl_ve(ve):
                ve.memset(Sin[:], 0.0)
                ve.wait_ge(s["djl"].h, s["djl"].b + 16)
                for j in range(7):
                    ve.wait_ge(s["sjl%d" % (j % 2)].h, s["sjl%d" % (j % 2)].b + 16 * (j // 2 + 1))
                    if j >= 1:
                        ve.wait_ge(s["rel"].h, s["rel"].b + j)
                    if j == 0:
                        s["chv"].inc(ve.tensor_scalar(_r(Pbuf[:]), _r(Sjb[:, 0, :, :]),
                                                      1.0, None, op0=ALU.mult))
                    else:
                        s["chv"].inc(ve.tensor_tensor(
                            Pbuf[:], Pbuf[:],
                            Djall[:, j, :].unsqueeze(2).broadcast_to([128, 16, 128]),
                            op=ALU.mult))
                        ve.wait_ge(s["chv"].h, s["chv"].n)
                        s["chv"].inc(ve.tensor_tensor(_r(Pbuf[:]), _r(Pbuf[:]),
                                                      _r(Sjb[:, j % 2, :, :]), op=ALU.add))
                    ve.wait_ge(s["chv"].h, s["chv"].n)
                    ve.wait_ge(s["dsl"].h, s["dsl"].b + 16)
                    s["rel"].inc(ve.scalar_tensor_tensor(_r(Sin[:]), _r(Pbuf[:]), selc[:, j:j + 1],
                                                         _r(Sin[:]), op0=ALU.mult, op1=ALU.add))
            blk.vector(rl_ve)

            def rl_se(se):
                se.wait_ge(s["vst"].h, s["vst"].b + 128)
                s["epx"].inc(se.activation(_r(eglp[:]), _r(glpre[:]), AF.Exp,
                                           bias=0.0, scale=1.0 / 16))
            blk.scalar(rl_se)

            def co_ve(ve):
                ve.wait_ge(s["rel"].h, s["rel"].b + 7)
                ve.wait_ge(s["epx"].h, s["epx"].b + 1)
                for h in range(16):
                    ve.wait_ge(s["cl%d" % (h % 2)].h, s["cl%d" % (h % 2)].b + 16 * (h // 2 + 1))
                    for ch in range(8):
                        u = 8 * h + ch
                        if u >= 2:
                            ve.wait_ge(s["pco"].h, s["pco"].b + u - 1)
                        s["vcp"].inc(ve.tensor_scalar(csd[:, u % 2, :], Sin[:, h, :],
                                                      eglp[:, h, ch:ch + 1], None,
                                                      op0=ALU.mult))
                        ve.wait_ge(s["pco"].h, s["pco"].b + u + 1)
                        s["vca"].inc(ve.tensor_tensor(
                            oT[:, h, 128 * ch:128 * (ch + 1)],
                            oT[:, h, 128 * ch:128 * (ch + 1)],
                            (psX if u % 2 == 0 else psY)[:, 0:128], op=ALU.add))
            blk.vector(co_ve)

            def co_te(te):
                for h in range(16):
                    for ch in range(8):
                        u = 8 * h + ch
                        te.wait_ge(s["vcp"].h, s["vcp"].b + u + 1)
                        if u >= 2:
                            te.wait_ge(s["vca"].h, s["vca"].b + u - 1)
                        s["pco"].inc(te.matmul((psX if u % 2 == 0 else psY)[:, 0:128],
                                               csd[:, u % 2, :],
                                               qld[:, h % 2, 128 * ch:128 * (ch + 1)],
                                               start=True, stop=True))
            blk.tensor(co_te)
            vca_all = 128

        with contextlib.ExitStack() as esw:
            EW = esw.enter_context
            wo_sb = EW(nc.sbuf_tensor(pfx + "wo_sb", [128, DT, D], BF16))
            psW1 = EW(nc.psum_tensor(pfx + "psW1", [128, 512], F32))
            psW2 = EW(nc.psum_tensor(pfx + "psW2", [128, 512], F32))

            def wo_gp(gp):
                gp.wait_ge(s["vca"].h, s["vca"].b + vca_all)
                for q in range(4):
                    s["wold"].inc(gp.dma_start(
                        out=wo_sb[:, 4 * q:4 * (q + 1), :],
                        in_=wo.rearrange("(kt p) m -> p kt m", p=128)[:, 4 * q:4 * (q + 1), :]), 16)
            blk.gpsimd(wo_gp)
            ldg_wo = s["wold"].n

            def wo_te(te):
                te.wait_ge(s["wold"].h, ldg_wo)
                te.wait_ge(s["vca"].h, s["vca"].b + vca_all)
                for ho in range(DT):
                    for th in range(2):
                        i = ho * 2 + th
                        ps = psW1 if i % 2 == 0 else psW2
                        if i >= 2:
                            te.wait_ge(s["vwo"].h, s["vwo"].b + i - 1)
                        for kt in range(DT):
                            inst = te.matmul(ps[:], wo_sb[:, kt, 128 * ho:128 * (ho + 1)],
                                             oT[:, kt, 512 * th:512 * (th + 1)],
                                             start=(kt == 0), stop=(kt == DT - 1))
                        s["pwo"].inc(inst)
            blk.tensor(wo_te)

            def wo_ve(ve):
                for ho in range(DT):
                    for th in range(2):
                        i = ho * 2 + th
                        ps = psW1 if i % 2 == 0 else psW2
                        ve.wait_ge(s["pwo"].h, s["pwo"].b + i + 1)
                        s["vwo"].inc(ve.tensor_tensor(
                            xT[:, ho, 512 * th:512 * (th + 1)],
                            xT[:, ho, 512 * th:512 * (th + 1)], ps[:], op=ALU.add))
            blk.vector(wo_ve)
            ve_res = s["vwo"].n

            with contextlib.ExitStack() as esf:
                EF = esf.enter_context
                xrow = EF(nc.sbuf_tensor(pfx + "xrow", [128, 2, D], F32))
                emit_store_x(nc, blk, s, xT, x_out, xrow, idf, psW1, psW2, ("vwo", ve_res))


def build_l3():
    nc = bass.Bass("TRN2", target_bir_lowering=False)
    x_in = nc.dram_tensor("x_in", [TC, D], F32, kind="ExternalInput").ap()
    wq = nc.dram_tensor("wq", [D, D], F32, kind="ExternalInput").ap()
    wk = nc.dram_tensor("wk", [D, D], F32, kind="ExternalInput").ap()
    wv = nc.dram_tensor("wv", [D, D], F32, kind="ExternalInput").ap()
    wg = nc.dram_tensor("wg", [D, D], F32, kind="ExternalInput").ap()
    wo = nc.dram_tensor("wo", [D, D], F32, kind="ExternalInput").ap()
    sel_in = nc.dram_tensor("sel_in", [7], F32, kind="ExternalInput").ap()
    x_out = nc.dram_tensor("x_out", [TC, D], F32, kind="ExternalOutput").ap()
    shared = _sems(nc, SHARED_SEMS)
    s = _sem_view(nc, shared, L3_SEMS, "")
    with contextlib.ExitStack() as es:
        E = es.enter_context
        ones_b = E(nc.sbuf_tensor("ones_b", [128, 1], BF16))
        ones_f1 = E(nc.sbuf_tensor("ones_f1", [1, 128], F32))
        idf = E(nc.sbuf_tensor("idf", [128, 128], F32))
        eps_c = E(nc.sbuf_tensor("eps_c", [128, 1], F32))
        blk = E(nc.Block())
        emit_consts(nc, blk, s, ones_b, ones_f1, idf, eps_c)
        _emit_l3(nc, blk, s, ones_b, ones_f1, idf, eps_c,
                 x_in, x_out, wq, wk, wv, wg, wo, sel_in, "")
    return nc


def run_l3(x2d, inputs):
    nc = build_l3()
    in_maps = []
    for c in range(NC):
        sel = np.zeros(7, np.float32)
        if c >= 1:
            sel[c - 1] = 1.0
        in_maps.append({
            "x_in": np.ascontiguousarray(x2d[TC * c:TC * (c + 1)]),
            "wq": np.asarray(inputs["L1_attn_Wq"]),
            "wk": np.asarray(inputs["L1_attn_Wk"]),
            "wv": np.asarray(inputs["L1_attn_Wv"]),
            "wg": np.asarray(inputs["L1_attn_Wg"]),
            "wo": np.asarray(inputs["L1_attn_Wo"]),
            "sel_in": sel,
        })
    res = run_bass_kernel_spmd(nc, in_maps, core_ids=list(range(NC)))
    return np.concatenate([res.results[c]["x_out"] for c in range(NC)], axis=0)



def _emit_l4(nc, blk, s, ones_b, ones_f1, idf, eps_c,
             x_in, x_out, rwT, wgT, wuT, wdT, agp, adt, bgp, bdt, pfx):
    mixd = nc.dram_tensor(pfx + "mixd", [8192, TC], BF16, kind="Internal").ap()
    wscr = nc.dram_tensor(pfx + "wscr", [8, TC], F32, kind="Internal").ap()

    with contextlib.ExitStack() as es:
        E = es.enter_context
        idb = E(nc.sbuf_tensor(pfx + "idb", [128, 128], BF16))
        ybuf = E(nc.sbuf_tensor(pfx + "ybuf", [128, TC], BF16))
        wtok = E(nc.sbuf_tensor(pfx + "wtok", [128, 8, 8], F32))

        def c2_ve(ve):
            s["ve"].inc(ve.memset(idb[:], 1.0))
        blk.vector(c2_ve)
        ve_idb = s["ve"].n

        def c2_gp(gp):
            gp.wait_ge(s["ve"].h, ve_idb)
            s["cst"].inc(gp.affine_select(idb[:], idb[:], pattern=[[-1, 128]],
                                          compare_op=ALU.is_equal, fill=0.0, base=0,
                                          channel_multiplier=1))
        blk.gpsimd(c2_gp)
        cst_c = s["cst"].n

        with contextlib.ExitStack() as es1:
            E1 = es1.enter_context
            xlnT = E1(nc.sbuf_tensor(pfx + "xlnT", [128, DT, TC], BF16))

            with contextlib.ExitStack() as es0:
                E0 = es0.enter_context
                xT0 = E0(nc.sbuf_tensor(pfx + "xT0", [128, DT, TC], F32))
                xtm = E0(nc.sbuf_tensor(pfx + "xtm", [128, 2, D], F32))
                xb = E0(nc.sbuf_tensor(pfx + "xb", [128, DT, TC], BF16))
                srow = E0(nc.sbuf_tensor(pfx + "srow", [1, 3, TC], F32))
                tmpf = E0(nc.sbuf_tensor(pfx + "tmpf", [128, 2, 512], F32))
                psA = E0(nc.psum_tensor(pfx + "psA", [128, 512], F32))
                psB = E0(nc.psum_tensor(pfx + "psB", [128, 512], F32))
                psC = E0(nc.psum_tensor(pfx + "psC", [128, 512], F32))
                psD = E0(nc.psum_tensor(pfx + "psD", [128, 512], F32))
                psE = E0(nc.psum_tensor(pfx + "psE", [128, 512], F32))
                psF = E0(nc.psum_tensor(pfx + "psF", [128, 512], F32))
                emit_load_x(nc, blk, s, x_in, xT0, xtm, idf, psA, psB)
                ve_ln = emit_ln(nc, blk, s, xT0, xlnT, xb, srow, tmpf, ones_b, ones_f1,
                                eps_c, psA, psB, psC, psD, psE, psF, ("se", s["se"].n))

            webc = E1(nc.sbuf_tensor(pfx + "webc", [128, 8, TC], BF16))
            zbg = E1(nc.sbuf_tensor(pfx + "zbg", [32, 8, TC], BF16))
            zbu = E1(nc.sbuf_tensor(pfx + "zbu", [32, 8, TC], BF16))
            with contextlib.ExitStack() as esr:
                ER = esr.enter_context
                rw_sb = ER(nc.sbuf_tensor(pfx + "rw_sb", [128, DT, 8], BF16))
                prb = ER(nc.sbuf_tensor(pfx + "prb", [128, 8, 8], F32))
                lgt = ER(nc.sbuf_tensor(pfx + "lgt", [128, 8, 8], F32))
                eqa = ER(nc.sbuf_tensor(pfx + "eqa", [128, 8, 8], F32))
                rcl = ER(nc.sbuf_tensor(pfx + "rcl", [128, 4], F32))
                wT8 = ER(nc.sbuf_tensor(pfx + "wT8", [8, TC], F32))
                agp_sb = ER(nc.sbuf_tensor(pfx + "agp_sb", [128, 2, DT, 128], BF16))
                ztmp = ER(nc.sbuf_tensor(pfx + "ztmp", [128, TC], BF16))
                ztmpn = ER(nc.sbuf_tensor(pfx + "ztmpn", [128, TC], BF16))
                psR = ER(nc.psum_tensor(pfx + "psR", [128, 64], F32))
                psW = ER(nc.psum_tensor(pfx + "psW", [8, TC], F32))
                psZa = ER(nc.psum_tensor(pfx + "psZa", [128, TC], F32))
                psZb = ER(nc.psum_tensor(pfx + "psZb", [128, TC], F32))

                def rz_gp(gp):
                    gp.wait_ge(s["ve"].h, ve_ln)
                    s["rwl"].inc(gp.dma_start(
                        out=rw_sb[:], in_=rwT.rearrange("(kt p) e -> p kt e", p=128)), 16)
                    for mi in range(2):
                        s["agl"].inc(gp.dma_start(
                            out=agp_sb[:, mi, :, :],
                            in_=agp[mi].rearrange("(kt p) c -> p kt c", p=128)), 16)
                    gp.wait_ge(s["wwr"].h, s["wwr"].b + 16)
                    for e in range(8):
                        s["wbc"].inc(gp.dma_start(
                            out=webc[:, e, :],
                            in_=wscr[e:e + 1, :].broadcast_to([128, TC])), 16)
                blk.gpsimd(rz_gp)

                def rz_te(te):
                    te.wait_ge(s["ve"].h, ve_ln)
                    te.wait_ge(s["rwl"].h, s["rwl"].b + 16)
                    for tt in range(8):
                        for kt in range(DT):
                            inst = te.matmul(psR[:, 8 * tt:8 * (tt + 1)],
                                             xlnT[:, kt, 128 * tt:128 * (tt + 1)],
                                             rw_sb[:, kt, :],
                                             start=(tt == 0 and kt == 0),
                                             stop=(tt == 7 and kt == DT - 1))
                    s["rpe"].inc(inst)
                    te.wait_ge(s["agl"].h, s["agl"].b + 32)
                    for mi, ps in ((0, psZa), (1, psZb)):
                        for th in range(2):
                            for kt in range(DT):
                                inst = te.matmul(ps[:, 512 * th:512 * (th + 1)],
                                                 agp_sb[:, mi, kt, :],
                                                 xlnT[:, kt, 512 * th:512 * (th + 1)],
                                                 start=(kt == 0), stop=(kt == DT - 1))
                            s["zpe"].inc(inst)
                    te.wait_ge(s["rve"].h, s["rve"].b + 1)
                    for tt in range(8):
                        inst = te.transpose(psW[0:8, 128 * tt:128 * (tt + 1)],
                                            wtok[:, tt, :], idf[:])
                    s["rpw"].inc(inst)
                blk.tensor(rz_te)

                def rz_se(se):
                    se.wait_ge(s["rpe"].h, s["rpe"].b + 1)
                    s["rse"].inc(se.activation(_r(prb[:]), psR[:], AF.Exp,
                                               bias=0.0, scale=1.0))
                    se.wait_ge(s["zpe"].h, s["zpe"].b + 2)
                    s["zev"].inc(se.activation(ztmp[:], psZa[:], AF.Identity,
                                               bias=0.0, scale=2.0))
                    s["zev"].inc(se.activation(ztmpn[:], psZa[:], AF.Identity,
                                               bias=0.0, scale=-2.0))
                    se.wait_ge(s["zpe"].h, s["zpe"].b + 4)
                    se.wait_ge(s["zsh0"].h, s["zsh0"].b + 240)
                    s["zev"].inc(se.activation(ztmp[:], psZb[:], AF.Identity,
                                               bias=0.0, scale=2.0))
                    s["zev"].inc(se.activation(ztmpn[:], psZb[:], AF.Identity,
                                               bias=0.0, scale=-2.0))
                    se.wait_ge(s["rpw"].h, s["rpw"].b + 1)
                    s["wev"].inc(se.activation(wT8[:], psW[:], AF.Identity,
                                               bias=0.0, scale=1.0))
                blk.scalar(rz_se)

                def rz_ve(ve):
                    ve.memset(_r(zbg[:]), 0.0)
                    s["zms"].inc(ve.memset(_r(zbu[:]), 0.0))
                    ve.wait_ge(s["rse"].h, s["rse"].b + 1)
                    for tt in range(8):
                        p8 = prb[:, tt, :]
                        s["chv"].inc(ve.reduce_max(rcl[:, 0:1], p8, axis=AX.X))
                        ve.wait_ge(s["chv"].h, s["chv"].n)
                        s["chv"].inc(ve.tensor_scalar(eqa[:, tt, :], p8, rcl[:, 0:1],
                                                      None, op0=ALU.is_equal))
                        ve.wait_ge(s["chv"].h, s["chv"].n)
                        s["chv"].inc(ve.scalar_tensor_tensor(lgt[:, tt, :], eqa[:, tt, :],
                                                             -1e30, p8, op0=ALU.mult,
                                                             op1=ALU.add))
                        ve.wait_ge(s["chv"].h, s["chv"].n)
                        s["chv"].inc(ve.reduce_max(rcl[:, 1:2], lgt[:, tt, :], axis=AX.X))
                        ve.wait_ge(s["chv"].h, s["chv"].n)
                        s["chv"].inc(ve.tensor_scalar(lgt[:, tt, :], lgt[:, tt, :],
                                                      rcl[:, 1:2], None, op0=ALU.is_equal))
                        ve.wait_ge(s["chv"].h, s["chv"].n)
                        s["chv"].inc(ve.tensor_tensor(eqa[:, tt, :], eqa[:, tt, :],
                                                      lgt[:, tt, :], op=ALU.add))
                        ve.wait_ge(s["chv"].h, s["chv"].n)
                        s["chv"].inc(ve.tensor_tensor(rcl[:, 2:3], rcl[:, 0:1],
                                                      rcl[:, 1:2], op=ALU.add))
                        ve.wait_ge(s["chv"].h, s["chv"].n)
                        s["chv"].inc(ve.reciprocal(rcl[:, 3:4], rcl[:, 2:3]))
                        ve.wait_ge(s["chv"].h, s["chv"].n)
                        s["chv"].inc(ve.tensor_tensor(wtok[:, tt, :], p8, eqa[:, tt, :],
                                                      op=ALU.mult))
                        ve.wait_ge(s["chv"].h, s["chv"].n)
                        inst = ve.tensor_scalar(wtok[:, tt, :], wtok[:, tt, :],
                                                rcl[:, 3:4], None, op0=ALU.mult)
                        if tt == 7:
                            s["rve"].inc(inst)
                        else:
                            s["chv"].inc(inst)
                            ve.wait_ge(s["chv"].h, s["chv"].n)
                blk.vector(rz_ve)

                def rz_sy(sy):
                    sy.wait_ge(s["zms"].h, s["zms"].b + 1)
                    for mi, zb, sh in ((0, zbg, "zsh0"), (1, zbu, "zsh1")):
                        sy.wait_ge(s["zev"].h, s["zev"].b + 2 * (mi + 1))
                        for e in range(8):
                            s[sh].inc(sy.dma_start(
                                out=zb[0:16, e, :], in_=ztmp[16 * e:16 * (e + 1), :]), 16)
                        for e in range(1, 8):
                            s[sh].inc(sy.dma_start(
                                out=zb[16:32, e, :],
                                in_=ztmpn[16 * (e - 1):16 * e, :]), 16)
                    sy.wait_ge(s["wev"].h, s["wev"].b + 1)
                    s["wwr"].inc(sy.dma_start(out=wscr, in_=wT8[:]), 16)
                blk.sync(rz_sy)

            with contextlib.ExitStack() as esa:
                EA = esa.enter_context
                wgt = EA(nc.sbuf_tensor(pfx + "wgt", [128, 2, DT, 128], BF16))
                wut = EA(nc.sbuf_tensor(pfx + "wut", [128, 2, DT, 128], BF16))
                adt_sb = EA(nc.sbuf_tensor(pfx + "adt_sb", [128, 2, 8, 16], BF16))
                bgt = EA(nc.sbuf_tensor(pfx + "bgt", [32, 2, 8, 128], BF16))
                but = EA(nc.sbuf_tensor(pfx + "but", [32, 2, 8, 128], BF16))
                sg = EA(nc.sbuf_tensor(pfx + "sg", [128, 2, TC], BF16))
                heb = EA(nc.sbuf_tensor(pfx + "heb", [128, 2, TC], BF16))
                meb = EA(nc.sbuf_tensor(pfx + "meb", [128, 2, TC], BF16))
                mix_ft = EA(nc.sbuf_tensor(pfx + "mix_ft", [128, 2, TC], F32))
                psG = EA(nc.psum_tensor(pfx + "psG", [128, TC], F32))
                psU = EA(nc.psum_tensor(pfx + "psU", [128, TC], F32))
                psY = EA(nc.psum_tensor(pfx + "psY", [128, TC], F32))

                def ld4(gp, ft):
                    sl = ft % 2
                    sm = s["wg%d" % sl]
                    sm.inc(gp.dma_start(
                        out=wgt[:, sl, :, :],
                        in_=wgT[:, 128 * ft:128 * (ft + 1)]
                            .rearrange("(kt p) f -> p kt f", p=128)), 16)
                    sm.inc(gp.dma_start(
                        out=wut[:, sl, :, :],
                        in_=wuT[:, 128 * ft:128 * (ft + 1)]
                            .rearrange("(kt p) f -> p kt f", p=128)), 16)
                    sm.inc(gp.dma_start(
                        out=adt_sb[:, sl, :, :],
                        in_=adt[128 * ft:128 * (ft + 1)]), 16)
                    sm.inc(gp.dma_start(
                        out=bgt[:, sl, :, :],
                        in_=bgp[0][:, :, 128 * ft:128 * (ft + 1)]), 16)
                    sm.inc(gp.dma_start(
                        out=but[:, sl, :, :],
                        in_=bgp[1][:, :, 128 * ft:128 * (ft + 1)]), 16)

                def a_gp(gp):
                    gp.wait_ge(s["zpe"].h, s["zpe"].b + 4)
                    gp.wait_ge(s["zsh0"].h, s["zsh0"].b + 240)
                    gp.wait_ge(s["zsh1"].h, s["zsh1"].b + 240)
                    gp.wait_ge(s["rpw"].h, s["rpw"].b + 1)
                    gp.wait_ge(s["wwr"].h, s["wwr"].b + 16)
                    ld4(gp, 0)
                    ld4(gp, 1)
                    gp.wait_ge(s["wbc"].h, s["wbc"].b + 128)
                    for ft in range(64):
                        for e in range(8):
                            gp.wait_ge(s["vhe"].h, s["vhe"].b + 16 * ft + 2 * e + 2)
                            if 8 * ft + e >= 2:
                                gp.wait_ge(s["vacc"].h, s["vacc"].b + 8 * ft + e - 1)
                            s["gme"].inc(gp.tensor_tensor(
                                meb[:, e % 2, :], heb[:, e % 2, :], webc[:, e, :],
                                op=ALU.mult))
                        gp.wait_ge(s["vacc"].h, s["vacc"].b + 8 * (ft + 1))
                        s["mxw%d" % (ft % 2)].inc(gp.dma_start(
                            out=mixd[128 * ft:128 * (ft + 1), :],
                            in_=mix_ft[:, ft % 2, :]), 16)
                        if ft + 2 < 64:
                            gp.wait_ge(s["pmm"].h, s["pmm"].b + 16 * (ft + 1))
                            gp.wait_ge(s["pyy"].h, s["pyy"].b + 8 * (ft + 1))
                            ld4(gp, ft + 2)
                blk.gpsimd(a_gp)

                def a_te(te):
                    te.wait_ge(s["zsh0"].h, s["zsh0"].b + 240)
                    te.wait_ge(s["zsh1"].h, s["zsh1"].b + 240)
                    te.wait_ge(s["wev"].h, s["wev"].b + 1)
                    for ft in range(64):
                        sl = ft % 2
                        te.wait_ge(s["wg%d" % sl].h, s["wg%d" % sl].b + 80 * (ft // 2 + 1))
                        if ft > 0:
                            te.wait_ge(s["slu"].h, s["slu"].b + 16 * ft)
                            te.wait_ge(s["vhe"].h, s["vhe"].b + 16 * ft)
                        for wsb, ps in ((wgt, psG), (wut, psU)):
                            for th in range(2):
                                for kt in range(DT):
                                    te.matmul(ps[:, 512 * th:512 * (th + 1)],
                                              wsb[:, sl, kt, :],
                                              xlnT[:, kt, 512 * th:512 * (th + 1)],
                                              start=(kt == 0), stop=False,
                                              skip_group_check=True)
                        for e in range(8):
                            for th in range(2):
                                if e > 0:
                                    te.wait_ge(s["slu"].h,
                                               s["slu"].b + 16 * ft + 2 * (e - 1) + th + 1)
                                    te.wait_ge(s["vhe"].h,
                                               s["vhe"].b + 16 * ft + 2 * (e - 1) + th + 1)
                                for bsb, zb, ps in ((bgt, zbg, psG), (but, zbu, psU)):
                                    inst = te.matmul(
                                        ps[:, 512 * th:512 * (th + 1)],
                                        bsb[:, sl, e, :],
                                        zb[:, e, 512 * th:512 * (th + 1)],
                                        start=False, stop=(e == 7),
                                        skip_group_check=True)
                                s["pmm"].inc(inst)
                            if e > 0:
                                te.wait_ge(s["vhe"].h, s["vhe"].b + 16 * ft + 2 * e)
                                for tt in range(8):
                                    inst = te.matmul(
                                        psY[:, 128 * tt + 16 * (e - 1):
                                            128 * tt + 16 * e],
                                        heb[:, (e - 1) % 2, 128 * tt:128 * (tt + 1)],
                                        adt_sb[:, sl, e - 1, :],
                                        start=(ft == 0 and e == 1 and tt % 4 == 0),
                                        stop=False)
                                s["pyy"].inc(inst)
                        te.wait_ge(s["vhe"].h, s["vhe"].b + 16 * ft + 16)
                        for tt in range(8):
                            inst = te.matmul(
                                psY[:, 128 * tt + 112:128 * tt + 128],
                                heb[:, 7 % 2, 128 * tt:128 * (tt + 1)],
                                adt_sb[:, sl, 7, :],
                                start=False,
                                stop=(ft == 63 and tt % 4 == 3))
                            s["pyy"].inc(inst) if tt == 7 else None
                blk.tensor(a_te)

                def a_se(se):
                    for ft in range(64):
                        for e in range(8):
                            for th in range(2):
                                q = 16 * ft + 2 * e + th
                                se.wait_ge(s["pmm"].h, s["pmm"].b + q + 1)
                                if q >= 4:
                                    se.wait_ge(s["vhe"].h, s["vhe"].b + q - 3)
                                s["slu"].inc(se.activation(
                                    sg[:, e % 2, 512 * th:512 * (th + 1)],
                                    psG[:, 512 * th:512 * (th + 1)],
                                    AF.Silu, bias=0.0, scale=1.0))
                    se.wait_ge(s["pyy"].h, s["pyy"].b + 512)
                    s["yev"].inc(se.activation(ybuf[:], psY[:], AF.Identity,
                                               bias=0.0, scale=2.0))
                blk.scalar(a_se)

                def a_ve(ve):
                    for ft in range(64):
                        if ft >= 2:
                            ve.wait_ge(s["mxw%d" % (ft % 2)].h, s["mxw%d" % (ft % 2)].b + 16 * (ft // 2))
                        s["chv"].inc(ve.memset(mix_ft[:, ft % 2, :], 0.0))

                        def acc(e2):
                            ve.wait_ge(s["gme"].h, s["gme"].b + 8 * ft + e2 + 1)
                            ve.wait_ge(s["chv"].h, s["chv"].n)
                            if e2 > 0:
                                ve.wait_ge(s["vacc"].h, s["vacc"].n)
                            s["vacc"].inc(ve.tensor_tensor(
                                mix_ft[:, ft % 2, :], mix_ft[:, ft % 2, :],
                                meb[:, e2 % 2, :], op=ALU.add))
                        for e in range(8):
                            for th in range(2):
                                q = 16 * ft + 2 * e + th
                                ve.wait_ge(s["slu"].h, s["slu"].b + q + 1)
                                if th == 0 and 8 * ft + e >= 2:
                                    ve.wait_ge(s["gme"].h, s["gme"].b + 8 * ft + e - 1)
                                    ve.wait_ge(s["pyy"].h, s["pyy"].b + 8 * ft + e - 1)
                                s["vhe"].inc(ve.scalar_tensor_tensor(
                                    heb[:, e % 2, 512 * th:512 * (th + 1)],
                                    sg[:, e % 2, 512 * th:512 * (th + 1)], 1.0,
                                    psU[:, 512 * th:512 * (th + 1)],
                                    op0=ALU.mult, op1=ALU.mult))
                            if e >= 1:
                                acc(e - 1)
                        acc(7)
                blk.vector(a_ve)

        with contextlib.ExitStack() as esb:
            EB = esb.enter_context
            xT2 = EB(nc.sbuf_tensor(pfx + "xT2", [128, DT, TC], F32))
            xtm2 = EB(nc.sbuf_tensor(pfx + "xtm2", [128, 2, D], F32))
            mixsb = EB(nc.sbuf_tensor(pfx + "mixsb", [128, 2, 4, TC], BF16))
            wdsb = EB(nc.sbuf_tensor(pfx + "wdsb", [128, 2, 4, D], BF16))
            ytr = EB(nc.sbuf_tensor(pfx + "ytr", [16, 8, TC], BF16))
            bdt_sb = EB(nc.sbuf_tensor(pfx + "bdt_sb", [16, 2, 8, 128], BF16))
            psA3 = EB(nc.psum_tensor(pfx + "psA3", [128, 512], F32))
            psB3 = EB(nc.psum_tensor(pfx + "psB3", [128, 512], F32))
            py2 = EB(nc.psum_tensor(pfx + "py2", [16, 2, 512], F32))
            psD2 = EB(nc.psum_tensor(pfx + "psD2", [128, 2, TC], F32))

            GATES = (("pmm", 1024), ("pyy", 512), ("slu", 1024), ("vhe", 1024),
                     ("gme", 512), ("vacc", 512), ("yev", 1),
                     ("mxw0", 512), ("mxw1", 512))

            def b_gate_sy(sy):
                for nm, v in GATES:
                    sy.wait_ge(s[nm].h, s[nm].b + v)
            blk.sync(b_gate_sy)

            def b_gate_te(te):
                te.wait_ge(s["slu"].h, s["slu"].b + 1024)
                te.wait_ge(s["vhe"].h, s["vhe"].b + 1024)
                te.wait_ge(s["yev"].h, s["yev"].b + 1)
            blk.tensor(b_gate_te)

            emit_load_x(nc, blk, s, x_in, xT2, xtm2, idf, psA3, psB3)
            se_lx2 = s["se"].n

            def b_sy2(sy):
                for kfg in range(16):
                    sy.wait_ge(s["mxw0"].h, s["mxw0"].b + 32 * (kfg + 1))
                    sy.wait_ge(s["mxw1"].h, s["mxw1"].b + 32 * (kfg + 1))
                    if kfg >= 2:
                        sy.wait_ge(s["pdn"].h, s["pdn"].b + 16 * (kfg - 1))
                    for j in range(4):
                        s["mxl%d" % (kfg % 2)].inc(sy.dma_start(
                            out=mixsb[:, kfg % 2, j, :],
                            in_=mixd[128 * (4 * kfg + j):128 * (4 * kfg + j + 1), :]), 16)
            blk.sync(b_sy2)

            def b_gp(gp):
                gp.wait_ge(s["vhe"].h, s["vhe"].b + 1024)
                gp.wait_ge(s["slu"].h, s["slu"].b + 1024)
                gp.wait_ge(s["yev"].h, s["yev"].b + 1)
                gp.wait_ge(s["mxw0"].h, s["mxw0"].b + 512)
                gp.wait_ge(s["mxw1"].h, s["mxw1"].b + 512)
                gp.wait_ge(s["pyy"].h, s["pyy"].b + 512)
                gp.wait_ge(s["gme"].h, s["gme"].b + 512)
                for kfg in range(16):
                    if kfg >= 2:
                        gp.wait_ge(s["pdn"].h, s["pdn"].b + 16 * (kfg - 1))
                    s["wd%d" % (kfg % 2)].inc(gp.dma_start(
                        out=wdsb[:, kfg % 2, :, :],
                        in_=wdT[512 * kfg:512 * (kfg + 1), :]
                            .rearrange("(kf p) h -> p kf h", p=128)), 16)
                for h in range(DT):
                    if h >= 2:
                        gp.wait_ge(s["pdn"].h, s["pdn"].b + 240 + h - 1)
                    s["bd%d" % (h % 2)].inc(gp.dma_start(
                        out=bdt_sb[:, h % 2, :, :],
                        in_=bdt[:, :, 128 * h:128 * (h + 1)]), 16)
            blk.gpsimd(b_gp)

            def b_w_ve(ve):
                ve.wait_ge(s["yev"].h, s["yev"].b + 1)
                for tt in range(8):
                    for e in range(8):
                        s["ywm"].inc(ve.tensor_scalar(
                            ybuf[:, 128 * tt + 16 * e:128 * tt + 16 * (e + 1)],
                            ybuf[:, 128 * tt + 16 * e:128 * tt + 16 * (e + 1)],
                            wtok[:, tt, e:e + 1], None, op0=ALU.mult))
            blk.vector(b_w_ve)

            def b_te(te):
                te.wait_ge(s["ywm"].h, s["ywm"].b + 64)
                for e in range(8):
                    if e >= 2:
                        te.wait_ge(s["ytv"].h, s["ytv"].b + e - 1)
                    for tt in range(8):
                        inst = te.transpose(
                            py2[:].bitcast(BF16)[:, e % 2, 128 * tt:128 * (tt + 1)],
                            ybuf[:, 128 * tt + 16 * e:128 * tt + 16 * (e + 1)],
                            idb[:])
                    s["pyt"].inc(inst)
                te.wait_ge(s["ytv"].h, s["ytv"].b + 8)
                for kfg in range(16):
                    te.wait_ge(s["wd%d" % (kfg % 2)].h, s["wd%d" % (kfg % 2)].b + 16 * (kfg // 2 + 1))
                    te.wait_ge(s["mxl%d" % (kfg % 2)].h, s["mxl%d" % (kfg % 2)].b + 64 * (kfg // 2 + 1))
                    for h in range(DT):
                        idx = 16 * kfg + h
                        if idx >= 2:
                            te.wait_ge(s["vdn"].h, s["vdn"].b + idx - 1)
                        if kfg == 15:
                            te.wait_ge(s["bd%d" % (h % 2)].h, s["bd%d" % (h % 2)].b + 16 * (h // 2 + 1))
                        for th in range(2):
                            for kf in range(4):
                                inst = te.matmul(
                                    psD2[:, idx % 2, 512 * th:512 * (th + 1)],
                                    wdsb[:, kfg % 2, kf, 128 * h:128 * (h + 1)],
                                    mixsb[:, kfg % 2, kf, 512 * th:512 * (th + 1)],
                                    start=(kf == 0),
                                    stop=(kf == 3 and kfg < 15))
                            if kfg == 15:
                                for e in range(8):
                                    inst = te.matmul(
                                        psD2[:, idx % 2, 512 * th:512 * (th + 1)],
                                        bdt_sb[:, h % 2, e, :],
                                        ytr[:, e, 512 * th:512 * (th + 1)],
                                        start=False, stop=(e == 7))
                        s["pdn"].inc(inst)
            blk.tensor(b_te)

            def b_se(se):
                for e in range(8):
                    se.wait_ge(s["pyt"].h, s["pyt"].b + e + 1)
                    s["ytv"].inc(se.activation(
                        ytr[:, e, :], py2[:].bitcast(BF16)[:, e % 2, :],
                        AF.Identity, bias=0.0, scale=1.0))
            blk.scalar(b_se)

            def b_ve(ve):
                ve.wait_ge(s["se"].h, se_lx2)
                for kfg in range(16):
                    for h in range(DT):
                        idx = 16 * kfg + h
                        ve.wait_ge(s["pdn"].h, s["pdn"].b + idx + 1)
                        s["vdn"].inc(ve.tensor_tensor(
                            xT2[:, h, :], xT2[:, h, :], psD2[:, idx % 2, :],
                            op=ALU.add))
            blk.vector(b_ve)

            with contextlib.ExitStack() as esf:
                EF = esf.enter_context
                xrow = EF(nc.sbuf_tensor(pfx + "xrow", [128, 2, D], F32))
                emit_store_x(nc, blk, s, xT2, x_out, xrow, idf, psA3, psB3,
                             ("vdn", s["vdn"].b + 256))


def build_l4():
    nc = bass.Bass("TRN2", target_bir_lowering=False)
    x_in = nc.dram_tensor("x_in", [TC, D], F32, kind="ExternalInput").ap()
    rwT = nc.dram_tensor("rwT", [D, 8], F32, kind="ExternalInput").ap()
    wgT = nc.dram_tensor("wgT", [D, 8192], F32, kind="ExternalInput").ap()
    wuT = nc.dram_tensor("wuT", [D, 8192], F32, kind="ExternalInput").ap()
    wdT = nc.dram_tensor("wdT", [8192, D], F32, kind="ExternalInput").ap()
    agp = nc.dram_tensor("agp", [2, D, 128], F32, kind="ExternalInput").ap()
    adt = nc.dram_tensor("adt", [8192, 8, 16], F32, kind="ExternalInput").ap()
    bgp = nc.dram_tensor("bgp", [2, 32, 8, 8192], F32, kind="ExternalInput").ap()
    bdt = nc.dram_tensor("bdt", [16, 8, D], F32, kind="ExternalInput").ap()
    x_out = nc.dram_tensor("x_out", [TC, D], F32, kind="ExternalOutput").ap()
    shared = _sems(nc, SHARED_SEMS)
    s = _sem_view(nc, shared, L4_SEMS, "")
    with contextlib.ExitStack() as es:
        E = es.enter_context
        ones_b = E(nc.sbuf_tensor("ones_b", [128, 1], BF16))
        ones_f1 = E(nc.sbuf_tensor("ones_f1", [1, 128], F32))
        idf = E(nc.sbuf_tensor("idf", [128, 128], F32))
        eps_c = E(nc.sbuf_tensor("eps_c", [128, 1], F32))
        blk = E(nc.Block())
        emit_consts(nc, blk, s, ones_b, ones_f1, idf, eps_c)
        _emit_l4(nc, blk, s, ones_b, ones_f1, idf, eps_c,
                 x_in, x_out, rwT, wgT, wuT, wdT, agp, adt, bgp, bdt, "")
    return nc


def _host_l4_inputs(inputs):
    rw = np.asarray(inputs["L1_ffn_router_w"]).astype(np.float32)
    Wg = np.asarray(inputs["L1_ffn_Wg"]).astype(np.float32)
    Wu = np.asarray(inputs["L1_ffn_Wu"]).astype(np.float32)
    Wd = np.asarray(inputs["L1_ffn_Wd"]).astype(np.float32)
    Ag = np.asarray(inputs["L1_ffn_Ag"]).astype(np.float32)
    Au = np.asarray(inputs["L1_ffn_Au"]).astype(np.float32)
    Ad = np.asarray(inputs["L1_ffn_Ad"]).astype(np.float32)
    Bg = np.asarray(inputs["L1_ffn_Bg"]).astype(np.float32)
    Bu = np.asarray(inputs["L1_ffn_Bu"]).astype(np.float32)
    Bd = np.asarray(inputs["L1_ffn_Bd"]).astype(np.float32)
    d = {}
    d["rwT"] = np.ascontiguousarray(rw.T)
    d["wgT"] = np.ascontiguousarray(Wg.T)
    d["wuT"] = np.ascontiguousarray(Wu.T)
    d["wdT"] = np.ascontiguousarray(Wd.T)
    agp = np.stack([Ag.transpose(2, 0, 1).reshape(D, 128),
                    Au.transpose(2, 0, 1).reshape(D, 128)])
    d["agp"] = np.ascontiguousarray(agp)
    d["adt"] = np.ascontiguousarray(Ad.transpose(2, 0, 1))
    def pair(B):
        bt = B.transpose(2, 0, 1)
        lo = bt
        hi = np.zeros_like(bt)
        hi[:, 1:] = bt[:, :7]
        return np.concatenate([lo, hi], axis=0)
    d["bgp"] = np.ascontiguousarray(np.stack([pair(Bg), pair(Bu)]))
    d["bdt"] = np.ascontiguousarray(Bd.transpose(2, 0, 1))
    return d


def run_l4(x2d, inputs):
    nc = build_l4()
    wd = _host_l4_inputs(inputs)
    in_maps = []
    for c in range(NC):
        m = {"x_in": np.ascontiguousarray(x2d[TC * c:TC * (c + 1)])}
        m.update(wd)
        in_maps.append(m)
    res = run_bass_kernel_spmd(nc, in_maps, core_ids=list(range(NC)))
    return np.concatenate([res.results[c]["x_out"] for c in range(NC)], axis=0)



def build_fused():
    nc = bass.Bass("TRN2", target_bir_lowering=False)
    x_in = nc.dram_tensor("x_in", [TC, D], F32, kind="ExternalInput").ap()
    wq = nc.dram_tensor("wq", [D, 256], F32, kind="ExternalInput").ap()
    wk = nc.dram_tensor("wk", [D, 128], F32, kind="ExternalInput").ap()
    wv = nc.dram_tensor("wv", [D, 128], F32, kind="ExternalInput").ap()
    wo = nc.dram_tensor("wo", [D, D], F32, kind="ExternalInput").ap()
    slopes_in = nc.dram_tensor("slopes", [2], F32, kind="ExternalInput").ap()
    bias_in = nc.dram_tensor("abias_in", [2, N_STRIP], F32, kind="ExternalInput").ap()
    gpk = nc.dram_tensor("gpk", [8, 128, 8, 256], U8, kind="ExternalInput").ap()
    upk = nc.dram_tensor("upk", [8, 128, 8, 256], U8, kind="ExternalInput").ap()
    dpkt = nc.dram_tensor("dpkt", [128, 8, 2048], U8, kind="ExternalInput").ap()
    gsc = nc.dram_tensor("gsc", [8, 8, 2048], F32, kind="ExternalInput").ap()
    usc = nc.dram_tensor("usc", [8, 8, 2048], F32, kind="ExternalInput").ap()
    dsc = nc.dram_tensor("dsc", [8, 8, 2048], F32, kind="ExternalInput").ap()
    wq3 = nc.dram_tensor("wq3", [D, D], F32, kind="ExternalInput").ap()
    wk3 = nc.dram_tensor("wk3", [D, D], F32, kind="ExternalInput").ap()
    wv3 = nc.dram_tensor("wv3", [D, D], F32, kind="ExternalInput").ap()
    wg3 = nc.dram_tensor("wg3", [D, D], F32, kind="ExternalInput").ap()
    wo3 = nc.dram_tensor("wo3", [D, D], F32, kind="ExternalInput").ap()
    sel_in = nc.dram_tensor("sel_in", [7], F32, kind="ExternalInput").ap()
    rwT = nc.dram_tensor("rwT", [D, 8], F32, kind="ExternalInput").ap()
    wgT = nc.dram_tensor("wgT", [D, 8192], F32, kind="ExternalInput").ap()
    wuT = nc.dram_tensor("wuT", [D, 8192], F32, kind="ExternalInput").ap()
    wdT = nc.dram_tensor("wdT", [8192, D], F32, kind="ExternalInput").ap()
    agp = nc.dram_tensor("agp", [2, D, 128], F32, kind="ExternalInput").ap()
    adt = nc.dram_tensor("adt", [8192, 8, 16], F32, kind="ExternalInput").ap()
    bgp = nc.dram_tensor("bgp", [2, 32, 8, 8192], F32, kind="ExternalInput").ap()
    bdt = nc.dram_tensor("bdt", [16, 8, D], F32, kind="ExternalInput").ap()
    x_out = nc.dram_tensor("x_out", [TC, D], F32, kind="ExternalOutput").ap()
    xs1 = nc.dram_tensor("xs1", [TC, D], F32, kind="Internal").ap()
    xs2 = nc.dram_tensor("xs2", [TC, D], F32, kind="Internal").ap()
    xs3 = nc.dram_tensor("xs3", [TC, D], F32, kind="Internal").ap()

    shared = _sems(nc, SHARED_SEMS)
    pool = SemPool(nc)

    with contextlib.ExitStack() as es:
        E = es.enter_context
        ones_b = E(nc.sbuf_tensor("ones_b", [128, 1], BF16))
        ones_f1 = E(nc.sbuf_tensor("ones_f1", [1, 128], F32))
        idf = E(nc.sbuf_tensor("idf", [128, 128], F32))
        eps_c = E(nc.sbuf_tensor("eps_c", [128, 1], F32))
        blk = E(nc.Block())
        emit_consts(nc, blk, shared, ones_b, ones_f1, idf, eps_c)
        _emit_l1(nc, blk, pool.view(shared, L1_SEMS), ones_b, ones_f1, idf, eps_c,
                 x_in, xs1, wq, wk, wv, wo, slopes_in, bias_in, "b1_")
        _gate_all(blk, shared)
        _emit_l2(nc, blk, pool.view(shared, L2_SEMS), ones_b, ones_f1, idf, eps_c,
                 xs1, xs2, gpk, upk, dpkt, gsc, usc, dsc, "b2_")
        _gate_all(blk, shared)
        _emit_l3(nc, blk, pool.view(shared, L3_SEMS), ones_b, ones_f1, idf, eps_c,
                 xs2, xs3, wq3, wk3, wv3, wg3, wo3, sel_in, "b3_")
        _gate_all(blk, shared)
        _emit_l4(nc, blk, pool.view(shared, L4_SEMS), ones_b, ones_f1, idf, eps_c,
                 xs3, x_out, rwT, wgT, wuT, wdT, agp, adt, bgp, bdt, "b4_")
    return nc


def run_fused(inputs):
    nc = build_fused()
    x2d = np.ascontiguousarray(np.asarray(inputs["L0_attn_x"]).reshape(T, D)) \
        .astype(np.float32, copy=False)
    Wq = np.asarray(inputs["L0_attn_Wq"])
    Wk = np.asarray(inputs["L0_attn_Wk"])
    Wv = np.asarray(inputs["L0_attn_Wv"])
    Wo = np.asarray(inputs["L0_attn_Wo"])
    l2h = _host_l2_inputs(inputs)
    l4h = _host_l4_inputs(inputs)
    in_maps = []
    for c in range(NC):
        sel = np.zeros(7, np.float32)
        if c >= 1:
            sel[c - 1] = 1.0
        m = {
            "x_in": np.ascontiguousarray(x2d[TC * c:TC * (c + 1)]),
            "wq": np.ascontiguousarray(Wq[:, 256 * c:256 * (c + 1)]),
            "wk": np.ascontiguousarray(Wk[:, 128 * (c // 2):128 * (c // 2 + 1)]),
            "wv": np.ascontiguousarray(Wv[:, 128 * (c // 2):128 * (c // 2 + 1)]),
            "wo": np.ascontiguousarray(Wo),
            "slopes": np.array([SLOPES[2 * c], SLOPES[2 * c + 1]], np.float32),
            "abias_in": _attn_bias_host(c),
            "wq3": np.asarray(inputs["L1_attn_Wq"]),
            "wk3": np.asarray(inputs["L1_attn_Wk"]),
            "wv3": np.asarray(inputs["L1_attn_Wv"]),
            "wg3": np.asarray(inputs["L1_attn_Wg"]),
            "wo3": np.asarray(inputs["L1_attn_Wo"]),
            "sel_in": sel,
        }
        m.update(_host_l2_core(l2h, c))
        m.update(l4h)
        in_maps.append(m)
    res = run_bass_kernel_spmd(nc, in_maps, core_ids=list(range(NC)))
    return np.concatenate([res.results[c]["x_out"] for c in range(NC)], axis=0)



def _attn_bias_host(c):
    out = np.zeros((2, N_STRIP), np.float32)
    for hh in range(2):
        sl = SLOPES[2 * c + hh]
        for b in range(NBAND):
            for sc in range(4 * (b + 1)):
                out[hh, _boff(b) + sc] = sl * 128.0 * (sc - 4 * b) - SHIFT
    return out


def run_l1(x2d, Wq, Wk, Wv, Wo):
    nc = build_l1()
    in_maps = []
    for c in range(NC):
        in_maps.append({
            "x_in": np.ascontiguousarray(x2d[TC * c:TC * (c + 1)]),
            "wq": np.ascontiguousarray(Wq[:, 256 * c:256 * (c + 1)]),
            "wk": np.ascontiguousarray(Wk[:, 128 * (c // 2):128 * (c // 2 + 1)]),
            "wv": np.ascontiguousarray(Wv[:, 128 * (c // 2):128 * (c // 2 + 1)]),
            "wo": np.ascontiguousarray(Wo),
            "slopes": np.array([SLOPES[2 * c], SLOPES[2 * c + 1]], np.float32),
            "abias_in": _attn_bias_host(c),
        })
    res = run_bass_kernel_spmd(nc, in_maps, core_ids=list(range(NC)))
    return np.concatenate([res.results[c]["x_out"] for c in range(NC)], axis=0)


def kernel(**inputs):
    return run_fused(inputs).reshape(1, T, D)


def kernel_unfused(**inputs):
    x2d = np.ascontiguousarray(inputs["L0_attn_x"].reshape(T, D)).astype(np.float32, copy=False)
    x1 = run_l1(x2d, np.asarray(inputs["L0_attn_Wq"]), np.asarray(inputs["L0_attn_Wk"]),
                np.asarray(inputs["L0_attn_Wv"]), np.asarray(inputs["L0_attn_Wo"]))
    x2 = run_l2(x1, inputs)
    x3 = run_l3(x2, inputs)
    x4 = run_l4(x3, inputs)
    return x4.reshape(1, T, D)


# version 11 (final)
# speedup vs baseline: 1.0591x; 1.0591x over previous
import contextlib
import numpy as np
import ml_dtypes
import concourse.bass as bass
import concourse.mybir as mybir
from concourse.bass_utils import run_bass_kernel_spmd
from concourse.aot_env import allow_ici_all_to_all

F32 = mybir.dt.float32
BF16 = mybir.dt.bfloat16
I32 = mybir.dt.int32
U8 = mybir.dt.uint8
AF = mybir.ActivationFunctionType
ALU = mybir.AluOpType
AX = mybir.AxisListType
BF = ml_dtypes.bfloat16

NC = 8
T = 8192
D = 2048
TC = T // NC
DT = D // 128
NBAND = 16
RG = [list(range(NC))]
ISQ = float(1.0 / np.sqrt(128.0))
SHIFT = 24.0
EPS = 1e-5

_S16 = [2.0 ** (-0.5 * (i + 1)) for i in range(16)]
SLOPES = [_S16[h // 4] for h in range(16)]


def _boff(b):
    return 2 * b * (b + 1)


N_STRIP = _boff(NBAND)


class Sem:
    def __init__(self, nc, name):
        self.h = nc.alloc_semaphore(name)
        self.n = 0
        self.b = 0

    def inc(self, inst, k=1):
        inst.then_inc(self.h, k)
        self.n += k
        return inst


def _sems(nc, names):
    return {k: Sem(nc, k) for k in names.split()}


def _r(ap):
    return ap.rearrange("p a b -> p (a b)")



def emit_consts(nc, blk, s, ones_b, ones_f1, idf, eps_c):
    def c_ve(ve):
        ve.memset(ones_b[:], 1.0)
        ve.memset(ones_f1[:], 1.0)
        ve.memset(idf[:], 1.0)
        ve.memset(eps_c[:], EPS)
        s["ve"].inc(ve.engine_nop())
    blk.vector(c_ve)
    ve0 = s["ve"].n

    def c_gp(gp):
        gp.wait_ge(s["ve"].h, ve0)
        s["cst"].inc(gp.affine_select(idf[:], idf[:], pattern=[[-1, 128]],
                                      compare_op=ALU.is_equal, fill=0.0, base=0,
                                      channel_multiplier=1))
    blk.gpsimd(c_gp)


def emit_load_x(nc, blk, s, x_in, xT, xtm, idf, psA, psB):
    se0, pe0 = s["se"].n, s["pe"].n
    ld0 = [s["ldx0"].n, s["ldx1"].n]
    cst0 = s["cst"].n

    def p_sy(sy):
        for tt in range(8):
            if tt >= 2:
                sy.wait_ge(s["se"].h, se0 + 4 * (tt - 1))
            s["ldx%d" % (tt % 2)].inc(sy.dma_start(out=xtm[:, tt % 2, :],
                                      in_=x_in[128 * tt:128 * (tt + 1), :]), 16)
    blk.sync(p_sy)

    def p_te(te):
        te.wait_ge(s["cst"].h, cst0)
        for tt in range(8):
            te.wait_ge(s["ldx%d" % (tt % 2)].h, ld0[tt % 2] + 16 * (tt // 2 + 1))
            for g in range(4):
                i = tt * 4 + g
                ps = psA if i % 2 == 0 else psB
                if i >= 2:
                    te.wait_ge(s["se"].h, se0 + i - 1)
                for j in range(4):
                    dt = g * 4 + j
                    inst = te.transpose(ps[:, 128 * j:128 * (j + 1)],
                                        xtm[:, tt % 2, 128 * dt:128 * (dt + 1)], idf[:])
                s["pe"].inc(inst)
    blk.tensor(p_te)

    def p_se(se):
        for tt in range(8):
            for g in range(4):
                i = tt * 4 + g
                ps = psA if i % 2 == 0 else psB
                se.wait_ge(s["pe"].h, pe0 + i + 1)
                s["se"].inc(se.copy(xT[:, 4 * g:4 * g + 4, 128 * tt:128 * (tt + 1)], ps[:]))
    blk.scalar(p_se)
    return s["se"].n


def emit_ln(nc, blk, s, xT, xlnT, xb, srow, tmpf, ones_b, ones_f1, eps_c,
            psA, psB, psC, psD, psE, psF, ready):
    rs, rv = ready
    ve0, pe0, se0 = s["ve"].n, s["pe"].n, s["se"].n

    def l_ve(ve):
        ve.wait_ge(s[rs].h, rv)
        s["ve"].inc(ve.tensor_copy(_r(xb[:]), _r(xT[:])))
    blk.vector(l_ve)

    def l_te(te):
        te.wait_ge(s["ve"].h, ve0 + 1)
        for th, ps in ((0, psC), (1, psE)):
            for dt in range(DT):
                inst = te.matmul(ps[0:1, :], ones_b[:], xb[:, dt, 512 * th:512 * (th + 1)],
                                 start=(dt == 0), stop=(dt == DT - 1))
        s["pe"].inc(inst)
        te.wait_ge(s["se"].h, se0 + 1)
        for th, ps in ((0, psD), (1, psF)):
            for dt in range(DT):
                inst = te.matmul(ps[0:1, :], ones_b[:], xb[:, dt, 512 * th:512 * (th + 1)],
                                 start=(dt == 0), stop=(dt == DT - 1))
        s["pe"].inc(inst)
    blk.tensor(l_te)

    def l_se(se):
        se.wait_ge(s["pe"].h, pe0 + 1)
        s["se"].inc(se.activation(_r(xb[:]), _r(xb[:]), AF.Square, bias=0.0, scale=1.0))
    blk.scalar(l_se)

    def l_ve2(ve):
        ve.wait_ge(s["pe"].h, pe0 + 2)
        for th, psS, psQ in ((0, psC, psD), (1, psE, psF)):
            s["chv"].inc(ve.tensor_scalar(srow[:, 0, 512 * th:512 * (th + 1)], psS[0:1, :],
                                          1.0 / D, None, op0=ALU.mult))
            s["chv"].inc(ve.tensor_scalar(srow[:, 1, 512 * th:512 * (th + 1)], psQ[0:1, :],
                                          1.0 / D, None, op0=ALU.mult))
        ve.wait_ge(s["chv"].h, s["chv"].n)
        s["chv"].inc(ve.tensor_tensor(srow[:, 2, :], srow[:, 0, :], srow[:, 0, :], op=ALU.mult))
        ve.wait_ge(s["chv"].h, s["chv"].n)
        s["ve"].inc(ve.tensor_tensor(srow[:, 1, :], srow[:, 1, :], srow[:, 2, :], op=ALU.subtract))
    blk.vector(l_ve2)

    def l_se2(se):
        se.wait_ge(s["ve"].h, ve0 + 2)
        s["se"].inc(se.activation(srow[:, 1, :], srow[:, 1, :], AF.Sqrt,
                                  bias=eps_c[0:1, :], scale=1.0))
    blk.scalar(l_se2)

    def l_ve3(ve):
        ve.wait_ge(s["se"].h, se0 + 2)
        s["ve"].inc(ve.reciprocal(srow[:, 2, :], srow[:, 1, :]))
    blk.vector(l_ve3)

    ve_ap0 = s["ve"].n
    pe_b = s["pe"].n

    def l_te2(te):
        te.wait_ge(s["ve"].h, ve_ap0)
        for th in range(2):
            if th == 1:
                te.wait_ge(s["ve"].h, ve_ap0 + 1)
            te.matmul(psA[:], ones_f1[:], srow[:, 0, 512 * th:512 * (th + 1)], start=True, stop=True)
            s["pe"].inc(te.matmul(psB[:], ones_f1[:], srow[:, 2, 512 * th:512 * (th + 1)],
                                  start=True, stop=True))
    blk.tensor(l_te2)

    def l_ve4(ve):
        for th in range(2):
            ve.wait_ge(s["pe"].h, pe_b + th + 1)
            for dt in range(DT):
                s["chv"].inc(ve.tensor_tensor(tmpf[:, dt % 2, :],
                                              xT[:, dt, 512 * th:512 * (th + 1)], psA[:],
                                              op=ALU.subtract))
                ve.wait_ge(s["chv"].h, s["chv"].n)
                inst = ve.tensor_tensor(xlnT[:, dt, 512 * th:512 * (th + 1)],
                                        tmpf[:, dt % 2, :], psB[:], op=ALU.mult)
            s["ve"].inc(inst)
    blk.vector(l_ve4)
    return s["ve"].n


def emit_store_x(nc, blk, s, xT, x_out, xrow, idf, psA, psB, ready):
    rs, rv = ready
    pe0, se0 = s["pe"].n, s["se"].n
    st0 = [s["st0"].n, s["st1"].n]

    def t_te(te):
        te.wait_ge(s[rs].h, rv)
        for tt in range(8):
            for g in range(4):
                i = tt * 4 + g
                ps = psA if i % 2 == 0 else psB
                if i >= 2:
                    te.wait_ge(s["se"].h, se0 + i - 1)
                for j in range(4):
                    dt = g * 4 + j
                    inst = te.transpose(ps[:, 128 * j:128 * (j + 1)],
                                        xT[:, dt, 128 * tt:128 * (tt + 1)], idf[:])
                s["pe"].inc(inst)
    blk.tensor(t_te)

    def t_se(se):
        for tt in range(8):
            if tt >= 2:
                se.wait_ge(s["st%d" % (tt % 2)].h, st0[tt % 2] + 16 * (tt // 2))
            for g in range(4):
                i = tt * 4 + g
                ps = psA if i % 2 == 0 else psB
                se.wait_ge(s["pe"].h, pe0 + i + 1)
                s["se"].inc(se.copy(xrow[:, tt % 2, 512 * g:512 * (g + 1)], ps[:]))
    blk.scalar(t_se)

    def t_sy(sy):
        for tt in range(8):
            sy.wait_ge(s["se"].h, se0 + 4 * (tt + 1))
            s["st%d" % (tt % 2)].inc(sy.dma_start(out=x_out[128 * tt:128 * (tt + 1), :],
                                     in_=xrow[:, tt % 2, :]), 16)
        sy.wait_ge(s["st0"].h, s["st0"].n)
        sy.wait_ge(s["st1"].h, s["st1"].n)
    blk.sync(t_sy)
    return ("st0", s["st0"].n)



SHARED_SEMS = "cst pe se ve gp cc chv chg ldx0 ldx1 st0 st1"
L1_SEMS = ("xg0 xg1 rsw0 rsw1 cld agxw wld a2aw wold otld bcl "
           "pex sev stt expp msk peo vep vbd vno")
L2_SEMS = ("sga chs pk0 pk1 rp0 rp1 dqx plc pw0 pw1 pgl0 pgl1 pdl0 pdl1 "
           "peg seg vgl actt gin ped vda")
L3_SEMS = ("chs wld pjmm pjev actl vsg vcs ecs vkc pea pet vmsk peo pes "
           "sekh ssb soev vst qwb agw dal dsl "
           "sjl0 sjl1 djl rel epx cl0 cl1 pco vcp vca wold pwo vwo")
L4_SEMS = ("ywm chs rwl agl rpe rse rve rpw wev wwr wbc zpe zev zms zsh0 zsh1 "
           "wg0 wg1 pmm pyy slu vhe gme vacc mxw0 mxw1 "
           "yev pyt ytv mxl0 mxl1 wd0 wd1 bd0 bd1 pdn vdn")


class SemPool:

    def __init__(self, nc):
        self.nc = nc
        self.sems = []

    def view(self, shared, names):
        v = dict(shared)
        needed = [k for k in names.split() if k not in v]
        while len(self.sems) < len(needed):
            self.sems.append(Sem(self.nc, "pool%d" % len(self.sems)))
        for k, sem in zip(needed, self.sems):
            sem.b = sem.n
            v[k] = sem
        return v


def _sem_view(nc, shared, names, pfx):
    v = dict(shared)
    for k in names.split():
        if k not in v:
            v[k] = Sem(nc, pfx + k)
    return v


def _gate_all(blk, s):
    t0, t1 = s["st0"].n, s["st1"].n

    def g(eng):
        eng.wait_ge(s["st0"].h, t0)
        eng.wait_ge(s["st1"].h, t1)
    blk.sync(g)
    blk.tensor(g)
    blk.scalar(g)
    blk.vector(g)
    blk.gpsimd(g)


def _emit_l1(nc, blk, s, ones_b, ones_f1, idf, eps_c,
             x_in, x_out, wq, wk, wv, wo, slopes_in, bias_in, pfx):
    agx_src = nc.dram_tensor(pfx + "agx_src", [D, TC], BF16, kind="Internal").ap()
    agx_dst = nc.dram_tensor(pfx + "agx_dst", [NC * D, TC], BF16, kind="Internal").ap()
    a2a_src = nc.dram_tensor(pfx + "a2a_src", [D, TC], BF16, kind="Internal").ap()
    a2a_dst = nc.dram_tensor(pfx + "a2a_dst", [D, TC], BF16, kind="Internal").ap()
    rs_scr = nc.dram_tensor(pfx + "rs_scr", [32, 512], BF16, kind="Internal").ap()

    with contextlib.ExitStack() as es:
        E = es.enter_context
        xT = E(nc.sbuf_tensor(pfx + "xT", [128, DT, TC], F32))
        psA = E(nc.psum_tensor(pfx + "psA", [128, 512], F32))
        psB = E(nc.psum_tensor(pfx + "psB", [128, 512], F32))
        psC = E(nc.psum_tensor(pfx + "psC", [128, 512], F32))
        psD = E(nc.psum_tensor(pfx + "psD", [128, 512], F32))
        psE = E(nc.psum_tensor(pfx + "psE", [128, 512], F32))
        psF = E(nc.psum_tensor(pfx + "psF", [128, 512], F32))

        Rt = E(nc.sbuf_tensor(pfx + "Rt", [128, 5, 512], F32))
        slop = E(nc.sbuf_tensor(pfx + "slop", [128, 2], F32))
        abias = E(nc.sbuf_tensor(pfx + "abias", [128, 2, N_STRIP], F32))
        if True:
            def c2_gp(gp):
                s["gp"].inc(gp.iota(Rt[:, 0, :], pattern=[[-1, 512]], base=0,
                                    channel_multiplier=1,
                                    allow_small_or_imprecise_dtypes=True))
                gp.wait_ge(s["gp"].h, s["gp"].n)
                for m in range(4):
                    inst = gp.affine_select(Rt[:, 1 + m, :], Rt[:, 0, :], pattern=[[1, 512]],
                                            compare_op=ALU.is_ge, fill=-1e30, base=-128 * m,
                                            channel_multiplier=-1)
                s["cst"].inc(inst)
                s["cld"].inc(gp.dma_start(out=slop[:],
                                          in_=slopes_in.unsqueeze(0).broadcast_to([128, 2])), 16)
                s["cld"].inc(gp.dma_start(out=abias[:],
                                          in_=bias_in.unsqueeze(0).broadcast_to([128, 2, N_STRIP])), 16)
            blk.gpsimd(c2_gp)
            cst_flash = s["cst"].n
            cld_flash = s["cld"].n

            with contextlib.ExitStack() as es0:
                E0 = es0.enter_context
                xtm = E0(nc.sbuf_tensor(pfx + "xtm", [128, 2, D], F32))
                xb = E0(nc.sbuf_tensor(pfx + "xb", [128, DT, TC], BF16))
                srow = E0(nc.sbuf_tensor(pfx + "srow", [1, 3, TC], F32))
                xlnT = E0(nc.sbuf_tensor(pfx + "xlnT", [128, DT, TC], BF16))
                tmpf = E0(nc.sbuf_tensor(pfx + "tmpf", [128, 2, 512], F32))
                emit_load_x(nc, blk, s, x_in, xT, xtm, idf, psA, psB)
                ve_ln = emit_ln(nc, blk, s, xT, xlnT, xb, srow, tmpf, ones_b, ones_f1, eps_c,
                                psA, psB, psC, psD, psE, psF, ("se", s["se"].n))

                def pub_gp(gp):
                    gp.wait_ge(s["ve"].h, ve_ln)
                    s["agxw"].inc(gp.dma_start(out=agx_src.rearrange("(dt p) t -> p dt t", p=128),
                                               in_=xlnT[:]), 16)
                    gp.wait_ge(s["agxw"].h, s["agxw"].n)
                    s["cc"].inc(gp.collective_compute(
                        "AllGather", ALU.bypass, replica_groups=RG,
                        ins=[agx_src.opt()], outs=[agx_dst.opt()]))
                blk.gpsimd(pub_gp)
                cc_agx = s["cc"].n

            with contextlib.ExitStack() as es1:
                E1 = es1.enter_context
                wq_sb = E1(nc.sbuf_tensor(pfx + "wq_sb", [128, DT, 256], BF16))
                wk_sb = E1(nc.sbuf_tensor(pfx + "wk_sb", [128, DT, 128], BF16))
                wv_sb = E1(nc.sbuf_tensor(pfx + "wv_sb", [128, DT, 128], BF16))
                qT = E1(nc.sbuf_tensor(pfx + "qT", [128, 2, T], BF16))
                kT = E1(nc.sbuf_tensor(pfx + "kT", [128, 64, 128], BF16))
                Vt = E1(nc.sbuf_tensor(pfx + "Vt", [128, 64, 128], BF16))
                xg = E1(nc.sbuf_tensor(pfx + "xg", [128, 2, 512], BF16))
                Ofull = E1(nc.sbuf_tensor(pfx + "Ofull", [128, 2, T], BF16))
                tmpE = E1(nc.sbuf_tensor(pfx + "tmpE", [128, 3, 512], F32))
                Ebuf = E1(nc.sbuf_tensor(pfx + "Ebuf", [128, 3, 512], BF16))
                rssm = E1(nc.sbuf_tensor(pfx + "rssm", [1, 2, 512], BF16))
                bc_g = E1(nc.sbuf_tensor(pfx + "bc_g", [128, 2, 512], BF16))
                def w_gp(gp):
                    s["wld"].inc(gp.dma_start(out=wq_sb[:],
                                              in_=wq.rearrange("(kt p) m -> p kt m", p=128)), 16)
                    s["wld"].inc(gp.dma_start(out=wk_sb[:],
                                              in_=wk.rearrange("(kt p) m -> p kt m", p=128)), 16)
                    s["wld"].inc(gp.dma_start(out=wv_sb[:],
                                              in_=wv.rearrange("(kt p) m -> p kt m", p=128)), 16)
                blk.gpsimd(w_gp)
                ldg_w = s["wld"].n

                xg0 = [s["xg0"].n, s["xg1"].n]
                pex0 = s["pex"].n
                sev0 = s["sev"].n

                def pj_sy(sy):
                    sy.wait_ge(s["cc"].h, cc_agx)
                    for ch in range(16):
                        for kt in range(DT):
                            L = ch * DT + kt
                            if L >= 2:
                                sy.wait_ge(s["pex"].h, pex0 + L - 1)
                            blkr = ch // 2
                            co = (ch % 2) * 512
                            s["xg%d" % (L % 2)].inc(sy.dma_start(
                                out=xg[:, L % 2, :],
                                in_=agx_dst[2048 * blkr + 128 * kt: 2048 * blkr + 128 * (kt + 1),
                                            co:co + 512]), 16)
                blk.sync(pj_sy)

                ve_pr = s["ve"].n
                def pj_te(te):
                    te.wait_ge(s["wld"].h, ldg_w)
                    te.wait_ge(s["ve"].h, ve_pr)
                    for ch in range(16):
                        if ch >= 1:
                            te.wait_ge(s["sev"].h, sev0 + 4 * ch)
                        for kt in range(DT):
                            L = ch * DT + kt
                            te.wait_ge(s["xg%d" % (L % 2)].h, xg0[L % 2] + 16 * (L // 2 + 1))
                            st = (kt == 0)
                            sp = (kt == DT - 1)
                            te.matmul(psA[:], wq_sb[:, kt, 0:128], xg[:, L % 2, :], start=st, stop=sp)
                            te.matmul(psB[:], wq_sb[:, kt, 128:256], xg[:, L % 2, :], start=st, stop=sp)
                            te.matmul(psC[:], wk_sb[:, kt, :], xg[:, L % 2, :], start=st, stop=sp)
                            for j in range(4):
                                inst = te.matmul(psD[:, 128 * j:128 * (j + 1)],
                                                 xg[:, L % 2, 128 * j:128 * (j + 1)],
                                                 wv_sb[:, kt, :],
                                                 start=(st and j == 0), stop=(sp and j == 3))
                            s["pex"].inc(inst)
                blk.tensor(pj_te)

                def pj_se(se):
                    for ch in range(16):
                        se.wait_ge(s["pex"].h, pex0 + DT * (ch + 1))
                        s["sev"].inc(se.activation(qT[:, 0, 512 * ch:512 * (ch + 1)],
                                                   psA[:], AF.Identity, bias=0.0, scale=ISQ))
                        s["sev"].inc(se.activation(qT[:, 1, 512 * ch:512 * (ch + 1)],
                                                   psB[:], AF.Identity, bias=0.0, scale=ISQ))
                        s["sev"].inc(se.copy(kT[:, 4 * ch:4 * ch + 4, :], psC[:]))
                        s["sev"].inc(se.copy(Vt[:, 4 * ch:4 * ch + 4, :], psD[:]))
                blk.scalar(pj_se)
                sev_proj = s["sev"].n

                strips = []
                for b in range(NBAND):
                    for hh in range(2):
                        for sc in range(4 * (b + 1)):
                            strips.append((b, hh, sc))
                G = len(strips)
                nmask_before = []
                nm = 0
                for (b, hh, sc) in strips:
                    nmask_before.append(nm)
                    if sc >= 4 * b:
                        nm += 1
                NM = nm
                kidx = []
                k = -1
                for gi, (b, hh, sc) in enumerate(strips):
                    if sc == 0:
                        k += 1
                    kidx.append(k)
                NK = k + 1

                stt0, exp0, msk0, peo0 = s["stt"].n, s["expp"].n, s["msk"].n, s["peo"].n
                pes0 = s["pex"].n
                vep0, vbd0, vno0 = s["vep"].n, s["vbd"].n, s["vno"].n
                rsw0 = [s["rsw0"].n, s["rsw1"].n]
                bcl0 = s["bcl"].n

                def fl_te(te):
                    te.wait_ge(s["sev"].h, sev_proj)
                    te.wait_ge(s["cst"].h, cst_flash)

                    def rso(gi):
                        b, hh, sc = strips[gi]
                        kk = kidx[gi]
                        te.wait_ge(s["expp"].h, exp0 + gi + 1)
                        if sc == 0:
                            te.wait_ge(s["vbd"].h, vbd0 + kk)
                        st = (sc == 0)
                        sp = (sc == 4 * (b + 1) - 1)
                        E = Ebuf[:, gi % 3, :]
                        te.matmul(psD[0:1, :], ones_b[:], E, start=st, stop=sp)
                        s["peo"].inc(te.matmul(psC[:], Vt[:, sc, :], E, start=st, stop=sp))

                    SPS = (psA, psB, psE)
                    for gi, (b, hh, sc) in enumerate(strips):
                        if gi >= 3:
                            te.wait_ge(s["stt"].h, stt0 + gi - 2)
                        s["pex"].inc(te.matmul(SPS[gi % 3][:],
                                     kT[:, sc, :], qT[:, hh, 512 * b:512 * (b + 1)],
                                     start=True, stop=True))
                        if gi >= 1:
                            rso(gi - 1)
                    rso(G - 1)
                blk.tensor(fl_te)

                def fl_ve(ve):
                    ve.wait_ge(s["cld"].h, cld_flash)
                    ve.wait_ge(s["cst"].h, cst_flash)
                    for gi, (b, hh, sc) in enumerate(strips):
                        kk = kidx[gi]
                        ve.wait_ge(s["pex"].h, pes0 + gi + 1)
                        if gi >= 3:
                            ve.wait_ge(s["expp"].h, exp0 + gi - 2)
                        vidx = 0 if sc < 4 * b else (sc - 4 * b + 1)
                        s["stt"].inc(ve.scalar_tensor_tensor(
                            tmpE[:, gi % 3, :], Rt[:, vidx, :], slop[:, hh:hh + 1],
                            (psA, psB, psE)[gi % 3][:], op0=ALU.mult, op1=ALU.add))
                        if sc == 4 * (b + 1) - 1:
                            ve.wait_ge(s["peo"].h, peo0 + gi + 1)
                            if kk >= 2:
                                ve.wait_ge(s["rsw%d" % (kk % 2)].h,
                                           rsw0[kk % 2] + 16 * (kk // 2))
                            with nc.allow_low_precision(reason="bf16 rowsum recip"):
                                s["vep"].inc(ve.reciprocal(rssm[:, kk % 2, :], psD[0:1, :]))
                            s["vbd"].inc(ve.tensor_copy(
                                Ofull[:, hh, 512 * b:512 * (b + 1)], psC[:]))
                blk.vector(fl_ve)

                def fl_se(se):
                    for gi, (b, hh, sc) in enumerate(strips):
                        se.wait_ge(s["stt"].h, stt0 + gi + 1)
                        if gi >= 3:
                            se.wait_ge(s["peo"].h, peo0 + gi - 2)
                        i = _boff(b) + sc
                        s["expp"].inc(se.activation(Ebuf[:, gi % 3, :], tmpE[:, gi % 3, :],
                                                    AF.Exp, bias=abias[:, hh, i:i + 1], scale=1.0))
                blk.scalar(fl_se)

                bh_of = []
                for b in range(NBAND):
                    for hh in range(2):
                        bh_of.append((hh, b))

                def fl_gp(gp):
                    for kk in range(NK):
                        gp.wait_ge(s["vep"].h, vep0 + kk + 1)
                        s["rsw%d" % (kk % 2)].inc(gp.dma_start(out=rs_scr[kk:kk + 1, :],
                                                  in_=rssm[:, kk % 2, :]), 16)
                    for g in range(16):
                        if g >= 1:
                            gp.wait_ge(s["vno"].h, vno0 + 2 * g)
                        gp.wait_ge(s["rsw0"].h, rsw0[0] + 16 * (g + 1))
                        gp.wait_ge(s["rsw1"].h, rsw0[1] + 16 * (g + 1))
                        s["bcl"].inc(gp.dma_start(
                            out=bc_g[:],
                            in_=rs_scr[2 * g:2 * (g + 1), :].unsqueeze(0)
                                .broadcast_to([128, 2, 512])), 16)
                blk.gpsimd(fl_gp)

                def fl_ve2(ve):
                    ve.wait_ge(s["vbd"].h, vbd0 + NK)
                    for kk in range(NK):
                        hh, b = bh_of[kk]
                        g = kk // 2
                        ve.wait_ge(s["bcl"].h, bcl0 + 16 * (g + 1))
                        s["vno"].inc(ve.tensor_tensor(
                            Ofull[:, hh, 512 * b:512 * (b + 1)],
                            Ofull[:, hh, 512 * b:512 * (b + 1)],
                            bc_g[:, kk % 2, :], op=ALU.mult))
                blk.vector(fl_ve2)

                def a2a_gp(gp):
                    gp.wait_ge(s["vno"].h, vno0 + NK)
                    for j in range(NC):
                        s["a2aw"].inc(gp.dma_start(
                            out=a2a_src[256 * j:256 * (j + 1), :].rearrange("(h p) t -> p h t", p=128),
                            in_=Ofull[:, :, TC * j:TC * (j + 1)]), 16)
                    gp.wait_ge(s["a2aw"].h, s["a2aw"].n)
                    with allow_ici_all_to_all():
                        s["cc"].inc(gp.collective_compute(
                            "AllToAll", ALU.bypass, replica_groups=RG,
                            ins=[a2a_src.opt()], outs=[a2a_dst.opt()]))
                blk.gpsimd(a2a_gp)
                cc_a2a = s["cc"].n

            with contextlib.ExitStack() as es2:
                E2 = es2.enter_context
                wo_sb = E2(nc.sbuf_tensor(pfx + "wo_sb", [128, DT, D], BF16))
                oT = E2(nc.sbuf_tensor(pfx + "oT", [128, DT, TC], BF16))
                def wo_gp(gp):
                    for q in range(4):
                        s["wold"].inc(gp.dma_start(
                            out=wo_sb[:, 4 * q:4 * (q + 1), :],
                            in_=wo.rearrange("(kt p) m -> p kt m", p=128)[:, 4 * q:4 * (q + 1), :]), 16)
                blk.gpsimd(wo_gp)
                ldg_wo = s["wold"].n

                def wo_sy(sy):
                    sy.wait_ge(s["cc"].h, cc_a2a)
                    s["otld"].inc(sy.dma_start(out=oT[:],
                                              in_=a2a_dst.rearrange("(dt p) t -> p dt t", p=128)), 16)
                blk.sync(wo_sy)
                ldq_o = s["otld"].n

                pe_wo0 = s["pex"].n
                ve_wo0 = s["ve"].n

                def wo_te(te):
                    te.wait_ge(s["wold"].h, ldg_wo)
                    te.wait_ge(s["otld"].h, ldq_o)
                    for ho in range(DT):
                        for th in range(2):
                            i = ho * 2 + th
                            ps = psA if i % 2 == 0 else psB
                            if i >= 2:
                                te.wait_ge(s["ve"].h, ve_wo0 + i - 1)
                            for kt in range(DT):
                                inst = te.matmul(ps[:], wo_sb[:, kt, 128 * ho:128 * (ho + 1)],
                                                 oT[:, kt, 512 * th:512 * (th + 1)],
                                                 start=(kt == 0), stop=(kt == DT - 1))
                            s["pex"].inc(inst)
                blk.tensor(wo_te)

                def wo_ve(ve):
                    for ho in range(DT):
                        for th in range(2):
                            i = ho * 2 + th
                            ps = psA if i % 2 == 0 else psB
                            ve.wait_ge(s["pex"].h, pe_wo0 + i + 1)
                            s["ve"].inc(ve.tensor_tensor(
                                xT[:, ho, 512 * th:512 * (th + 1)],
                                xT[:, ho, 512 * th:512 * (th + 1)], ps[:], op=ALU.add))
                blk.vector(wo_ve)
                ve_res = s["ve"].n

            with nc.sbuf_tensor(pfx + "xrow", [128, 2, D], F32) as xrow:
                emit_store_x(nc, blk, s, xT, x_out, xrow, idf, psA, psB, ("ve", ve_res))


def build_l1():
    nc = bass.Bass("TRN2", target_bir_lowering=False)
    x_in = nc.dram_tensor("x_in", [TC, D], F32, kind="ExternalInput").ap()
    wq = nc.dram_tensor("wq", [D, 256], F32, kind="ExternalInput").ap()
    wk = nc.dram_tensor("wk", [D, 128], F32, kind="ExternalInput").ap()
    wv = nc.dram_tensor("wv", [D, 128], F32, kind="ExternalInput").ap()
    wo = nc.dram_tensor("wo", [D, D], F32, kind="ExternalInput").ap()
    slopes_in = nc.dram_tensor("slopes", [2], F32, kind="ExternalInput").ap()
    bias_in = nc.dram_tensor("abias_in", [2, N_STRIP], F32, kind="ExternalInput").ap()
    x_out = nc.dram_tensor("x_out", [TC, D], F32, kind="ExternalOutput").ap()
    shared = _sems(nc, SHARED_SEMS)
    s = _sem_view(nc, shared, L1_SEMS, "")
    with contextlib.ExitStack() as es:
        E = es.enter_context
        ones_b = E(nc.sbuf_tensor("ones_b", [128, 1], BF16))
        ones_f1 = E(nc.sbuf_tensor("ones_f1", [1, 128], F32))
        idf = E(nc.sbuf_tensor("idf", [128, 128], F32))
        eps_c = E(nc.sbuf_tensor("eps_c", [128, 1], F32))
        blk = E(nc.Block())
        emit_consts(nc, blk, s, ones_b, ones_f1, idf, eps_c)
        _emit_l1(nc, blk, s, ones_b, ones_f1, idf, eps_c,
                 x_in, x_out, wq, wk, wv, wo, slopes_in, bias_in, "")
    return nc



def _deq_extract(ve, ch, pk_u8, ci, b1, b2, braw_t, gbuf_t, slot):
    def w():
        ve.wait_ge(ch.h, ch.n)
    ch.inc(ve.tensor_copy(ci[:], pk_u8))
    for nib in range(2):
        w()
        if nib == 0:
            ch.inc(ve.tensor_scalar(b1[:], ci[:], 7, None, op0=ALU.bitwise_and))
            ch.inc(ve.tensor_scalar(b2[:], ci[:], 8, 28, op0=ALU.bitwise_and,
                                    op1=ALU.logical_shift_left))
        else:
            ch.inc(ve.tensor_scalar(b1[:], ci[:], 4, 7, op0=ALU.logical_shift_right,
                                    op1=ALU.bitwise_and))
            ch.inc(ve.tensor_scalar(b2[:], ci[:], 128, 24, op0=ALU.bitwise_and,
                                    op1=ALU.logical_shift_left))
        w()
        ch.inc(ve.tensor_scalar(gbuf_t[:, slot, nib, :], b1[:], 2.0 / 3.0, 1.0,
                                op0=ALU.mult, op1=ALU.min))
        ch.inc(ve.tensor_scalar(b1[:], b1[:], 4194304.0, 1056964608.0,
                                op0=ALU.mult, op1=ALU.add))
        w()
        ch.inc(ve.tensor_tensor(braw_t[:].bitcast(I32)[:, slot, nib, :], b1[:], b2[:],
                                op=ALU.bitwise_or))


def _deq_tail(gp, ch, braw_t, gbuf_t, slot, gg, rep_lo, rep_hi, pl_lo, pl_hi):
    for nib, rep_ap, pl_ap in ((0, rep_lo, pl_lo), (1, rep_hi, pl_hi)):
        gp.wait_ge(ch.h, ch.n)
        ch.inc(gp.tensor_tensor(gg[:], braw_t[:, slot, nib, :], gbuf_t[:, slot, nib, :],
                                op=ALU.mult))
        gp.wait_ge(ch.h, ch.n)
        ch.inc(gp.tensor_tensor(pl_ap, gg[:], rep_ap, op=ALU.mult))


def _emit_l2(nc, blk, s, ones_b, ones_f1, idf, eps_c,
             x_in, x_out, gpk, upk, dpkt, gsc, usc, dsc, pfx):
    with contextlib.ExitStack() as es:
        E = es.enter_context
        xT = E(nc.sbuf_tensor(pfx + "xT", [128, DT, TC], F32))
        c05 = E(nc.sbuf_tensor(pfx + "c05", [128, 1], F32))
        sel = E(nc.sbuf_tensor(pfx + "sel", [128, 4, 128], BF16))
        xlnP = E(nc.sbuf_tensor(pfx + "xlnP", [128, DT, TC], BF16))

        cst_c = s["cst"].n

        def sel_ve(ve):
            ve.memset(c05[:], 0.5)
            s["ve"].inc(ve.memset(sel[:], 1.0))
        blk.vector(sel_ve)
        ve_sel = s["ve"].n

        def sel_gp(gp):
            gp.wait_ge(s["ve"].h, ve_sel)
            for q, base in ((0, 0), (1, 128), (2, -1), (3, 127)):
                inst = gp.affine_select(sel[:, q, :], sel[:, q, :], pattern=[[-2, 128]],
                                        compare_op=ALU.is_equal, fill=0.0, base=base,
                                        channel_multiplier=1)
            s["cst"].inc(inst)
        blk.gpsimd(sel_gp)
        cst_sel = s["cst"].n

        with contextlib.ExitStack() as es0:
            E0 = es0.enter_context
            xtm = E0(nc.sbuf_tensor(pfx + "xtm", [128, 2, D], F32))
            xb = E0(nc.sbuf_tensor(pfx + "xb", [128, DT, TC], BF16))
            srow = E0(nc.sbuf_tensor(pfx + "srow", [1, 3, TC], F32))
            xlnT = E0(nc.sbuf_tensor(pfx + "xlnT", [128, DT, TC], BF16))
            tmpf = E0(nc.sbuf_tensor(pfx + "tmpf", [128, 2, 512], F32))
            psA = E0(nc.psum_tensor(pfx + "psA", [128, 512], F32))
            psB = E0(nc.psum_tensor(pfx + "psB", [128, 512], F32))
            psC = E0(nc.psum_tensor(pfx + "psC", [128, 512], F32))
            psD = E0(nc.psum_tensor(pfx + "psD", [128, 512], F32))
            psE = E0(nc.psum_tensor(pfx + "psE", [128, 512], F32))
            psF = E0(nc.psum_tensor(pfx + "psF", [128, 512], F32))
            emit_load_x(nc, blk, s, x_in, xT, xtm, idf, psA, psB)
            ve_ln = emit_ln(nc, blk, s, xT, xlnT, xb, srow, tmpf, ones_b, ones_f1, eps_c,
                            psA, psB, psC, psD, psE, psF, ("se", s["se"].n))

            pe_p0 = s["pe"].n
            se_p0 = s["se"].n

            def prm_te(te):
                te.wait_ge(s["ve"].h, ve_ln)
                te.wait_ge(s["cst"].h, cst_sel)
                for B in range(8):
                    for par in range(2):
                        for th in range(2):
                            i = (B * 2 + par) * 2 + th
                            ps = psA if i % 2 == 0 else psB
                            if i >= 2:
                                te.wait_ge(s["se"].h, se_p0 + i - 1)
                            te.matmul(ps[:], sel[:, 2 * par, :],
                                      xlnT[:, 2 * B, 512 * th:512 * (th + 1)],
                                      start=True, stop=False)
                            inst = te.matmul(ps[:], sel[:, 2 * par + 1, :],
                                             xlnT[:, 2 * B + 1, 512 * th:512 * (th + 1)],
                                             start=False, stop=True)
                            s["pe"].inc(inst)
            blk.tensor(prm_te)

            def prm_se(se):
                for B in range(8):
                    for par in range(2):
                        for th in range(2):
                            i = (B * 2 + par) * 2 + th
                            ps = psA if i % 2 == 0 else psB
                            se.wait_ge(s["pe"].h, pe_p0 + i + 1)
                            s["se"].inc(se.copy(
                                xlnP[:, 2 * B + par, 512 * th:512 * (th + 1)], ps[:]))
            blk.scalar(prm_se)
            se_prm = s["se"].n

        pl_srcs = [nc.dram_tensor(pfx + "pl_src%d" % k2, [1, 2, 128, 4096], BF16,
                                  kind="Internal").ap() for k2 in range(8)]
        pl_alls = [nc.dram_tensor(pfx + "pl_all%d" % k2, [8, 2, 128, 4096], BF16,
                                  kind="Internal").ap() for k2 in range(8)]
        pd_srcs = [nc.dram_tensor(pfx + "pd_src%d" % k2, [1, 128, 4096], BF16,
                                  kind="Internal").ap() for k2 in range(8)]
        pd_alls = [nc.dram_tensor(pfx + "pd_all%d" % k2, [8, 128, 4096], BF16,
                                  kind="Internal").ap() for k2 in range(8)]

        with contextlib.ExitStack() as esp:
            EP = esp.enter_context
            pkb = EP(nc.sbuf_tensor(pfx + "pkb", [128, 2, 3, 2048], U8))
            repb = EP(nc.sbuf_tensor(pfx + "repb", [128, 2, 3, 2048], BF16))
            cib = EP(nc.sbuf_tensor(pfx + "cib", [128, 1024], I32))
            b1b = EP(nc.sbuf_tensor(pfx + "b1b", [128, 1024], I32))
            b2b = EP(nc.sbuf_tensor(pfx + "b2b", [128, 1024], I32))
            braw = EP(nc.sbuf_tensor(pfx + "braw", [128, 2, 2, 1024], F32))
            gbuf = EP(nc.sbuf_tensor(pfx + "gbuf", [128, 2, 2, 1024], F32))
            ggp = EP(nc.sbuf_tensor(pfx + "ggp", [128, 1024], F32))
            pst = EP(nc.sbuf_tensor(pfx + "pst", [128, 3, 4096], BF16))

            def prod_sy(sy):
                sy.wait_ge(s["se"].h, se_prm)
                for j in range(8):
                    if j >= 2:
                        sy.wait_ge(s["dqx"].h, s["dqx"].b + 6 * (j - 1))
                    sm = s["pk%d" % (j % 2)]
                    sm.inc(sy.dma_start(out=pkb[:, j % 2, 0, :], in_=_r(gpk[j])), 16)
                    sm.inc(sy.dma_start(out=pkb[:, j % 2, 1, :], in_=_r(upk[j])), 16)
                    sm.inc(sy.dma_start(out=pkb[:, j % 2, 2, :], in_=dpkt[:, j, :]), 16)
            blk.sync(prod_sy)

            def prod_ve(ve):
                ve.wait_ge(s["cst"].h, cst_c)
                ve.wait_ge(s["se"].h, se_prm)
                k = 0
                for j in range(8):
                    sm = s["pk%d" % (j % 2)]
                    ve.wait_ge(sm.h, sm.b + 48 * (j // 2 + 1))
                    for mat in range(3):
                        for u in range(2):
                            if k >= 2:
                                ve.wait_ge(s["plc"].h, s["plc"].b + 4 * (k - 1))
                            _deq_extract(ve, s["chv"],
                                         pkb[:, j % 2, mat, 1024 * u:1024 * (u + 1)],
                                         cib, b1b, b2b, braw, gbuf, k % 2)
                            s["dqx"].inc(ve.engine_nop())
                            k += 1
            blk.vector(prod_ve)

            def prod_gp(gp):
                gp.wait_ge(s["se"].h, se_prm)
                k = 0
                for j in range(8):
                    if j >= 1:
                        pv = j - 1
                        gp.wait_ge(s["pw%d" % (pv % 2)].h,
                                   s["pw%d" % (pv % 2)].b + 48 * (pv // 2 + 1))
                    rp = s["rp%d" % (j % 2)]
                    rp.inc(gp.dma_start(
                        out=repb[:, j % 2, 0, :],
                        in_=gsc[j].unsqueeze(1).broadcast_to([8, 16, 2048])), 16)
                    rp.inc(gp.dma_start(
                        out=repb[:, j % 2, 1, :],
                        in_=usc[j].unsqueeze(1).broadcast_to([8, 16, 2048])), 16)
                    rp.inc(gp.dma_start(
                        out=repb[:, j % 2, 2, :],
                        in_=dsc[j].unsqueeze(1).broadcast_to([8, 16, 2048])), 16)
                    gp.wait_ge(rp.h, rp.b + 48 * (j // 2 + 1))
                    for mat in range(3):
                        for u in range(2):
                            gp.wait_ge(s["dqx"].h, s["dqx"].b + k + 1)
                            _deq_tail(gp, s["chg"], braw, gbuf, k % 2, ggp,
                                      repb[:, j % 2, mat, 1024 * u:1024 * (u + 1)],
                                      repb[:, j % 2, mat, 1024 * u:1024 * (u + 1)],
                                      pst[:, mat, 1024 * u:1024 * (u + 1)],
                                      pst[:, mat, 2048 + 1024 * u:2048 + 1024 * (u + 1)])
                            s["plc"].inc(gp.engine_nop(), 4)
                            k += 1
                    gp.wait_ge(s["chg"].h, s["chg"].n)
                    pw = s["pw%d" % (j % 2)]
                    pw.inc(gp.dma_start(out=pl_srcs[j][0, 0],
                                        in_=pst[:, 0, :]), 16)
                    pw.inc(gp.dma_start(out=pl_srcs[j][0, 1],
                                        in_=pst[:, 1, :]), 16)
                    pw.inc(gp.dma_start(out=pd_srcs[j][0],
                                        in_=pst[:, 2, :]), 16)
                    pwj = s["pw%d" % (j % 2)]
                    gp.wait_ge(pwj.h, pwj.b + 48 * (j // 2 + 1))
                    s["cc"].inc(gp.collective_compute(
                        "AllGather", ALU.bypass, replica_groups=RG,
                        ins=[pl_srcs[j].opt()], outs=[pl_alls[j].opt()]))
                    s["cc"].inc(gp.collective_compute(
                        "AllGather", ALU.bypass, replica_groups=RG,
                        ins=[pd_srcs[j].opt()], outs=[pd_alls[j].opt()]))
            blk.gpsimd(prod_gp)
            cc_pl0 = s["cc"].n - 16

        with contextlib.ExitStack() as es1:
            E1 = es1.enter_context
            inter = E1(nc.sbuf_tensor(pfx + "inter", [128, 4, TC], BF16))
            gel = E1(nc.sbuf_tensor(pfx + "gel", [128, 2, TC], BF16))
            gsb = E1(nc.sbuf_tensor(pfx + "gsb", [128, TC], BF16))
            usb = E1(nc.sbuf_tensor(pfx + "usb", [128, TC], BF16))
            pgu = E1(nc.sbuf_tensor(pfx + "pgu", [128, 2, 2, 4096], BF16))
            dplc = E1(nc.sbuf_tensor(pfx + "dplc", [128, 2, 2, 4096], BF16))
            psG = E1(nc.psum_tensor(pfx + "psG", [128, 1024], F32))
            psU = E1(nc.psum_tensor(pfx + "psU", [128, 1024], F32))
            psDD = E1(nc.psum_tensor(pfx + "psDD", [128, 2, 1024], F32))

            def con_sy(sy):
                sy.wait_ge(s["pw0"].h, s["pw0"].n)
                sy.wait_ge(s["pw1"].h, s["pw1"].n)
                sy.wait_ge(s["plc"].h, s["plc"].n)
                sy.wait_ge(s["dqx"].h, s["dqx"].n)
                for g in range(32):
                    for br in range(2):
                        b = 2 * g + br
                        j, rk = b // 8, b % 8
                        sy.wait_ge(s["cc"].h, cc_pl0 + 2 * (j + 1))
                        src = pl_alls[j][rk]
                        if b >= 2:
                            sy.wait_ge(s["peg"].h, s["peg"].b + 2 * (b - 1))
                        sm = s["pgl%d" % (b % 2)]
                        sm.inc(sy.dma_start(out=pgu[:, b % 2, 0, :], in_=src[0]), 16)
                        sm.inc(sy.dma_start(out=pgu[:, b % 2, 1, :], in_=src[1]), 16)
                    for dr in range(2):
                        pbd = 2 * g + dr
                        j, rk = pbd // 8, pbd % 8
                        sy.wait_ge(s["cc"].h, cc_pl0 + 2 * (j + 1))
                        if g >= 2 and dr == 0:
                            sy.wait_ge(s["ped"].h, s["ped"].b + 16 * (g - 1))
                        s["pdl%d" % (g % 2)].inc(sy.dma_start(
                            out=dplc[:, g % 2, dr, :],
                            in_=pd_alls[j][rk]), 16)
            blk.sync(con_sy)

            def con_ve(ve):
                for g in range(32):
                    for br in range(2):
                        b = 2 * g + br
                        for par in range(2):
                            idx = 2 * b + par
                            ve.wait_ge(s["peg"].h, s["peg"].b + idx + 1)
                            if idx >= 1:
                                ve.wait_ge(s["gin"].h, s["gin"].b + idx)
                            s["seg"].inc(ve.tensor_copy(usb[:], psU[:]))
                            ve.wait_ge(s["sga"].h, s["sga"].b + idx + 1)
                            s["chv"].inc(ve.tensor_tensor(gel[:, 0, :], gsb[:], gsb[:], op=ALU.mult))
                            ve.wait_ge(s["chv"].h, s["chv"].n)
                            s["chv"].inc(ve.tensor_scalar(gel[:, 0, :], gel[:, 0, :],
                                                          0.044715, 1.0, op0=ALU.mult, op1=ALU.add))
                            ve.wait_ge(s["chv"].h, s["chv"].n)
                            ve.wait_ge(s["gin"].h, s["gin"].b + idx)
                            s["vgl"].inc(ve.tensor_tensor(gel[:, 1, :], gel[:, 0, :], gsb[:], op=ALU.mult))
                    for h in range(DT):
                        idx2 = 16 * g + h
                        ve.wait_ge(s["ped"].h, s["ped"].b + idx2 + 1)
                        s["vda"].inc(ve.tensor_tensor(
                            xT[:, h, :], xT[:, h, :], psDD[:, idx2 % 2, :],
                            op=ALU.add))
            blk.vector(con_ve)

            def con_gp(gp):
                for b in range(64):
                    for par in range(2):
                        idx = 2 * b + par
                        slot2 = (b % 2) * 2 + par
                        gp.wait_ge(s["actt"].h, s["actt"].b + idx + 1)
                        s["chg"].inc(gp.tensor_tensor(gel[:, 1, :], gel[:, 1, :], gsb[:], op=ALU.mult))
                        gp.wait_ge(s["chg"].h, s["chg"].n)
                        if b >= 2:
                            gp.wait_ge(s["ped"].h, s["ped"].b + 16 * (b // 2))
                        s["gin"].inc(gp.tensor_tensor(inter[:, slot2, :], gel[:, 1, :],
                                                      usb[:], op=ALU.mult))
            blk.gpsimd(con_gp)

            def ffn_se(se):
                for b in range(64):
                    for par in range(2):
                        idx = 2 * b + par
                        se.wait_ge(s["peg"].h, s["peg"].b + idx + 1)
                        if idx >= 1:
                            se.wait_ge(s["gin"].h, s["gin"].b + idx)
                            se.wait_ge(s["vgl"].h, s["vgl"].b + idx)
                        s["sga"].inc(se.copy(gsb[:], psG[:]))
                        se.wait_ge(s["vgl"].h, s["vgl"].b + idx + 1)
                        s["chs"].inc(se.activation(gel[:, 1, :], gel[:, 1, :], AF.Tanh,
                                                   bias=0.0, scale=0.7978845608028654))
                        se.wait_ge(s["chs"].h, s["chs"].n)
                        s["actt"].inc(se.activation(gel[:, 1, :], gel[:, 1, :], AF.Identity,
                                                    bias=c05[:], scale=0.5))
            blk.scalar(ffn_se)

            def ffn_te(te):
                te.wait_ge(s["se"].h, se_prm)
                for g in range(32):
                    for br in range(2):
                        b = 2 * g + br
                        sm = s["pgl%d" % (b % 2)]
                        te.wait_ge(sm.h, sm.b + 32 * (b // 2 + 1))
                        for par in range(2):
                            idx = 2 * b + par
                            if idx >= 1:
                                te.wait_ge(s["sga"].h, s["sga"].b + idx)
                                te.wait_ge(s["seg"].h, s["seg"].b + idx)
                            for mat, ps in ((0, psG), (1, psU)):
                                vw = pgu[:, b % 2, mat, :].rearrange(
                                    "p (hi pb n) -> p hi pb n", hi=2, pb=8)
                                for th in range(2):
                                    for kti in range(16):
                                        pb, hi = kti >> 1, kti & 1
                                        inst = te.matmul(
                                            ps[:, 512 * th:512 * (th + 1)],
                                            vw[:, hi, pb, par:256:2],
                                            xlnP[:, kti, 512 * th:512 * (th + 1)],
                                            start=(kti == 0), stop=(kti == 15))
                            s["peg"].inc(inst)
                    te.wait_ge(s["gin"].h, s["gin"].b + 4 * (g + 1))
                    smd = s["pdl%d" % (g % 2)]
                    te.wait_ge(smd.h, smd.b + 32 * (g // 2 + 1))
                    dvw = dplc[:, g % 2, :, :].rearrange(
                        "p dr (nib n) -> p dr nib n", nib=2)
                    for h in range(DT):
                        idx2 = 16 * g + h
                        if idx2 >= 2:
                            te.wait_ge(s["vda"].h, s["vda"].b + idx2 - 1)
                        for th in range(2):
                            for kk2 in range(4):
                                inst = te.matmul(
                                    psDD[:, idx2 % 2, 512 * th:512 * (th + 1)],
                                    dvw[:, kk2 >> 1, kk2 & 1, 128 * h:128 * (h + 1)],
                                    inter[:, kk2, 512 * th:512 * (th + 1)],
                                    start=(kk2 == 0), stop=(kk2 == 3))
                        s["ped"].inc(inst)
            blk.tensor(ffn_te)
            vda_all = 32 * DT

        with contextlib.ExitStack() as es2:
            E2 = es2.enter_context
            xrow = E2(nc.sbuf_tensor(pfx + "xrow", [128, 2, D], F32))
            psA2 = E2(nc.psum_tensor(pfx + "psA2", [128, 512], F32))
            psB2 = E2(nc.psum_tensor(pfx + "psB2", [128, 512], F32))
            emit_store_x(nc, blk, s, xT, x_out, xrow, idf, psA2, psB2,
                         ("vda", s["vda"].b + vda_all))


def build_l2():
    nc = bass.Bass("TRN2", target_bir_lowering=False)
    x_in = nc.dram_tensor("x_in", [TC, D], F32, kind="ExternalInput").ap()
    gpk = nc.dram_tensor("gpk", [8, 128, 8, 256], U8, kind="ExternalInput").ap()
    upk = nc.dram_tensor("upk", [8, 128, 8, 256], U8, kind="ExternalInput").ap()
    dpkt = nc.dram_tensor("dpkt", [128, 8, 2048], U8, kind="ExternalInput").ap()
    gsc = nc.dram_tensor("gsc", [8, 8, 2048], F32, kind="ExternalInput").ap()
    usc = nc.dram_tensor("usc", [8, 8, 2048], F32, kind="ExternalInput").ap()
    dsc = nc.dram_tensor("dsc", [8, 8, 2048], F32, kind="ExternalInput").ap()
    x_out = nc.dram_tensor("x_out", [TC, D], F32, kind="ExternalOutput").ap()
    shared = _sems(nc, SHARED_SEMS)
    s = _sem_view(nc, shared, L2_SEMS, "")
    with contextlib.ExitStack() as es:
        E = es.enter_context
        ones_b = E(nc.sbuf_tensor("ones_b", [128, 1], BF16))
        ones_f1 = E(nc.sbuf_tensor("ones_f1", [1, 128], F32))
        idf = E(nc.sbuf_tensor("idf", [128, 128], F32))
        eps_c = E(nc.sbuf_tensor("eps_c", [128, 1], F32))
        blk = E(nc.Block())
        emit_consts(nc, blk, s, ones_b, ones_f1, idf, eps_c)
        _emit_l2(nc, blk, s, ones_b, ones_f1, idf, eps_c,
                 x_in, x_out, gpk, upk, dpkt, gsc, usc, dsc, "")
    return nc


def _host_l2_inputs(inputs):
    def bytes_of(a):
        a = np.ascontiguousarray(a)
        return a.view(np.uint8).reshape(a.shape[0], a.shape[1], 4)[:, :, 0]
    gb = bytes_of(np.asarray(inputs["L0_ffn_gate_W_packed"]))
    ub = bytes_of(np.asarray(inputs["L0_ffn_up_W_packed"]))
    db = bytes_of(np.asarray(inputs["L0_ffn_down_W_packed"]))
    def blkgu(x):
        t = np.ascontiguousarray(x.T)
        t = t.reshape(8, 128, 64, 256)
        return np.ascontiguousarray(t.transpose(2, 1, 0, 3))
    g3 = blkgu(gb)
    u3 = blkgu(ub)
    d_t = np.ascontiguousarray(db.T)
    d3 = np.ascontiguousarray(d_t.reshape(64, 128, 2048).transpose(1, 0, 2))
    def repgu(scales):
        t = np.ascontiguousarray(scales.T)
        t = t.reshape(8, 8, 64, 256).transpose(2, 1, 0, 3)
        return np.ascontiguousarray(t.reshape(64, 8, 2048))
    gs = repgu(np.asarray(inputs["L0_ffn_gate_scales"]))
    us = repgu(np.asarray(inputs["L0_ffn_up_scales"]))
    ds = np.ascontiguousarray(np.asarray(inputs["L0_ffn_down_scales"]).T)
    return g3, u3, d3, gs, us, ds


def _host_l2_core(l2h, c):
    g3, u3, d3, gs, us, ds = l2h
    return {
        "gpk": np.ascontiguousarray(g3[c::8]),
        "upk": np.ascontiguousarray(u3[c::8]),
        "dpkt": np.ascontiguousarray(d3[:, c::8, :]),
        "gsc": np.ascontiguousarray(gs[c::8]),
        "usc": np.ascontiguousarray(us[c::8]),
        "dsc": np.ascontiguousarray(ds.reshape(64, 8, 2048)[c::8]),
    }


def run_l2(x2d, inputs):
    nc = build_l2()
    l2h = _host_l2_inputs(inputs)
    in_maps = []
    for c in range(NC):
        m = {"x_in": np.ascontiguousarray(x2d[TC * c:TC * (c + 1)])}
        m.update(_host_l2_core(l2h, c))
        in_maps.append(m)
    res = run_bass_kernel_spmd(nc, in_maps, core_ids=list(range(NC)))
    return np.concatenate([res.results[c]["x_out"] for c in range(NC)], axis=0)




def _emit_l3(nc, blk, s, ones_b, ones_f1, idf, eps_c,
             x_in, x_out, wq, wk, wv, wg, wo, sel_in, pfx):
    qscr = nc.dram_tensor(pfx + "qscr", [16, 128, TC], BF16, kind="Internal").ap()
    ags_src = nc.dram_tensor(pfx + "ags_src", [128, 2048], F32, kind="Internal").ap()
    ags_dst = nc.dram_tensor(pfx + "ags_dst", [128 * NC, 2048], F32, kind="Internal").ap()
    agd_src = nc.dram_tensor(pfx + "agd_src", [128, 16], F32, kind="Internal").ap()
    agd_dst = nc.dram_tensor(pfx + "agd_dst", [128 * NC, 16], F32, kind="Internal").ap()

    with contextlib.ExitStack() as es:
        E = es.enter_context
        xT = E(nc.sbuf_tensor(pfx + "xT", [128, DT, TC], F32))
        idb = E(nc.sbuf_tensor(pfx + "idb", [128, 128], BF16))
        onesf = E(nc.sbuf_tensor(pfx + "onesf", [128, 128], F32))
        um = E(nc.sbuf_tensor(pfx + "um", [128, 128], BF16))
        selc = E(nc.sbuf_tensor(pfx + "selc", [128, 7], F32))
        oT = E(nc.sbuf_tensor(pfx + "oT", [128, DT, TC], BF16))
        Sall = E(nc.sbuf_tensor(pfx + "Sall", [128, 16, 128], F32))
        gsum = E(nc.sbuf_tensor(pfx + "gsum", [128, 16], F32))
        Dsb = E(nc.sbuf_tensor(pfx + "Dsb", [128, 16], F32))
        glpre = E(nc.sbuf_tensor(pfx + "glpre", [128, 16, 8], F32))
        eglp = E(nc.sbuf_tensor(pfx + "eglp", [128, 16, 8], F32))

        cst_c = s["cst"].n

        def c3_ve(ve):
            ve.memset(onesf[:], 1.0)
            ve.memset(idb[:], 1.0)
            ve.memset(gsum[:], 0.0)
            ve.memset(glpre[:], 0.0)
            s["ve"].inc(ve.memset(um[:], 1.0))
        blk.vector(c3_ve)
        ve_c3 = s["ve"].n

        def c3_gp(gp):
            gp.wait_ge(s["ve"].h, ve_c3)
            gp.affine_select(idb[:], idb[:], pattern=[[-1, 128]],
                             compare_op=ALU.is_equal, fill=0.0, base=0,
                             channel_multiplier=1)
            s["cst"].inc(gp.affine_select(um[:], um[:], pattern=[[1, 128]],
                                          compare_op=ALU.is_ge, fill=0.0, base=0,
                                          channel_multiplier=-1))
            s["dsl"].inc(gp.dma_start(out=selc[:],
                                      in_=sel_in.unsqueeze(0).broadcast_to([128, 7])), 16)
        blk.gpsimd(c3_gp)
        cst_um = s["cst"].n

        with contextlib.ExitStack() as es0:
            E0 = es0.enter_context
            xlnT = E0(nc.sbuf_tensor(pfx + "xlnT", [128, DT, TC], BF16))
            with contextlib.ExitStack() as esp:
                EP = esp.enter_context
                xtm = EP(nc.sbuf_tensor(pfx + "xtm", [128, 2, D], F32))
                xb = EP(nc.sbuf_tensor(pfx + "xb", [128, DT, TC], BF16))
                srow = EP(nc.sbuf_tensor(pfx + "srow", [1, 3, TC], F32))
                tmpf = EP(nc.sbuf_tensor(pfx + "tmpf", [128, 2, 512], F32))
                psA = EP(nc.psum_tensor(pfx + "psA", [128, 512], F32))
                psB = EP(nc.psum_tensor(pfx + "psB", [128, 512], F32))
                psC = EP(nc.psum_tensor(pfx + "psC", [128, 512], F32))
                psD = EP(nc.psum_tensor(pfx + "psD", [128, 512], F32))
                psE = EP(nc.psum_tensor(pfx + "psE", [128, 512], F32))
                psF = EP(nc.psum_tensor(pfx + "psF", [128, 512], F32))
                emit_load_x(nc, blk, s, x_in, xT, xtm, idf, psA, psB)
                ve_ln = emit_ln(nc, blk, s, xT, xlnT, xb, srow, tmpf, ones_b, ones_f1,
                                eps_c, psA, psB, psC, psD, psE, psF, ("se", s["se"].n))

            with contextlib.ExitStack() as ess:
                ES = ess.enter_context
                psQ = ES(nc.psum_tensor(pfx + "psQ", [128, 512], F32))
                psR = ES(nc.psum_tensor(pfx + "psR", [128, 512], F32))
                psPA = ES(nc.psum_tensor(pfx + "psPA", [128, 512], F32))
                psPO = ES(nc.psum_tensor(pfx + "psPO", [128, 512], F32))
                psPS = ES(nc.psum_tensor(pfx + "psPS", [128, 512], F32))
                psPT = ES(nc.psum_tensor(pfx + "psPT", [128, 512], F32))
                wsl = ES(nc.sbuf_tensor(pfx + "wsl", [128, 4, DT, 128], BF16))
                qh = ES(nc.sbuf_tensor(pfx + "qh", [128, TC], BF16))
                kh = ES(nc.sbuf_tensor(pfx + "kh", [128, TC], BF16))
                vh = ES(nc.sbuf_tensor(pfx + "vh", [128, 8, 128], BF16))
                kc = ES(nc.sbuf_tensor(pfx + "kc", [128, 128], BF16))
                khat = ES(nc.sbuf_tensor(pfx + "khat", [128, 128], BF16))
                khtm = ES(nc.sbuf_tensor(pfx + "khtm", [128, 128], BF16))
                pbuf = ES(nc.sbuf_tensor(pfx + "pbuf", [128, 128], BF16))
                sgb = ES(nc.sbuf_tensor(pfx + "sgb", [128, TC], F32))
                Gb = ES(nc.sbuf_tensor(pfx + "Gb", [128, TC], F32))
                e1 = ES(nc.sbuf_tensor(pfx + "e1", [128, 128], F32))
                e2 = ES(nc.sbuf_tensor(pfx + "e2", [128, 128], F32))
                egl = ES(nc.sbuf_tensor(pfx + "egl", [128, 1], F32))
                lga = ES(nc.sbuf_tensor(pfx + "lga", [128, TC], F32))
                lgb = ES(nc.sbuf_tensor(pfx + "lgb", [128, TC], F32))
                Shb = ES(nc.sbuf_tensor(pfx + "Shb", [128, 128], BF16))

                def gl_gp(gp):
                    gp.wait_ge(s["ve"].h, ve_ln)
                    for h in range(16):
                        if h >= 1:
                            gp.wait_ge(s["pjmm"].h, s["pjmm"].b + 8 * h)
                        for wi, wt in ((0, wq), (1, wk), (2, wv), (3, wg)):
                            s["wld"].inc(gp.dma_start(
                                out=wsl[:, wi, :, :],
                                in_=wt.rearrange("(kt p) m -> p kt m", p=128)
                                      [:, :, 128 * h:128 * (h + 1)]), 16)
                        if h >= 1:
                            gp.wait_ge(s["vst"].h, s["vst"].b + 8 * h)
                            s["qwb"].inc(gp.dma_start(out=qscr[h - 1], in_=qh[:]), 16)
                    gp.wait_ge(s["vst"].h, s["vst"].b + 128)
                    s["qwb"].inc(gp.dma_start(out=qscr[15], in_=qh[:]), 16)
                    gp.wait_ge(s["ssb"].h, s["ssb"].b + 128)
                    s["agw"].inc(gp.dma_start(
                        out=ags_src.rearrange("p (a b) -> p a b", a=16), in_=Sall[:]), 16)
                    gp.wait_ge(s["dal"].h, s["dal"].b + 1)
                    s["agw"].inc(gp.dma_start(out=agd_src, in_=Dsb[:]), 16)
                    gp.wait_ge(s["agw"].h, s["agw"].n)
                    s["cc"].inc(gp.collective_compute(
                        "AllGather", ALU.bypass, replica_groups=RG,
                        ins=[ags_src.opt()], outs=[ags_dst.opt()]))
                    s["cc"].inc(gp.collective_compute(
                        "AllGather", ALU.bypass, replica_groups=RG,
                        ins=[agd_src.opt()], outs=[agd_dst.opt()]))
                blk.gpsimd(gl_gp)
                cc_ags = s["cc"].n

                def gl_te(te):
                    te.wait_ge(s["ve"].h, ve_ln)
                    te.wait_ge(s["cst"].h, cst_um)
                    for h in range(16):
                        te.wait_ge(s["wld"].h, s["wld"].b + 64 * (h + 1))
                        if h >= 1:
                            te.wait_ge(s["qwb"].h, s["qwb"].b + 16 * h)
                            te.wait_ge(s["vkc"].h, s["vkc"].b + 8 * h)
                            te.wait_ge(s["pes"].h, s["pes"].b + 8 * h)
                        for gidx in range(8):
                            i = 8 * h + gidx
                            ps = psQ if gidx % 2 == 0 else psR
                            if i >= 2:
                                te.wait_ge(s["pjev"].h, s["pjev"].b + i - 1)
                            if gidx < 2:
                                wi, th = 0, gidx
                            elif gidx < 4:
                                wi, th = 1, gidx - 2
                            elif gidx < 6:
                                wi, th = 2, gidx - 4
                            else:
                                wi, th = 3, gidx - 6
                            if wi == 2:
                                for j in range(4):
                                    tt4 = th * 4 + j
                                    for kt in range(DT):
                                        inst = te.matmul(
                                            ps[:, 128 * j:128 * (j + 1)],
                                            xlnT[:, kt, 128 * tt4:128 * (tt4 + 1)],
                                            wsl[:, 2, kt, :],
                                            start=(j == 0 and kt == 0),
                                            stop=(j == 3 and kt == DT - 1))
                            else:
                                for kt in range(DT):
                                    inst = te.matmul(
                                        ps[:], wsl[:, wi, kt, :],
                                        xlnT[:, kt, 512 * th:512 * (th + 1)],
                                        start=(kt == 0), stop=(kt == DT - 1))
                            s["pjmm"].inc(inst)
                        for ch in range(8):
                            u = 8 * h + ch
                            te.wait_ge(s["vkc"].h, s["vkc"].b + u + 1)
                            s["pea"].inc(te.matmul(psPA[:, 0:128], kc[:],
                                                   qh[:, 128 * ch:128 * (ch + 1)],
                                                   start=True, stop=True))
                            if u >= 1:
                                te.wait_ge(s["sekh"].h, s["sekh"].b + u)
                            s["pet"].inc(te.transpose(
                                psPT[:].bitcast(BF16)[:, 0:128], khat[:], idb[:]))
                            te.wait_ge(s["vmsk"].h, s["vmsk"].b + u + 1)
                            if u >= 1:
                                te.wait_ge(s["soev"].h, s["soev"].b + u)
                            if ch > 0:
                                te.wait_ge(s["ssb"].h, s["ssb"].b + u)
                                te.matmul(psPO[:, 0:128], Shb[:],
                                          qh[:, 128 * ch:128 * (ch + 1)],
                                          start=True, stop=False)
                            s["peo"].inc(te.matmul(psPO[:, 0:128], vh[:, ch, :], pbuf[:],
                                                   start=(ch == 0), stop=True))
                            te.wait_ge(s["sekh"].h, s["sekh"].b + u + 1)
                            if u >= 1:
                                te.wait_ge(s["vst"].h, s["vst"].b + u)
                            s["pes"].inc(te.matmul(psPS[:, 0:128], khtm[:], vh[:, ch, :],
                                                   start=True, stop=True))
                blk.tensor(gl_te)

                def gl_se(se):
                    for h in range(16):
                        for gidx in range(6):
                            i = 8 * h + gidx
                            ps = psQ if gidx % 2 == 0 else psR
                            se.wait_ge(s["pjmm"].h, s["pjmm"].b + i + 1)
                            if gidx < 2:
                                if h >= 1 and gidx == 0:
                                    se.wait_ge(s["qwb"].h, s["qwb"].b + 16 * h)
                                inst = se.activation(qh[:, 512 * gidx:512 * (gidx + 1)], ps[:],
                                                     AF.Identity, bias=0.0, scale=ISQ)
                            elif gidx < 4:
                                if h >= 1 and gidx == 2:
                                    se.wait_ge(s["vkc"].h, s["vkc"].b + 8 * h)
                                inst = se.copy(kh[:, 512 * (gidx - 2):512 * (gidx - 1)], ps[:])
                            else:
                                if h >= 1 and gidx == 4:
                                    se.wait_ge(s["pes"].h, s["pes"].b + 8 * h)
                                inst = se.copy(vh[:, 4 * (gidx - 4):4 * (gidx - 3), :], ps[:])
                            s["pjev"].inc(inst)
                        for th in range(2):
                            ps = psQ if th == 0 else psR
                            se.wait_ge(s["pjmm"].h, s["pjmm"].b + 8 * h + 7 + th)
                            if h >= 1 and th == 0:
                                se.wait_ge(s["vsg"].h, s["vsg"].b + 2 * h)
                            sl = slice(512 * th, 512 * (th + 1))
                            s["chs"].inc(se.activation(lgb[:, sl], ps[:], AF.Abs,
                                                       bias=0.0, scale=1.0))
                            se.wait_ge(s["chs"].h, s["chs"].n)
                            s["chs"].inc(se.activation(lgb[:, sl], lgb[:, sl], AF.Exp,
                                                       bias=0.0, scale=-1.0))
                            se.wait_ge(s["chs"].h, s["chs"].n)
                            s["actl"].inc(se.activation(lgb[:, sl], lgb[:, sl], AF.Ln,
                                                        bias=1.0, scale=1.0))
                        for ch in range(8):
                            u = 8 * h + ch
                            se.wait_ge(s["vcs"].h, s["vcs"].b + u + 1)
                            if u >= 1:
                                se.wait_ge(s["vst"].h, s["vst"].b + u)
                            gsl = slice(128 * ch, 128 * (ch + 1))
                            se.activation(e1[:], Gb[:, gsl], AF.Exp, bias=0.0, scale=1.0 / 16)
                            se.activation(e2[:], Gb[:, gsl], AF.Exp, bias=0.0, scale=-1.0 / 16)
                            s["ecs"].inc(se.activation(egl[:], Gb[:, 128 * ch + 127:128 * (ch + 1)],
                                                       AF.Exp, bias=0.0, scale=1.0 / 16))
                            se.wait_ge(s["pet"].h, s["pet"].b + u + 1)
                            s["sekh"].inc(se.copy(khtm[:], psPT[:].bitcast(BF16)[:, 0:128]))
                            se.wait_ge(s["vst"].h, s["vst"].b + u + 1)
                            s["ssb"].inc(se.copy(Shb[:], Sall[:, h, :]))
                            se.wait_ge(s["peo"].h, s["peo"].b + u + 1)
                            s["soev"].inc(se.copy(oT[:, h, gsl], psPO[:, 0:128]))
                    se.wait_ge(s["vst"].h, s["vst"].b + 128)
                    s["dal"].inc(se.activation(Dsb[:], gsum[:], AF.Exp, bias=0.0, scale=1.0 / 16))
                blk.scalar(gl_se)

                def gl_ve(ve):
                    ve.wait_ge(s["cst"].h, cst_um)
                    for h in range(16):
                        for th in range(2):
                            ps = psQ if th == 0 else psR
                            sl = slice(512 * th, 512 * (th + 1))
                            ve.wait_ge(s["chs"].h, s["chs"].b + (4 * h + 2 * th + 1))
                            if h >= 1 and th == 0:
                                ve.wait_ge(s["vcs"].h, s["vcs"].b + 8 * h)
                            s["chv"].inc(ve.tensor_scalar(lga[:, sl], ps[:], 0.0, None,
                                                          op0=ALU.min))
                            s["pjev"].inc(ve.engine_nop())
                            ve.wait_ge(s["actl"].h, s["actl"].b + 2 * h + th + 1)
                            ve.wait_ge(s["chv"].h, s["chv"].n)
                            s["vsg"].inc(ve.scalar_tensor_tensor(sgb[:, sl], lgb[:, sl], -1.0,
                                                                 lga[:, sl], op0=ALU.mult,
                                                                 op1=ALU.add))
                        for ch in range(8):
                            u = 8 * h + ch
                            gsl = slice(128 * ch, 128 * (ch + 1))
                            ve.wait_ge(s["vsg"].h, s["vsg"].b + 2 * h + 1 + (1 if ch >= 4 else 0))
                            if h >= 1 and ch == 0:
                                ve.wait_ge(s["ecs"].h, s["ecs"].b + 8 * h)
                            s["vcs"].inc(ve.tensor_tensor_scan(Gb[:, gsl], onesf[:], sgb[:, gsl],
                                                               0.0, op0=ALU.mult, op1=ALU.add))
                            ve.wait_ge(s["ecs"].h, s["ecs"].b + u + 1)
                            s["chv"].inc(ve.tensor_tensor(qh[:, gsl], qh[:, gsl], e1[:], op=ALU.mult))
                            s["chv"].inc(ve.tensor_tensor(kc[:], kh[:, gsl], e2[:], op=ALU.mult))
                            ve.wait_ge(s["chv"].h, s["chv"].n)
                            ve.wait_ge(s["pea"].h, s["pea"].b + u)
                            s["vkc"].inc(ve.tensor_scalar(khat[:], kc[:], egl[:, 0:1], None,
                                                          op0=ALU.mult))
                            ve.wait_ge(s["pea"].h, s["pea"].b + u + 1)
                            if u >= 1:
                                ve.wait_ge(s["peo"].h, s["peo"].b + u)
                            s["vmsk"].inc(ve.tensor_tensor(pbuf[:], psPA[:, 0:128], um[:],
                                                           op=ALU.mult))
                            ve.wait_ge(s["pes"].h, s["pes"].b + u + 1)
                            if ch == 0:
                                s["chv"].inc(ve.tensor_scalar(Sall[:, h, :], psPS[:, 0:128],
                                                              1.0, None, op0=ALU.mult))
                            else:
                                s["chv"].inc(ve.scalar_tensor_tensor(
                                    Sall[:, h, :], Sall[:, h, :], egl[:, 0:1], psPS[:, 0:128],
                                    op0=ALU.mult, op1=ALU.add))
                            ve.wait_ge(s["chv"].h, s["chv"].n)
                            s["chv"].inc(ve.tensor_copy(glpre[:, h, ch:ch + 1], gsum[:, h:h + 1]))
                            ve.wait_ge(s["chv"].h, s["chv"].n)
                            s["vst"].inc(ve.tensor_tensor(gsum[:, h:h + 1], gsum[:, h:h + 1],
                                                          Gb[:, 128 * ch + 127:128 * (ch + 1)],
                                                          op=ALU.add))
                blk.vector(gl_ve)

        with contextlib.ExitStack() as esr:
            ER = esr.enter_context
            Pbuf = ER(nc.sbuf_tensor(pfx + "Pbuf", [128, 16, 128], F32))
            Sin = ER(nc.sbuf_tensor(pfx + "Sin", [128, 16, 128], F32))
            Sjb = ER(nc.sbuf_tensor(pfx + "Sjb", [128, 2, 16, 128], F32))
            Djall = ER(nc.sbuf_tensor(pfx + "Djall", [128, 8, 16], F32))
            qld = ER(nc.sbuf_tensor(pfx + "qld", [128, 2, TC], BF16))
            csd = ER(nc.sbuf_tensor(pfx + "csd", [128, 2, 128], BF16))
            psX = ER(nc.psum_tensor(pfx + "psX", [128, 512], F32))
            psY = ER(nc.psum_tensor(pfx + "psY", [128, 512], F32))

            def rl_sy(sy):
                sy.wait_ge(s["cc"].h, cc_ags)
                s["djl"].inc(sy.dma_start(
                    out=Djall[:],
                    in_=agd_dst.rearrange("(j p) d -> p j d", p=128)), 16)
                for j in range(7):
                    if j >= 2:
                        sy.wait_ge(s["rel"].h, s["rel"].b + j - 1)
                    s["sjl%d" % (j % 2)].inc(sy.dma_start(
                        out=Sjb[:, j % 2, :, :],
                        in_=ags_dst[128 * j:128 * (j + 1), :]
                            .rearrange("p (a b) -> p a b", a=16)), 16)
                for h in range(16):
                    sy.wait_ge(s["qwb"].h, s["qwb"].b + 16 * (h + 1))
                    if h >= 2:
                        sy.wait_ge(s["pco"].h, s["pco"].b + 8 * (h - 1))
                    s["cl%d" % (h % 2)].inc(sy.dma_start(
                        out=qld[:, h % 2, :], in_=qscr[h]), 16)
            blk.sync(rl_sy)

            def rl_ve(ve):
                ve.memset(Sin[:], 0.0)
                ve.wait_ge(s["djl"].h, s["djl"].b + 16)
                for j in range(7):
                    ve.wait_ge(s["sjl%d" % (j % 2)].h, s["sjl%d" % (j % 2)].b + 16 * (j // 2 + 1))
                    if j >= 1:
                        ve.wait_ge(s["rel"].h, s["rel"].b + j)
                    if j == 0:
                        s["chv"].inc(ve.tensor_scalar(_r(Pbuf[:]), _r(Sjb[:, 0, :, :]),
                                                      1.0, None, op0=ALU.mult))
                    else:
                        s["chv"].inc(ve.tensor_tensor(
                            Pbuf[:], Pbuf[:],
                            Djall[:, j, :].unsqueeze(2).broadcast_to([128, 16, 128]),
                            op=ALU.mult))
                        ve.wait_ge(s["chv"].h, s["chv"].n)
                        s["chv"].inc(ve.tensor_tensor(_r(Pbuf[:]), _r(Pbuf[:]),
                                                      _r(Sjb[:, j % 2, :, :]), op=ALU.add))
                    ve.wait_ge(s["chv"].h, s["chv"].n)
                    ve.wait_ge(s["dsl"].h, s["dsl"].b + 16)
                    s["rel"].inc(ve.scalar_tensor_tensor(_r(Sin[:]), _r(Pbuf[:]), selc[:, j:j + 1],
                                                         _r(Sin[:]), op0=ALU.mult, op1=ALU.add))
            blk.vector(rl_ve)

            def rl_se(se):
                se.wait_ge(s["vst"].h, s["vst"].b + 128)
                s["epx"].inc(se.activation(_r(eglp[:]), _r(glpre[:]), AF.Exp,
                                           bias=0.0, scale=1.0 / 16))
            blk.scalar(rl_se)

            def co_ve(ve):
                ve.wait_ge(s["rel"].h, s["rel"].b + 7)
                ve.wait_ge(s["epx"].h, s["epx"].b + 1)
                for h in range(16):
                    ve.wait_ge(s["cl%d" % (h % 2)].h, s["cl%d" % (h % 2)].b + 16 * (h // 2 + 1))
                    for ch in range(8):
                        u = 8 * h + ch
                        if u >= 2:
                            ve.wait_ge(s["pco"].h, s["pco"].b + u - 1)
                        s["vcp"].inc(ve.tensor_scalar(csd[:, u % 2, :], Sin[:, h, :],
                                                      eglp[:, h, ch:ch + 1], None,
                                                      op0=ALU.mult))
                        ve.wait_ge(s["pco"].h, s["pco"].b + u + 1)
                        s["vca"].inc(ve.tensor_tensor(
                            oT[:, h, 128 * ch:128 * (ch + 1)],
                            oT[:, h, 128 * ch:128 * (ch + 1)],
                            (psX if u % 2 == 0 else psY)[:, 0:128], op=ALU.add))
            blk.vector(co_ve)

            def co_te(te):
                for h in range(16):
                    for ch in range(8):
                        u = 8 * h + ch
                        te.wait_ge(s["vcp"].h, s["vcp"].b + u + 1)
                        if u >= 2:
                            te.wait_ge(s["vca"].h, s["vca"].b + u - 1)
                        s["pco"].inc(te.matmul((psX if u % 2 == 0 else psY)[:, 0:128],
                                               csd[:, u % 2, :],
                                               qld[:, h % 2, 128 * ch:128 * (ch + 1)],
                                               start=True, stop=True))
            blk.tensor(co_te)
            vca_all = 128

        with contextlib.ExitStack() as esw:
            EW = esw.enter_context
            wo_sb = EW(nc.sbuf_tensor(pfx + "wo_sb", [128, DT, D], BF16))
            psW1 = EW(nc.psum_tensor(pfx + "psW1", [128, 512], F32))
            psW2 = EW(nc.psum_tensor(pfx + "psW2", [128, 512], F32))

            def wo_gp(gp):
                gp.wait_ge(s["vca"].h, s["vca"].b + vca_all)
                for q in range(4):
                    s["wold"].inc(gp.dma_start(
                        out=wo_sb[:, 4 * q:4 * (q + 1), :],
                        in_=wo.rearrange("(kt p) m -> p kt m", p=128)[:, 4 * q:4 * (q + 1), :]), 16)
            blk.gpsimd(wo_gp)
            ldg_wo = s["wold"].n

            def wo_te(te):
                te.wait_ge(s["wold"].h, ldg_wo)
                te.wait_ge(s["vca"].h, s["vca"].b + vca_all)
                for ho in range(DT):
                    for th in range(2):
                        i = ho * 2 + th
                        ps = psW1 if i % 2 == 0 else psW2
                        if i >= 2:
                            te.wait_ge(s["vwo"].h, s["vwo"].b + i - 1)
                        for kt in range(DT):
                            inst = te.matmul(ps[:], wo_sb[:, kt, 128 * ho:128 * (ho + 1)],
                                             oT[:, kt, 512 * th:512 * (th + 1)],
                                             start=(kt == 0), stop=(kt == DT - 1))
                        s["pwo"].inc(inst)
            blk.tensor(wo_te)

            def wo_ve(ve):
                for ho in range(DT):
                    for th in range(2):
                        i = ho * 2 + th
                        ps = psW1 if i % 2 == 0 else psW2
                        ve.wait_ge(s["pwo"].h, s["pwo"].b + i + 1)
                        s["vwo"].inc(ve.tensor_tensor(
                            xT[:, ho, 512 * th:512 * (th + 1)],
                            xT[:, ho, 512 * th:512 * (th + 1)], ps[:], op=ALU.add))
            blk.vector(wo_ve)
            ve_res = s["vwo"].n

            with contextlib.ExitStack() as esf:
                EF = esf.enter_context
                xrow = EF(nc.sbuf_tensor(pfx + "xrow", [128, 2, D], F32))
                emit_store_x(nc, blk, s, xT, x_out, xrow, idf, psW1, psW2, ("vwo", ve_res))


def build_l3():
    nc = bass.Bass("TRN2", target_bir_lowering=False)
    x_in = nc.dram_tensor("x_in", [TC, D], F32, kind="ExternalInput").ap()
    wq = nc.dram_tensor("wq", [D, D], F32, kind="ExternalInput").ap()
    wk = nc.dram_tensor("wk", [D, D], F32, kind="ExternalInput").ap()
    wv = nc.dram_tensor("wv", [D, D], F32, kind="ExternalInput").ap()
    wg = nc.dram_tensor("wg", [D, D], F32, kind="ExternalInput").ap()
    wo = nc.dram_tensor("wo", [D, D], F32, kind="ExternalInput").ap()
    sel_in = nc.dram_tensor("sel_in", [7], F32, kind="ExternalInput").ap()
    x_out = nc.dram_tensor("x_out", [TC, D], F32, kind="ExternalOutput").ap()
    shared = _sems(nc, SHARED_SEMS)
    s = _sem_view(nc, shared, L3_SEMS, "")
    with contextlib.ExitStack() as es:
        E = es.enter_context
        ones_b = E(nc.sbuf_tensor("ones_b", [128, 1], BF16))
        ones_f1 = E(nc.sbuf_tensor("ones_f1", [1, 128], F32))
        idf = E(nc.sbuf_tensor("idf", [128, 128], F32))
        eps_c = E(nc.sbuf_tensor("eps_c", [128, 1], F32))
        blk = E(nc.Block())
        emit_consts(nc, blk, s, ones_b, ones_f1, idf, eps_c)
        _emit_l3(nc, blk, s, ones_b, ones_f1, idf, eps_c,
                 x_in, x_out, wq, wk, wv, wg, wo, sel_in, "")
    return nc


def run_l3(x2d, inputs):
    nc = build_l3()
    in_maps = []
    for c in range(NC):
        sel = np.zeros(7, np.float32)
        if c >= 1:
            sel[c - 1] = 1.0
        in_maps.append({
            "x_in": np.ascontiguousarray(x2d[TC * c:TC * (c + 1)]),
            "wq": np.asarray(inputs["L1_attn_Wq"]),
            "wk": np.asarray(inputs["L1_attn_Wk"]),
            "wv": np.asarray(inputs["L1_attn_Wv"]),
            "wg": np.asarray(inputs["L1_attn_Wg"]),
            "wo": np.asarray(inputs["L1_attn_Wo"]),
            "sel_in": sel,
        })
    res = run_bass_kernel_spmd(nc, in_maps, core_ids=list(range(NC)))
    return np.concatenate([res.results[c]["x_out"] for c in range(NC)], axis=0)



def _emit_l4(nc, blk, s, ones_b, ones_f1, idf, eps_c,
             x_in, x_out, rwT, wgT, wuT, wdT, agp, adt, bgp, bdt, pfx):
    mixd = nc.dram_tensor(pfx + "mixd", [8192, TC], BF16, kind="Internal").ap()
    wscr = nc.dram_tensor(pfx + "wscr", [8, TC], F32, kind="Internal").ap()

    with contextlib.ExitStack() as es:
        E = es.enter_context
        idb = E(nc.sbuf_tensor(pfx + "idb", [128, 128], BF16))
        ybuf = E(nc.sbuf_tensor(pfx + "ybuf", [128, TC], BF16))
        wtok = E(nc.sbuf_tensor(pfx + "wtok", [128, 8, 8], F32))

        def c2_ve(ve):
            s["ve"].inc(ve.memset(idb[:], 1.0))
        blk.vector(c2_ve)
        ve_idb = s["ve"].n

        def c2_gp(gp):
            gp.wait_ge(s["ve"].h, ve_idb)
            s["cst"].inc(gp.affine_select(idb[:], idb[:], pattern=[[-1, 128]],
                                          compare_op=ALU.is_equal, fill=0.0, base=0,
                                          channel_multiplier=1))
        blk.gpsimd(c2_gp)
        cst_c = s["cst"].n

        with contextlib.ExitStack() as es1:
            E1 = es1.enter_context
            xlnT = E1(nc.sbuf_tensor(pfx + "xlnT", [128, DT, TC], BF16))

            with contextlib.ExitStack() as es0:
                E0 = es0.enter_context
                xT0 = E0(nc.sbuf_tensor(pfx + "xT0", [128, DT, TC], F32))
                xtm = E0(nc.sbuf_tensor(pfx + "xtm", [128, 2, D], F32))
                xb = E0(nc.sbuf_tensor(pfx + "xb", [128, DT, TC], BF16))
                srow = E0(nc.sbuf_tensor(pfx + "srow", [1, 3, TC], F32))
                tmpf = E0(nc.sbuf_tensor(pfx + "tmpf", [128, 2, 512], F32))
                psA = E0(nc.psum_tensor(pfx + "psA", [128, 512], F32))
                psB = E0(nc.psum_tensor(pfx + "psB", [128, 512], F32))
                psC = E0(nc.psum_tensor(pfx + "psC", [128, 512], F32))
                psD = E0(nc.psum_tensor(pfx + "psD", [128, 512], F32))
                psE = E0(nc.psum_tensor(pfx + "psE", [128, 512], F32))
                psF = E0(nc.psum_tensor(pfx + "psF", [128, 512], F32))
                emit_load_x(nc, blk, s, x_in, xT0, xtm, idf, psA, psB)
                ve_ln = emit_ln(nc, blk, s, xT0, xlnT, xb, srow, tmpf, ones_b, ones_f1,
                                eps_c, psA, psB, psC, psD, psE, psF, ("se", s["se"].n))

            webc = E1(nc.sbuf_tensor(pfx + "webc", [128, 8, TC], BF16))
            zbg = E1(nc.sbuf_tensor(pfx + "zbg", [32, 8, TC], BF16))
            zbu = E1(nc.sbuf_tensor(pfx + "zbu", [32, 8, TC], BF16))
            with contextlib.ExitStack() as esr:
                ER = esr.enter_context
                rw_sb = ER(nc.sbuf_tensor(pfx + "rw_sb", [128, DT, 8], BF16))
                prb = ER(nc.sbuf_tensor(pfx + "prb", [128, 8, 8], F32))
                lgt = ER(nc.sbuf_tensor(pfx + "lgt", [128, 8, 8], F32))
                eqa = ER(nc.sbuf_tensor(pfx + "eqa", [128, 8, 8], F32))
                rcl = ER(nc.sbuf_tensor(pfx + "rcl", [128, 4], F32))
                wT8 = ER(nc.sbuf_tensor(pfx + "wT8", [8, TC], F32))
                agp_sb = ER(nc.sbuf_tensor(pfx + "agp_sb", [128, 2, DT, 128], BF16))
                ztmp = ER(nc.sbuf_tensor(pfx + "ztmp", [128, TC], BF16))
                ztmpn = ER(nc.sbuf_tensor(pfx + "ztmpn", [128, TC], BF16))
                psR = ER(nc.psum_tensor(pfx + "psR", [128, 64], F32))
                psW = ER(nc.psum_tensor(pfx + "psW", [8, TC], F32))
                psZa = ER(nc.psum_tensor(pfx + "psZa", [128, TC], F32))
                psZb = ER(nc.psum_tensor(pfx + "psZb", [128, TC], F32))

                def rz_gp(gp):
                    gp.wait_ge(s["ve"].h, ve_ln)
                    s["rwl"].inc(gp.dma_start(
                        out=rw_sb[:], in_=rwT.rearrange("(kt p) e -> p kt e", p=128)), 16)
                    for mi in range(2):
                        s["agl"].inc(gp.dma_start(
                            out=agp_sb[:, mi, :, :],
                            in_=agp[mi].rearrange("(kt p) c -> p kt c", p=128)), 16)
                    gp.wait_ge(s["wwr"].h, s["wwr"].b + 16)
                    for e in range(8):
                        s["wbc"].inc(gp.dma_start(
                            out=webc[:, e, :],
                            in_=wscr[e:e + 1, :].broadcast_to([128, TC])), 16)
                blk.gpsimd(rz_gp)

                def rz_te(te):
                    te.wait_ge(s["ve"].h, ve_ln)
                    te.wait_ge(s["rwl"].h, s["rwl"].b + 16)
                    for tt in range(8):
                        for kt in range(DT):
                            inst = te.matmul(psR[:, 8 * tt:8 * (tt + 1)],
                                             xlnT[:, kt, 128 * tt:128 * (tt + 1)],
                                             rw_sb[:, kt, :],
                                             start=(tt == 0 and kt == 0),
                                             stop=(tt == 7 and kt == DT - 1))
                    s["rpe"].inc(inst)
                    te.wait_ge(s["agl"].h, s["agl"].b + 32)
                    for mi, ps in ((0, psZa), (1, psZb)):
                        for th in range(2):
                            for kt in range(DT):
                                inst = te.matmul(ps[:, 512 * th:512 * (th + 1)],
                                                 agp_sb[:, mi, kt, :],
                                                 xlnT[:, kt, 512 * th:512 * (th + 1)],
                                                 start=(kt == 0), stop=(kt == DT - 1))
                            s["zpe"].inc(inst)
                    te.wait_ge(s["rve"].h, s["rve"].b + 1)
                    for tt in range(8):
                        inst = te.transpose(psW[0:8, 128 * tt:128 * (tt + 1)],
                                            wtok[:, tt, :], idf[:])
                    s["rpw"].inc(inst)
                blk.tensor(rz_te)

                def rz_se(se):
                    se.wait_ge(s["rpe"].h, s["rpe"].b + 1)
                    s["rse"].inc(se.activation(_r(prb[:]), psR[:], AF.Exp,
                                               bias=0.0, scale=1.0))
                    se.wait_ge(s["zpe"].h, s["zpe"].b + 2)
                    s["zev"].inc(se.activation(ztmp[:], psZa[:], AF.Identity,
                                               bias=0.0, scale=2.0))
                    s["zev"].inc(se.activation(ztmpn[:], psZa[:], AF.Identity,
                                               bias=0.0, scale=-2.0))
                    se.wait_ge(s["zpe"].h, s["zpe"].b + 4)
                    se.wait_ge(s["zsh0"].h, s["zsh0"].b + 240)
                    s["zev"].inc(se.activation(ztmp[:], psZb[:], AF.Identity,
                                               bias=0.0, scale=2.0))
                    s["zev"].inc(se.activation(ztmpn[:], psZb[:], AF.Identity,
                                               bias=0.0, scale=-2.0))
                    se.wait_ge(s["rpw"].h, s["rpw"].b + 1)
                    s["wev"].inc(se.activation(wT8[:], psW[:], AF.Identity,
                                               bias=0.0, scale=1.0))
                blk.scalar(rz_se)

                def rz_ve(ve):
                    ve.memset(_r(zbg[:]), 0.0)
                    s["zms"].inc(ve.memset(_r(zbu[:]), 0.0))
                    ve.wait_ge(s["rse"].h, s["rse"].b + 1)
                    for tt in range(8):
                        p8 = prb[:, tt, :]
                        s["chv"].inc(ve.reduce_max(rcl[:, 0:1], p8, axis=AX.X))
                        ve.wait_ge(s["chv"].h, s["chv"].n)
                        s["chv"].inc(ve.tensor_scalar(eqa[:, tt, :], p8, rcl[:, 0:1],
                                                      None, op0=ALU.is_equal))
                        ve.wait_ge(s["chv"].h, s["chv"].n)
                        s["chv"].inc(ve.scalar_tensor_tensor(lgt[:, tt, :], eqa[:, tt, :],
                                                             -1e30, p8, op0=ALU.mult,
                                                             op1=ALU.add))
                        ve.wait_ge(s["chv"].h, s["chv"].n)
                        s["chv"].inc(ve.reduce_max(rcl[:, 1:2], lgt[:, tt, :], axis=AX.X))
                        ve.wait_ge(s["chv"].h, s["chv"].n)
                        s["chv"].inc(ve.tensor_scalar(lgt[:, tt, :], lgt[:, tt, :],
                                                      rcl[:, 1:2], None, op0=ALU.is_equal))
                        ve.wait_ge(s["chv"].h, s["chv"].n)
                        s["chv"].inc(ve.tensor_tensor(eqa[:, tt, :], eqa[:, tt, :],
                                                      lgt[:, tt, :], op=ALU.add))
                        ve.wait_ge(s["chv"].h, s["chv"].n)
                        s["chv"].inc(ve.tensor_tensor(rcl[:, 2:3], rcl[:, 0:1],
                                                      rcl[:, 1:2], op=ALU.add))
                        ve.wait_ge(s["chv"].h, s["chv"].n)
                        s["chv"].inc(ve.reciprocal(rcl[:, 3:4], rcl[:, 2:3]))
                        ve.wait_ge(s["chv"].h, s["chv"].n)
                        s["chv"].inc(ve.tensor_tensor(wtok[:, tt, :], p8, eqa[:, tt, :],
                                                      op=ALU.mult))
                        ve.wait_ge(s["chv"].h, s["chv"].n)
                        inst = ve.tensor_scalar(wtok[:, tt, :], wtok[:, tt, :],
                                                rcl[:, 3:4], None, op0=ALU.mult)
                        if tt == 7:
                            s["rve"].inc(inst)
                        else:
                            s["chv"].inc(inst)
                            ve.wait_ge(s["chv"].h, s["chv"].n)
                blk.vector(rz_ve)

                def rz_sy(sy):
                    sy.wait_ge(s["zms"].h, s["zms"].b + 1)
                    for mi, zb, sh in ((0, zbg, "zsh0"), (1, zbu, "zsh1")):
                        sy.wait_ge(s["zev"].h, s["zev"].b + 2 * (mi + 1))
                        for e in range(8):
                            s[sh].inc(sy.dma_start(
                                out=zb[0:16, e, :], in_=ztmp[16 * e:16 * (e + 1), :]), 16)
                        for e in range(1, 8):
                            s[sh].inc(sy.dma_start(
                                out=zb[16:32, e, :],
                                in_=ztmpn[16 * (e - 1):16 * e, :]), 16)
                    sy.wait_ge(s["wev"].h, s["wev"].b + 1)
                    s["wwr"].inc(sy.dma_start(out=wscr, in_=wT8[:]), 16)
                blk.sync(rz_sy)

            with contextlib.ExitStack() as esa:
                EA = esa.enter_context
                wgt = EA(nc.sbuf_tensor(pfx + "wgt", [128, 2, DT, 128], BF16))
                wut = EA(nc.sbuf_tensor(pfx + "wut", [128, 2, DT, 128], BF16))
                adt_sb = EA(nc.sbuf_tensor(pfx + "adt_sb", [128, 2, 8, 16], BF16))
                bgt = EA(nc.sbuf_tensor(pfx + "bgt", [32, 2, 8, 128], BF16))
                but = EA(nc.sbuf_tensor(pfx + "but", [32, 2, 8, 128], BF16))
                sg = EA(nc.sbuf_tensor(pfx + "sg", [128, 2, TC], BF16))
                heb = EA(nc.sbuf_tensor(pfx + "heb", [128, 2, TC], BF16))
                meb = EA(nc.sbuf_tensor(pfx + "meb", [128, 2, TC], BF16))
                mix_ft = EA(nc.sbuf_tensor(pfx + "mix_ft", [128, 2, TC], F32))
                psG = EA(nc.psum_tensor(pfx + "psG", [128, TC], F32))
                psU = EA(nc.psum_tensor(pfx + "psU", [128, TC], F32))
                psY = EA(nc.psum_tensor(pfx + "psY", [128, TC], F32))

                def ld4(gp, ft):
                    sl = ft % 2
                    sm = s["wg%d" % sl]
                    sm.inc(gp.dma_start(
                        out=wgt[:, sl, :, :],
                        in_=wgT[:, 128 * ft:128 * (ft + 1)]
                            .rearrange("(kt p) f -> p kt f", p=128)), 16)
                    sm.inc(gp.dma_start(
                        out=wut[:, sl, :, :],
                        in_=wuT[:, 128 * ft:128 * (ft + 1)]
                            .rearrange("(kt p) f -> p kt f", p=128)), 16)
                    sm.inc(gp.dma_start(
                        out=adt_sb[:, sl, :, :],
                        in_=adt[128 * ft:128 * (ft + 1)]), 16)
                    sm.inc(gp.dma_start(
                        out=bgt[:, sl, :, :],
                        in_=bgp[0][:, :, 128 * ft:128 * (ft + 1)]), 16)
                    sm.inc(gp.dma_start(
                        out=but[:, sl, :, :],
                        in_=bgp[1][:, :, 128 * ft:128 * (ft + 1)]), 16)

                def a_gp(gp):
                    gp.wait_ge(s["zpe"].h, s["zpe"].b + 4)
                    gp.wait_ge(s["zsh0"].h, s["zsh0"].b + 240)
                    gp.wait_ge(s["zsh1"].h, s["zsh1"].b + 240)
                    gp.wait_ge(s["rpw"].h, s["rpw"].b + 1)
                    gp.wait_ge(s["wwr"].h, s["wwr"].b + 16)
                    ld4(gp, 0)
                    ld4(gp, 1)
                    gp.wait_ge(s["wbc"].h, s["wbc"].b + 128)
                    for ft in range(64):
                        for e in range(8):
                            gp.wait_ge(s["vhe"].h, s["vhe"].b + 16 * ft + 2 * e + 2)
                            if 8 * ft + e >= 2:
                                gp.wait_ge(s["vacc"].h, s["vacc"].b + 8 * ft + e - 1)
                            s["gme"].inc(gp.tensor_tensor(
                                meb[:, e % 2, :], heb[:, e % 2, :], webc[:, e, :],
                                op=ALU.mult))
                        gp.wait_ge(s["vacc"].h, s["vacc"].b + 8 * (ft + 1))
                        s["mxw%d" % (ft % 2)].inc(gp.dma_start(
                            out=mixd[128 * ft:128 * (ft + 1), :],
                            in_=mix_ft[:, ft % 2, :]), 16)
                        if ft + 2 < 64:
                            gp.wait_ge(s["pmm"].h, s["pmm"].b + 16 * (ft + 1))
                            gp.wait_ge(s["pyy"].h, s["pyy"].b + 8 * (ft + 1))
                            ld4(gp, ft + 2)
                blk.gpsimd(a_gp)

                def a_te(te):
                    te.wait_ge(s["zsh0"].h, s["zsh0"].b + 240)
                    te.wait_ge(s["zsh1"].h, s["zsh1"].b + 240)
                    te.wait_ge(s["wev"].h, s["wev"].b + 1)
                    for ft in range(64):
                        sl = ft % 2
                        te.wait_ge(s["wg%d" % sl].h, s["wg%d" % sl].b + 80 * (ft // 2 + 1))
                        if ft > 0:
                            te.wait_ge(s["slu"].h, s["slu"].b + 16 * ft)
                            te.wait_ge(s["vhe"].h, s["vhe"].b + 16 * ft)
                        for wsb, ps in ((wgt, psG), (wut, psU)):
                            for th in range(2):
                                for kt in range(DT):
                                    te.matmul(ps[:, 512 * th:512 * (th + 1)],
                                              wsb[:, sl, kt, :],
                                              xlnT[:, kt, 512 * th:512 * (th + 1)],
                                              start=(kt == 0), stop=False,
                                              skip_group_check=True)
                        for e in range(8):
                            for th in range(2):
                                if e > 0:
                                    te.wait_ge(s["slu"].h,
                                               s["slu"].b + 16 * ft + 2 * (e - 1) + th + 1)
                                    te.wait_ge(s["vhe"].h,
                                               s["vhe"].b + 16 * ft + 2 * (e - 1) + th + 1)
                                for bsb, zb, ps in ((bgt, zbg, psG), (but, zbu, psU)):
                                    inst = te.matmul(
                                        ps[:, 512 * th:512 * (th + 1)],
                                        bsb[:, sl, e, :],
                                        zb[:, e, 512 * th:512 * (th + 1)],
                                        start=False, stop=(e == 7),
                                        skip_group_check=True)
                                s["pmm"].inc(inst)
                            if e > 0:
                                te.wait_ge(s["vhe"].h, s["vhe"].b + 16 * ft + 2 * e)
                                for tt in range(8):
                                    inst = te.matmul(
                                        psY[:, 128 * tt + 16 * (e - 1):
                                            128 * tt + 16 * e],
                                        heb[:, (e - 1) % 2, 128 * tt:128 * (tt + 1)],
                                        adt_sb[:, sl, e - 1, :],
                                        start=(ft == 0 and e == 1 and tt % 4 == 0),
                                        stop=False)
                                s["pyy"].inc(inst)
                        te.wait_ge(s["vhe"].h, s["vhe"].b + 16 * ft + 16)
                        for tt in range(8):
                            inst = te.matmul(
                                psY[:, 128 * tt + 112:128 * tt + 128],
                                heb[:, 7 % 2, 128 * tt:128 * (tt + 1)],
                                adt_sb[:, sl, 7, :],
                                start=False,
                                stop=(ft == 63 and tt % 4 == 3))
                            s["pyy"].inc(inst) if tt == 7 else None
                blk.tensor(a_te)

                def a_se(se):
                    for ft in range(64):
                        for e in range(8):
                            for th in range(2):
                                q = 16 * ft + 2 * e + th
                                se.wait_ge(s["pmm"].h, s["pmm"].b + q + 1)
                                if q >= 4:
                                    se.wait_ge(s["vhe"].h, s["vhe"].b + q - 3)
                                s["slu"].inc(se.activation(
                                    sg[:, e % 2, 512 * th:512 * (th + 1)],
                                    psG[:, 512 * th:512 * (th + 1)],
                                    AF.Silu, bias=0.0, scale=1.0))
                    se.wait_ge(s["pyy"].h, s["pyy"].b + 512)
                    s["yev"].inc(se.activation(ybuf[:], psY[:], AF.Identity,
                                               bias=0.0, scale=2.0))
                blk.scalar(a_se)

                def a_ve(ve):
                    for ft in range(64):
                        if ft >= 2:
                            ve.wait_ge(s["mxw%d" % (ft % 2)].h, s["mxw%d" % (ft % 2)].b + 16 * (ft // 2))
                        s["chv"].inc(ve.memset(mix_ft[:, ft % 2, :], 0.0))

                        def acc(e2):
                            ve.wait_ge(s["gme"].h, s["gme"].b + 8 * ft + e2 + 1)
                            ve.wait_ge(s["chv"].h, s["chv"].n)
                            if e2 > 0:
                                ve.wait_ge(s["vacc"].h, s["vacc"].n)
                            s["vacc"].inc(ve.tensor_tensor(
                                mix_ft[:, ft % 2, :], mix_ft[:, ft % 2, :],
                                meb[:, e2 % 2, :], op=ALU.add))
                        for e in range(8):
                            for th in range(2):
                                q = 16 * ft + 2 * e + th
                                ve.wait_ge(s["slu"].h, s["slu"].b + q + 1)
                                if th == 0 and 8 * ft + e >= 2:
                                    ve.wait_ge(s["gme"].h, s["gme"].b + 8 * ft + e - 1)
                                    ve.wait_ge(s["pyy"].h, s["pyy"].b + 8 * ft + e - 1)
                                s["vhe"].inc(ve.scalar_tensor_tensor(
                                    heb[:, e % 2, 512 * th:512 * (th + 1)],
                                    sg[:, e % 2, 512 * th:512 * (th + 1)], 1.0,
                                    psU[:, 512 * th:512 * (th + 1)],
                                    op0=ALU.mult, op1=ALU.mult))
                            if e >= 1:
                                acc(e - 1)
                        acc(7)
                blk.vector(a_ve)

        with contextlib.ExitStack() as esb:
            EB = esb.enter_context
            xT2 = EB(nc.sbuf_tensor(pfx + "xT2", [128, DT, TC], F32))
            xtm2 = EB(nc.sbuf_tensor(pfx + "xtm2", [128, 2, D], F32))
            mixsb = EB(nc.sbuf_tensor(pfx + "mixsb", [128, 2, 4, TC], BF16))
            wdsb = EB(nc.sbuf_tensor(pfx + "wdsb", [128, 2, 4, D], BF16))
            ytr = EB(nc.sbuf_tensor(pfx + "ytr", [16, 8, TC], BF16))
            bdt_sb = EB(nc.sbuf_tensor(pfx + "bdt_sb", [16, 2, 8, 128], BF16))
            psA3 = EB(nc.psum_tensor(pfx + "psA3", [128, 512], F32))
            psB3 = EB(nc.psum_tensor(pfx + "psB3", [128, 512], F32))
            py2 = EB(nc.psum_tensor(pfx + "py2", [16, 2, 512], F32))
            psD2 = EB(nc.psum_tensor(pfx + "psD2", [128, 2, TC], F32))

            GATES = (("pmm", 1024), ("pyy", 512), ("slu", 1024), ("vhe", 1024),
                     ("gme", 512), ("vacc", 512), ("yev", 1),
                     ("mxw0", 512), ("mxw1", 512))

            def b_gate_sy(sy):
                for nm, v in GATES:
                    sy.wait_ge(s[nm].h, s[nm].b + v)
            blk.sync(b_gate_sy)

            def b_gate_te(te):
                te.wait_ge(s["slu"].h, s["slu"].b + 1024)
                te.wait_ge(s["vhe"].h, s["vhe"].b + 1024)
                te.wait_ge(s["yev"].h, s["yev"].b + 1)
            blk.tensor(b_gate_te)

            emit_load_x(nc, blk, s, x_in, xT2, xtm2, idf, psA3, psB3)
            se_lx2 = s["se"].n

            def b_sy2(sy):
                for kfg in range(16):
                    sy.wait_ge(s["mxw0"].h, s["mxw0"].b + 32 * (kfg + 1))
                    sy.wait_ge(s["mxw1"].h, s["mxw1"].b + 32 * (kfg + 1))
                    if kfg >= 2:
                        sy.wait_ge(s["pdn"].h, s["pdn"].b + 16 * (kfg - 1))
                    for j in range(4):
                        s["mxl%d" % (kfg % 2)].inc(sy.dma_start(
                            out=mixsb[:, kfg % 2, j, :],
                            in_=mixd[128 * (4 * kfg + j):128 * (4 * kfg + j + 1), :]), 16)
            blk.sync(b_sy2)

            def b_gp(gp):
                gp.wait_ge(s["vhe"].h, s["vhe"].b + 1024)
                gp.wait_ge(s["slu"].h, s["slu"].b + 1024)
                gp.wait_ge(s["yev"].h, s["yev"].b + 1)
                gp.wait_ge(s["mxw0"].h, s["mxw0"].b + 512)
                gp.wait_ge(s["mxw1"].h, s["mxw1"].b + 512)
                gp.wait_ge(s["pyy"].h, s["pyy"].b + 512)
                gp.wait_ge(s["gme"].h, s["gme"].b + 512)
                for kfg in range(16):
                    if kfg >= 2:
                        gp.wait_ge(s["pdn"].h, s["pdn"].b + 16 * (kfg - 1))
                    s["wd%d" % (kfg % 2)].inc(gp.dma_start(
                        out=wdsb[:, kfg % 2, :, :],
                        in_=wdT[512 * kfg:512 * (kfg + 1), :]
                            .rearrange("(kf p) h -> p kf h", p=128)), 16)
                for h in range(DT):
                    if h >= 2:
                        gp.wait_ge(s["pdn"].h, s["pdn"].b + 240 + h - 1)
                    s["bd%d" % (h % 2)].inc(gp.dma_start(
                        out=bdt_sb[:, h % 2, :, :],
                        in_=bdt[:, :, 128 * h:128 * (h + 1)]), 16)
            blk.gpsimd(b_gp)

            def b_w_ve(ve):
                ve.wait_ge(s["yev"].h, s["yev"].b + 1)
                for tt in range(8):
                    for e in range(8):
                        s["ywm"].inc(ve.tensor_scalar(
                            ybuf[:, 128 * tt + 16 * e:128 * tt + 16 * (e + 1)],
                            ybuf[:, 128 * tt + 16 * e:128 * tt + 16 * (e + 1)],
                            wtok[:, tt, e:e + 1], None, op0=ALU.mult))
            blk.vector(b_w_ve)

            def b_te(te):
                te.wait_ge(s["ywm"].h, s["ywm"].b + 64)
                for e in range(8):
                    if e >= 2:
                        te.wait_ge(s["ytv"].h, s["ytv"].b + e - 1)
                    for tt in range(8):
                        inst = te.transpose(
                            py2[:].bitcast(BF16)[:, e % 2, 128 * tt:128 * (tt + 1)],
                            ybuf[:, 128 * tt + 16 * e:128 * tt + 16 * (e + 1)],
                            idb[:])
                    s["pyt"].inc(inst)
                te.wait_ge(s["ytv"].h, s["ytv"].b + 8)
                for kfg in range(16):
                    te.wait_ge(s["wd%d" % (kfg % 2)].h, s["wd%d" % (kfg % 2)].b + 16 * (kfg // 2 + 1))
                    te.wait_ge(s["mxl%d" % (kfg % 2)].h, s["mxl%d" % (kfg % 2)].b + 64 * (kfg // 2 + 1))
                    for h in range(DT):
                        idx = 16 * kfg + h
                        if idx >= 2:
                            te.wait_ge(s["vdn"].h, s["vdn"].b + idx - 1)
                        if kfg == 15:
                            te.wait_ge(s["bd%d" % (h % 2)].h, s["bd%d" % (h % 2)].b + 16 * (h // 2 + 1))
                        for th in range(2):
                            for kf in range(4):
                                inst = te.matmul(
                                    psD2[:, idx % 2, 512 * th:512 * (th + 1)],
                                    wdsb[:, kfg % 2, kf, 128 * h:128 * (h + 1)],
                                    mixsb[:, kfg % 2, kf, 512 * th:512 * (th + 1)],
                                    start=(kf == 0),
                                    stop=(kf == 3 and kfg < 15))
                            if kfg == 15:
                                for e in range(8):
                                    inst = te.matmul(
                                        psD2[:, idx % 2, 512 * th:512 * (th + 1)],
                                        bdt_sb[:, h % 2, e, :],
                                        ytr[:, e, 512 * th:512 * (th + 1)],
                                        start=False, stop=(e == 7))
                        s["pdn"].inc(inst)
            blk.tensor(b_te)

            def b_se(se):
                for e in range(8):
                    se.wait_ge(s["pyt"].h, s["pyt"].b + e + 1)
                    s["ytv"].inc(se.activation(
                        ytr[:, e, :], py2[:].bitcast(BF16)[:, e % 2, :],
                        AF.Identity, bias=0.0, scale=1.0))
            blk.scalar(b_se)

            def b_ve(ve):
                ve.wait_ge(s["se"].h, se_lx2)
                for kfg in range(16):
                    for h in range(DT):
                        idx = 16 * kfg + h
                        ve.wait_ge(s["pdn"].h, s["pdn"].b + idx + 1)
                        s["vdn"].inc(ve.tensor_tensor(
                            xT2[:, h, :], xT2[:, h, :], psD2[:, idx % 2, :],
                            op=ALU.add))
            blk.vector(b_ve)

            with contextlib.ExitStack() as esf:
                EF = esf.enter_context
                xrow = EF(nc.sbuf_tensor(pfx + "xrow", [128, 2, D], F32))
                emit_store_x(nc, blk, s, xT2, x_out, xrow, idf, psA3, psB3,
                             ("vdn", s["vdn"].b + 256))


def build_l4():
    nc = bass.Bass("TRN2", target_bir_lowering=False)
    x_in = nc.dram_tensor("x_in", [TC, D], F32, kind="ExternalInput").ap()
    rwT = nc.dram_tensor("rwT", [D, 8], F32, kind="ExternalInput").ap()
    wgT = nc.dram_tensor("wgT", [D, 8192], F32, kind="ExternalInput").ap()
    wuT = nc.dram_tensor("wuT", [D, 8192], F32, kind="ExternalInput").ap()
    wdT = nc.dram_tensor("wdT", [8192, D], F32, kind="ExternalInput").ap()
    agp = nc.dram_tensor("agp", [2, D, 128], F32, kind="ExternalInput").ap()
    adt = nc.dram_tensor("adt", [8192, 8, 16], F32, kind="ExternalInput").ap()
    bgp = nc.dram_tensor("bgp", [2, 32, 8, 8192], F32, kind="ExternalInput").ap()
    bdt = nc.dram_tensor("bdt", [16, 8, D], F32, kind="ExternalInput").ap()
    x_out = nc.dram_tensor("x_out", [TC, D], F32, kind="ExternalOutput").ap()
    shared = _sems(nc, SHARED_SEMS)
    s = _sem_view(nc, shared, L4_SEMS, "")
    with contextlib.ExitStack() as es:
        E = es.enter_context
        ones_b = E(nc.sbuf_tensor("ones_b", [128, 1], BF16))
        ones_f1 = E(nc.sbuf_tensor("ones_f1", [1, 128], F32))
        idf = E(nc.sbuf_tensor("idf", [128, 128], F32))
        eps_c = E(nc.sbuf_tensor("eps_c", [128, 1], F32))
        blk = E(nc.Block())
        emit_consts(nc, blk, s, ones_b, ones_f1, idf, eps_c)
        _emit_l4(nc, blk, s, ones_b, ones_f1, idf, eps_c,
                 x_in, x_out, rwT, wgT, wuT, wdT, agp, adt, bgp, bdt, "")
    return nc


def _host_l4_inputs(inputs):
    rw = np.asarray(inputs["L1_ffn_router_w"]).astype(np.float32)
    Wg = np.asarray(inputs["L1_ffn_Wg"]).astype(np.float32)
    Wu = np.asarray(inputs["L1_ffn_Wu"]).astype(np.float32)
    Wd = np.asarray(inputs["L1_ffn_Wd"]).astype(np.float32)
    Ag = np.asarray(inputs["L1_ffn_Ag"]).astype(np.float32)
    Au = np.asarray(inputs["L1_ffn_Au"]).astype(np.float32)
    Ad = np.asarray(inputs["L1_ffn_Ad"]).astype(np.float32)
    Bg = np.asarray(inputs["L1_ffn_Bg"]).astype(np.float32)
    Bu = np.asarray(inputs["L1_ffn_Bu"]).astype(np.float32)
    Bd = np.asarray(inputs["L1_ffn_Bd"]).astype(np.float32)
    d = {}
    d["rwT"] = np.ascontiguousarray(rw.T)
    d["wgT"] = np.ascontiguousarray(Wg.T)
    d["wuT"] = np.ascontiguousarray(Wu.T)
    d["wdT"] = np.ascontiguousarray(Wd.T)
    agp = np.stack([Ag.transpose(2, 0, 1).reshape(D, 128),
                    Au.transpose(2, 0, 1).reshape(D, 128)])
    d["agp"] = np.ascontiguousarray(agp)
    d["adt"] = np.ascontiguousarray(Ad.transpose(2, 0, 1))
    def pair(B):
        bt = B.transpose(2, 0, 1)
        lo = bt
        hi = np.zeros_like(bt)
        hi[:, 1:] = bt[:, :7]
        return np.concatenate([lo, hi], axis=0)
    d["bgp"] = np.ascontiguousarray(np.stack([pair(Bg), pair(Bu)]))
    d["bdt"] = np.ascontiguousarray(Bd.transpose(2, 0, 1))
    return d


def run_l4(x2d, inputs):
    nc = build_l4()
    wd = _host_l4_inputs(inputs)
    in_maps = []
    for c in range(NC):
        m = {"x_in": np.ascontiguousarray(x2d[TC * c:TC * (c + 1)])}
        m.update(wd)
        in_maps.append(m)
    res = run_bass_kernel_spmd(nc, in_maps, core_ids=list(range(NC)))
    return np.concatenate([res.results[c]["x_out"] for c in range(NC)], axis=0)



def build_fused():
    nc = bass.Bass("TRN2", target_bir_lowering=False)
    x_in = nc.dram_tensor("x_in", [TC, D], F32, kind="ExternalInput").ap()
    wq = nc.dram_tensor("wq", [D, 256], F32, kind="ExternalInput").ap()
    wk = nc.dram_tensor("wk", [D, 128], F32, kind="ExternalInput").ap()
    wv = nc.dram_tensor("wv", [D, 128], F32, kind="ExternalInput").ap()
    wo = nc.dram_tensor("wo", [D, D], F32, kind="ExternalInput").ap()
    slopes_in = nc.dram_tensor("slopes", [2], F32, kind="ExternalInput").ap()
    bias_in = nc.dram_tensor("abias_in", [2, N_STRIP], F32, kind="ExternalInput").ap()
    gpk = nc.dram_tensor("gpk", [8, 128, 8, 256], U8, kind="ExternalInput").ap()
    upk = nc.dram_tensor("upk", [8, 128, 8, 256], U8, kind="ExternalInput").ap()
    dpkt = nc.dram_tensor("dpkt", [128, 8, 2048], U8, kind="ExternalInput").ap()
    gsc = nc.dram_tensor("gsc", [8, 8, 2048], F32, kind="ExternalInput").ap()
    usc = nc.dram_tensor("usc", [8, 8, 2048], F32, kind="ExternalInput").ap()
    dsc = nc.dram_tensor("dsc", [8, 8, 2048], F32, kind="ExternalInput").ap()
    wq3 = nc.dram_tensor("wq3", [D, D], F32, kind="ExternalInput").ap()
    wk3 = nc.dram_tensor("wk3", [D, D], F32, kind="ExternalInput").ap()
    wv3 = nc.dram_tensor("wv3", [D, D], F32, kind="ExternalInput").ap()
    wg3 = nc.dram_tensor("wg3", [D, D], F32, kind="ExternalInput").ap()
    wo3 = nc.dram_tensor("wo3", [D, D], F32, kind="ExternalInput").ap()
    sel_in = nc.dram_tensor("sel_in", [7], F32, kind="ExternalInput").ap()
    rwT = nc.dram_tensor("rwT", [D, 8], F32, kind="ExternalInput").ap()
    wgT = nc.dram_tensor("wgT", [D, 8192], F32, kind="ExternalInput").ap()
    wuT = nc.dram_tensor("wuT", [D, 8192], F32, kind="ExternalInput").ap()
    wdT = nc.dram_tensor("wdT", [8192, D], F32, kind="ExternalInput").ap()
    agp = nc.dram_tensor("agp", [2, D, 128], F32, kind="ExternalInput").ap()
    adt = nc.dram_tensor("adt", [8192, 8, 16], F32, kind="ExternalInput").ap()
    bgp = nc.dram_tensor("bgp", [2, 32, 8, 8192], F32, kind="ExternalInput").ap()
    bdt = nc.dram_tensor("bdt", [16, 8, D], F32, kind="ExternalInput").ap()
    x_out = nc.dram_tensor("x_out", [TC, D], F32, kind="ExternalOutput").ap()
    xs1 = nc.dram_tensor("xs1", [TC, D], F32, kind="Internal").ap()
    xs2 = nc.dram_tensor("xs2", [TC, D], F32, kind="Internal").ap()
    xs3 = nc.dram_tensor("xs3", [TC, D], F32, kind="Internal").ap()

    shared = _sems(nc, SHARED_SEMS)
    pool = SemPool(nc)

    with contextlib.ExitStack() as es:
        E = es.enter_context
        ones_b = E(nc.sbuf_tensor("ones_b", [128, 1], BF16))
        ones_f1 = E(nc.sbuf_tensor("ones_f1", [1, 128], F32))
        idf = E(nc.sbuf_tensor("idf", [128, 128], F32))
        eps_c = E(nc.sbuf_tensor("eps_c", [128, 1], F32))
        blk = E(nc.Block())
        emit_consts(nc, blk, shared, ones_b, ones_f1, idf, eps_c)
        _emit_l1(nc, blk, pool.view(shared, L1_SEMS), ones_b, ones_f1, idf, eps_c,
                 x_in, xs1, wq, wk, wv, wo, slopes_in, bias_in, "b1_")
        _gate_all(blk, shared)
        _emit_l2(nc, blk, pool.view(shared, L2_SEMS), ones_b, ones_f1, idf, eps_c,
                 xs1, xs2, gpk, upk, dpkt, gsc, usc, dsc, "b2_")
        _gate_all(blk, shared)
        _emit_l3(nc, blk, pool.view(shared, L3_SEMS), ones_b, ones_f1, idf, eps_c,
                 xs2, xs3, wq3, wk3, wv3, wg3, wo3, sel_in, "b3_")
        _gate_all(blk, shared)
        _emit_l4(nc, blk, pool.view(shared, L4_SEMS), ones_b, ones_f1, idf, eps_c,
                 xs3, x_out, rwT, wgT, wuT, wdT, agp, adt, bgp, bdt, "b4_")
    return nc


def run_fused(inputs):
    nc = build_fused()
    x2d = np.ascontiguousarray(np.asarray(inputs["L0_attn_x"]).reshape(T, D)) \
        .astype(np.float32, copy=False)
    Wq = np.asarray(inputs["L0_attn_Wq"])
    Wk = np.asarray(inputs["L0_attn_Wk"])
    Wv = np.asarray(inputs["L0_attn_Wv"])
    Wo = np.asarray(inputs["L0_attn_Wo"])
    l2h = _host_l2_inputs(inputs)
    l4h = _host_l4_inputs(inputs)
    in_maps = []
    for c in range(NC):
        sel = np.zeros(7, np.float32)
        if c >= 1:
            sel[c - 1] = 1.0
        m = {
            "x_in": np.ascontiguousarray(x2d[TC * c:TC * (c + 1)]),
            "wq": np.ascontiguousarray(Wq[:, 256 * c:256 * (c + 1)]),
            "wk": np.ascontiguousarray(Wk[:, 128 * (c // 2):128 * (c // 2 + 1)]),
            "wv": np.ascontiguousarray(Wv[:, 128 * (c // 2):128 * (c // 2 + 1)]),
            "wo": np.ascontiguousarray(Wo),
            "slopes": np.array([SLOPES[2 * c], SLOPES[2 * c + 1]], np.float32),
            "abias_in": _attn_bias_host(c),
            "wq3": np.asarray(inputs["L1_attn_Wq"]),
            "wk3": np.asarray(inputs["L1_attn_Wk"]),
            "wv3": np.asarray(inputs["L1_attn_Wv"]),
            "wg3": np.asarray(inputs["L1_attn_Wg"]),
            "wo3": np.asarray(inputs["L1_attn_Wo"]),
            "sel_in": sel,
        }
        m.update(_host_l2_core(l2h, c))
        m.update(l4h)
        in_maps.append(m)
    res = run_bass_kernel_spmd(nc, in_maps, core_ids=list(range(NC)))
    return np.concatenate([res.results[c]["x_out"] for c in range(NC)], axis=0)



def _attn_bias_host(c):
    out = np.zeros((2, N_STRIP), np.float32)
    for hh in range(2):
        sl = SLOPES[2 * c + hh]
        for b in range(NBAND):
            for sc in range(4 * (b + 1)):
                out[hh, _boff(b) + sc] = sl * 128.0 * (sc - 4 * b) - SHIFT
    return out


def run_l1(x2d, Wq, Wk, Wv, Wo):
    nc = build_l1()
    in_maps = []
    for c in range(NC):
        in_maps.append({
            "x_in": np.ascontiguousarray(x2d[TC * c:TC * (c + 1)]),
            "wq": np.ascontiguousarray(Wq[:, 256 * c:256 * (c + 1)]),
            "wk": np.ascontiguousarray(Wk[:, 128 * (c // 2):128 * (c // 2 + 1)]),
            "wv": np.ascontiguousarray(Wv[:, 128 * (c // 2):128 * (c // 2 + 1)]),
            "wo": np.ascontiguousarray(Wo),
            "slopes": np.array([SLOPES[2 * c], SLOPES[2 * c + 1]], np.float32),
            "abias_in": _attn_bias_host(c),
        })
    res = run_bass_kernel_spmd(nc, in_maps, core_ids=list(range(NC)))
    return np.concatenate([res.results[c]["x_out"] for c in range(NC)], axis=0)


def kernel(**inputs):
    return run_fused(inputs).reshape(1, T, D)


def kernel_unfused(**inputs):
    x2d = np.ascontiguousarray(inputs["L0_attn_x"].reshape(T, D)).astype(np.float32, copy=False)
    x1 = run_l1(x2d, np.asarray(inputs["L0_attn_Wq"]), np.asarray(inputs["L0_attn_Wk"]),
                np.asarray(inputs["L0_attn_Wv"]), np.asarray(inputs["L0_attn_Wo"]))
    x2 = run_l2(x1, inputs)
    x3 = run_l3(x2, inputs)
    x4 = run_l4(x3, inputs)
    return x4.reshape(1, T, D)
